# version 02
# baseline (speedup 1.0000x reference)
import numpy as np
import concourse.bass as bass
import concourse.mybir as mybir
from concourse.bass_utils import run_bass_kernel_spmd

F32 = mybir.dt.float32
BF16 = mybir.dt.bfloat16
F16 = mybir.dt.float16
I32 = mybir.dt.int32
AF = mybir.ActivationFunctionType
ALU = mybir.AluOpType
AX = mybir.AxisListType

D_MODEL = 2048
D_INNER = 4096
D_STATE = 16
DT_RANK = 128
INTER = 8192
FFN_I = 5632
BATCH, SEQ = 2, 4096
EPS_LN = 1e-5
EPS_RMS = 1e-5
EPS_FFN = 1e-6

NCORE = 8
TOK = 1024
NGP = 8
SUB = 512
NSUB = SEQ // SUB
M_INPROJ = 21
PE_WINDOW = 3


def _dlocal(gp):
    p = np.arange(128)
    return 16 * (p % 64) + (2 * gp + p // 64)


def _repl_matrices():
    RB = np.zeros((16, 128, 128), np.float32)
    for n in range(16):
        for p in range(128):
            RB[n, (n % 8) * 16 + (p % 64) // 4, p] = 1.0
    RC = np.zeros((2, 128, 128), np.float32)
    for par in range(2):
        for p in range(128):
            RC[par, 64 * par + p % 64, p] = 1.0
    RX = np.zeros((8, 2, 128, 128), np.float32)
    for gp in range(8):
        for p in range(128):
            g = 2 * gp + p // 64
            RX[gp, g // 8, (g % 8) * 16 + (p % 64) // 4, p] = 1.0
    return RB, RC, RX


class CoreProgram:
    def __init__(self, nc):
        self.nc = nc
        self._sem_names = ["pe", "eva", "evd", "evg", "ldw", "lda", "sto",
                          "cc", "ldc", "stg"]
        self.sems = {}
        self.cnt = {}
        self.pe_issued = 0
        self.pe_waited = 0
        self._keep = []

    def open(self, ctx):
        v = ctx.__enter__()
        self._keep.append(ctx)
        return v

    def sem_init(self):
        for n in self._sem_names:
            self.sems[n] = self.open(self.nc.semaphore(n))
            self.cnt[n] = 0

    def inc(self, inst, sem, n=1):
        inst.then_inc(self.sems[sem], n)
        self.cnt[sem] += n
        return self.cnt[sem]

    def wait(self, eng, sem, val):
        eng.wait_ge(self.sems[sem], val)

    def pe_mm(self, out, lhsT, rhs, start, stop):
        t = self.nc.tensor
        if self.pe_issued - self.pe_waited >= PE_WINDOW:
            need = self.pe_issued - PE_WINDOW + 1
            t.wait_ge(self.sems["pe"], need)
            self.pe_waited = need
        t.matmul(out, lhsT, rhs, start=start, stop=stop).then_inc(self.sems["pe"], 1)
        self.pe_issued += 1
        self.cnt["pe"] += 1
        return self.pe_issued

    def pe_transpose(self, out, in_):
        t = self.nc.tensor
        if self.pe_issued - self.pe_waited >= PE_WINDOW:
            need = self.pe_issued - PE_WINDOW + 1
            t.wait_ge(self.sems["pe"], need)
            self.pe_waited = need
        t.transpose(out, in_, self.ident_sb[:, :]).then_inc(self.sems["pe"], 1)
        self.pe_issued += 1
        self.cnt["pe"] += 1
        return self.pe_issued

    def pe_drain(self):
        t = self.nc.tensor
        if self.pe_issued - self.pe_waited >= PE_WINDOW:
            need = self.pe_issued - PE_WINDOW + 1
            t.wait_ge(self.sems["pe"], need)
            self.pe_waited = need
        t.drain().then_inc(self.sems["pe"], 1)
        self.pe_issued += 1
        self.cnt["pe"] += 1
        return self.pe_issued

    def build_p0(self, t_in):
        nc = self.nc
        op = self.open
        self.dtb_sb = op(nc.sbuf_tensor("dtb_sb", [128, NGP], F32))
        self.a_sb = op(nc.sbuf_tensor("a_sb", [128, NGP * 16], F32))
        self.convw_sb = op(nc.sbuf_tensor("convw_sb", [128, NGP * 4], F32))
        self.convb_sb = op(nc.sbuf_tensor("convb_sb", [128, NGP], F32))
        self.dpar_sb = op(nc.sbuf_tensor("dpar_sb", [128, NGP], F32))
        self.ln1_sb = op(nc.sbuf_tensor("ln1_sb", [128, 16], F32))
        self.ln2_sb = op(nc.sbuf_tensor("ln2_sb", [128, 16], F32))

        RB, RC, RX = _repl_matrices()
        rb_d = nc.inline_tensor(
            np.ascontiguousarray(RB.transpose(1, 0, 2).reshape(128, 16 * 128)),
            name="rb_c").ap()
        rc_d = nc.inline_tensor(
            np.ascontiguousarray(RC.transpose(1, 0, 2).reshape(128, 2 * 128)),
            name="rc_c").ap()
        rx_d = nc.inline_tensor(
            np.ascontiguousarray(RX.transpose(2, 0, 1, 3).reshape(128, 16 * 128)),
            name="rx_c").ap()

        import os as _osp
        if _osp.environ.get("P0SKIP") == "1":
            return
        with nc.Block() as blk:
            @blk.gpsimd
            def _(g):
                self.inc(g.dma_start(out=self.rb_sb[:, :], in_=rb_d[:, :]), "ldw", 16)
                self.inc(g.dma_start(out=self.rc_sb[:, :], in_=rc_d[:, :]), "ldw", 16)
                self.inc(g.dma_start(out=self.rx_sb[:, :], in_=rx_d[:, :]), "ldw", 16)

            @blk.sync
            def _(s):
                for name, sb in (("dt_bias", self.dtb_sb), ("a_log_p", self.a_sb),
                                 ("conv_w_p", self.convw_sb),
                                 ("conv_b_p", self.convb_sb),
                                 ("d_par_p", self.dpar_sb), ("ln1_w", self.ln1_sb),
                                 ("ln2_w", self.ln2_sb)):
                    self.inc(s.dma_start(out=sb[:, :], in_=t_in[name]), "ldc", 16)

            @blk.scalar
            def _(sc):
                self.wait(sc, "ldc", self.cnt["ldc"])
                self.inc(sc.activation(self.a_sb[:, :], self.a_sb[:, :], AF.Exp),
                         "eva", 1)

            @blk.vector
            def _(v):
                self.wait(v, "eva", self.cnt["eva"])
                self.inc(v.tensor_scalar_mul(self.a_sb[:, :], self.a_sb[:, :], -1.0),
                         "evd", 1)

    def build_m1(self, t_in, v_src, bzs, wait_cc=None):
        nc = self.nc
        NTC = 8
        PERTC_PE = M_INPROJ * 17 + NGP * 2

        import contextlib
        with contextlib.ExitStack() as es:
            E = es.enter_context
            w_in_sb = E(nc.sbuf_tensor("w_in_sb", [128, M_INPROJ * 2048], BF16))
            w_dt_sb = E(nc.sbuf_tensor("w_dt_sb", [128, NGP * 128], BF16))
            vbuf = E(nc.sbuf_tensor("vbuf", [128, 2 * 16 * 512], BF16))
            zst = E(nc.sbuf_tensor("zst", [128, 8 * 512], BF16))
            xbst = E(nc.sbuf_tensor("xbst", [128, 2 * 512], BF16))
            bst = E(nc.sbuf_tensor("bst", [128, 2 * 512], BF16))
            cst = E(nc.sbuf_tensor("cst", [128, 8 * 512], BF16))
            dtst = E(nc.sbuf_tensor("dtst", [128, 8 * 512], F32))
            dtraw = E(nc.sbuf_tensor("dtraw", [128, 2 * 512], BF16))
            sp1 = E(nc.sbuf_tensor("sp1", [128, 2 * 512], F32))
            sp2 = E(nc.sbuf_tensor("sp2", [128, 2 * 512], F32))
            ps_z = E(nc.psum_tensor("ps_z", [128, 2 * 512], F32))
            ps_o = E(nc.psum_tensor("ps_o", [128, 2 * 512], F32))
            ps_dt = E(nc.psum_tensor("ps_dt", [128, 2 * 512], F32))
            blk = E(nc.Block())
            pe0 = self.pe_issued
            eva0 = self.cnt["eva"]
            evd0 = self.cnt["evd"]
            lda0 = self.cnt["lda"]
            stg0 = self.cnt["stg"]

            def pe_tile(tc, m):
                return pe0 + tc * PERTC_PE + (m + 1) * 17

            def pe_dt(tc, gp):
                return pe0 + tc * PERTC_PE + M_INPROJ * 17 + (gp + 1) * 2

            def eva_silu(tc, zi):
                return eva0 + tc * 16 + zi + 1

            def eva_u3(tc, gp):
                return eva0 + tc * 16 + 8 + gp + 1

            def evd_copy(tc, m):
                return evd0 + tc * 21 + (m - 8) + 1

            def evd_dtfin(tc, gp):
                return evd0 + tc * 21 + 13 + gp + 1

            def stg_tc(tc):
                return stg0 + (tc + 1) * 5 * 16

            def lda_tc(tc):
                return lda0 + (tc + 1) * 16 * 16

            ldw_need = self.cnt["ldw"] + 22 * 16

            def o_idx(tc, m):
                return tc * 13 + (m - 8)

            def o_from_idx(i):
                return i // 13, i % 13 + 8

            @blk.tensor
            def _(t):
                t.wait_ge(self.sems["ldw"], ldw_need)
                for tc in range(NTC):
                    t.wait_ge(self.sems["lda"], lda_tc(tc))
                    voff = (tc % 2) * 16 * 512
                    for m in range(M_INPROJ):
                        if m < 8:
                            if tc * 8 + m >= 2:
                                pi = tc * 8 + m - 2
                                t.wait_ge(self.sems["eva"],
                                          eva_silu(pi // 8, pi % 8))
                            pb = ps_z[:, (m % 2) * 512:(m % 2) * 512 + 512]
                        else:
                            oi = o_idx(tc, m)
                            if oi >= 2:
                                ptc, pm = o_from_idx(oi - 2)
                                t.wait_ge(self.sems["evd"], evd_copy(ptc, pm))
                            pb = ps_o[:, (oi % 2) * 512:(oi % 2) * 512 + 512]
                        for k in range(16):
                            self.pe_mm(
                                pb,
                                w_in_sb[:, m * 2048 + k * 128:
                                        m * 2048 + (k + 1) * 128],
                                vbuf[:, voff + k * 512:voff + (k + 1) * 512],
                                start=(k == 0), stop=(k == 15))
                        self.pe_drain()
                    t.wait_ge(self.sems["evd"], evd_copy(tc, 20))
                    for gp in range(NGP):
                        i = tc * NGP + gp
                        if i >= 2:
                            ptc, pgp = (i - 2) // NGP, (i - 2) % NGP
                            t.wait_ge(self.sems["evd"], evd_dtfin(ptc, pgp))
                        self.pe_mm(ps_dt[:, (gp % 2) * 512:(gp % 2) * 512 + 512],
                                   w_dt_sb[:, gp * 128:(gp + 1) * 128],
                                   dtraw[:, (tc % 2) * 512:(tc % 2) * 512 + 512],
                                   start=True, stop=True)
                        self.pe_drain()

            @blk.sync
            def _(s):
                if wait_cc is not None:
                    self.wait(s, "cc", wait_cc)
                for tc in range(NTC):
                    if tc >= 1:
                        s.wait_ge(self.sems["lda"], lda_tc(tc - 1))
                    if tc >= 2:
                        s.wait_ge(self.sems["pe"], pe_tile(tc - 2, M_INPROJ - 1))
                    r, cb = tc // 2, (tc % 2) * 512
                    for k in range(16):
                        self.inc(s.dma_start(
                            out=vbuf[:, (tc % 2) * 16 * 512 + k * 512:
                                     (tc % 2) * 16 * 512 + (k + 1) * 512],
                            in_=v_src[2048 * r + 128 * k:2048 * r + 128 * (k + 1),
                                      cb:cb + 512]), "lda", 16)

            @blk.scalar
            def _(sc):
                for tc in range(NTC):
                    if tc >= 1:
                        self.wait(sc, "stg", stg_tc(tc - 1))
                    for zi in range(8):
                        sc.wait_ge(self.sems["pe"], pe_tile(tc, zi))
                        self.inc(sc.activation(
                            zst[:, zi * 512:(zi + 1) * 512],
                            ps_z[:, (zi % 2) * 512:(zi % 2) * 512 + 512],
                            AF.Silu), "eva", 1)
                    for gp in range(NGP):
                        i = tc * NGP + gp
                        if i >= 2:
                            ptc, pgp = (i - 2) // NGP, (i - 2) % NGP
                            sc.wait_ge(self.sems["evd"], evd_dtfin(ptc, pgp))
                        sc.wait_ge(self.sems["pe"], pe_dt(tc, gp))
                        h = (gp % 2) * 512
                        sc.activation(sp2[:, h:h + 512], ps_dt[:, h:h + 512],
                                      AF.Relu, bias=self.dtb_sb[:, gp:gp + 1])
                        sc.activation(sp1[:, h:h + 512], ps_dt[:, h:h + 512],
                                      AF.Abs, bias=self.dtb_sb[:, gp:gp + 1])
                        sc.activation(sp1[:, h:h + 512], sp1[:, h:h + 512],
                                      AF.Exp, scale=-1.0)
                        self.inc(sc.activation(sp1[:, h:h + 512], sp1[:, h:h + 512],
                                               AF.Ln, bias=1.0), "eva", 1)

            @blk.vector
            def _(v):
                for tc in range(NTC):
                    if tc >= 1:
                        self.wait(v, "stg", stg_tc(tc - 1))
                    for m in range(8, M_INPROJ):
                        v.wait_ge(self.sems["pe"], pe_tile(tc, m))
                        if m == 20 and tc >= 2:
                            v.wait_ge(self.sems["pe"], pe_dt(tc - 2, NGP - 1))
                        oi = o_idx(tc, m)
                        srcp = ps_o[:, (oi % 2) * 512:(oi % 2) * 512 + 512]
                        if m < 10:
                            dst = xbst[:, (m - 8) * 512:(m - 7) * 512]
                        elif m < 12:
                            dst = bst[:, (m - 10) * 512:(m - 9) * 512]
                        elif m < 20:
                            dst = cst[:, (m - 12) * 512:(m - 11) * 512]
                        else:
                            dst = dtraw[:, (tc % 2) * 512:(tc % 2) * 512 + 512]
                        self.inc(v.tensor_copy(dst, srcp), "evd", 1)
                    for gp in range(NGP):
                        h = (gp % 2) * 512
                        v.wait_ge(self.sems["eva"], eva_u3(tc, gp))
                        self.inc(v.tensor_add(dtst[:, gp * 512:(gp + 1) * 512],
                                              sp2[:, h:h + 512],
                                              sp1[:, h:h + 512]), "evd", 1)

            @blk.gpsimd
            def _(g):
                for m in range(M_INPROJ):
                    self.inc(g.dma_start(
                        out=w_in_sb[:, m * 2048:(m + 1) * 2048].rearrange(
                            "p (k c) -> p k c", k=16),
                        in_=t_in["w_inproj"][m].rearrange(
                            "(k p) c -> p k c", p=128)), "ldw", 16)
                self.inc(g.dma_start(
                    out=w_dt_sb[:, :].rearrange("p (g c) -> p g c", g=NGP),
                    in_=t_in["w_dtproj"].rearrange("g p c -> p g c")), "ldw", 16)
                for tc in range(NTC):
                    g.wait_ge(self.sems["eva"], eva0 + (tc + 1) * 16)
                    g.wait_ge(self.sems["evd"], evd0 + (tc + 1) * 21)
                    c0 = 512 * tc
                    for dram, sb, nblk in ((bzs["sz_b"], zst, 8),
                                           (bzs["xr_b"], xbst, 2),
                                           (bzs["br_b"], bst, 2),
                                           (bzs["cr_b"], cst, 8),
                                           (bzs["dt_b"], dtst, 8)):
                        self.inc(g.dma_start(
                            out=dram.ap()[:, c0:c0 + 512].rearrange(
                                "(a p) t -> p a t", p=128),
                            in_=sb[:, :].rearrange("p (a t) -> p a t", a=nblk)),
                            "stg", 16)

    def build_m2(self, bzs, yg_dst, state_dbg=None):
        nc = self.nc
        stg_m1_done = self.cnt["stg"]
        import contextlib
        with contextlib.ExitStack() as es:
            E = es.enter_context
            dtb2 = E(nc.sbuf_tensor("dtb2", [128, 2 * NGP * SUB], F32))
            szb2 = E(nc.sbuf_tensor("szb2", [128, NGP * SUB], BF16))
            crb2 = E(nc.sbuf_tensor("crb2", [128, NGP * SUB], BF16))
            brb2 = E(nc.sbuf_tensor("brb2", [128, 2 * SUB], BF16))
            xrb2 = E(nc.sbuf_tensor("xrb2", [128, 2 * SUB], BF16))
            brep = E(nc.sbuf_tensor("brep", [128, 16 * SUB], BF16))
            crep = E(nc.sbuf_tensor("crep", [128, 16 * SUB], BF16))
            xexp = E(nc.sbuf_tensor("xexp", [128, 2 * (SUB + 3)], F32))
            xtail = E(nc.sbuf_tensor("xtail", [128, NGP * 3], F32))
            xcb = E(nc.sbuf_tensor("xcb", [128, 2 * SUB], F32))
            dtxb = E(nc.sbuf_tensor("dtxb", [128, 2 * SUB], F32))
            dab = E(nc.sbuf_tensor("dab", [128, 4 * SUB], F32))
            ub = E(nc.sbuf_tensor("ub", [128, 4 * SUB], F32))
            hb = E(nc.sbuf_tensor("hb", [128, 4 * SUB], BF16))
            tmpb = E(nc.sbuf_tensor("tmpb", [128, 16 * SUB], BF16))
            yb = E(nc.sbuf_tensor("yb", [128, 2 * SUB], F32))
            ygst = E(nc.sbuf_tensor("ygst", [128, 2 * NGP * SUB], BF16))
            stateb = E(nc.sbuf_tensor("stateb", [128, 128], F32))
            ps_rb = E(nc.psum_tensor("ps_rb", [128, 2 * 512], F32))
            ps_rc = E(nc.psum_tensor("ps_rc", [128, 2 * 512], F32))
            ps_x = E(nc.psum_tensor("ps_x", [128, 2 * 512], F32))
            blk = E(nc.Block())
            pe0 = self.pe_issued
            eva0 = self.cnt["eva"]
            evd0 = self.cnt["evd"]
            evg0 = self.cnt["evg"]
            lda0 = self.cnt["lda"]
            stg0 = self.cnt["stg"]

            def pe_brep(s, n):
                return pe0 + s * 88 + 2 * (n + 1)

            def pe_crep(s, n):
                return pe0 + s * 88 + 32 + 2 * (n + 1)

            def pe_xexp(s, gp):
                return pe0 + s * 88 + 64 + 3 * (gp + 1)

            EVD_PER_GP = 1 + 1 + 1 + 16 + 1
            EVD_PER_S = 16 + NGP * EVD_PER_GP

            def evd_brepev(s, n):
                return evd0 + s * EVD_PER_S + n + 1

            def evd_gp_base(s, gp):
                return evd0 + s * EVD_PER_S + 16 + gp * EVD_PER_GP

            def evd_xexpev(s, gp):
                return evd_gp_base(s, gp) + 1

            def evd_conv(s, gp):
                return evd_gp_base(s, gp) + 2

            def evd_dtx(s, gp):
                return evd_gp_base(s, gp) + 3

            def evd_scan(s, gp, n):
                return evd_gp_base(s, gp) + 4 + n

            def evd_yg(s, gp):
                return evd_gp_base(s, gp) + 4 + 16

            EVA_PER_GP = 1 + 16
            EVA_PER_S = 16 + NGP * EVA_PER_GP

            def eva_crepev(s, n):
                return eva0 + s * EVA_PER_S + n + 1

            def eva_silu(s, gp):
                return eva0 + s * EVA_PER_S + 16 + gp * EVA_PER_GP + 1

            def eva_da(s, gp, n):
                return eva0 + s * EVA_PER_S + 16 + gp * EVA_PER_GP + 1 + n + 1

            EVG_PER_GP = 16

            def evg_u(s, gp, n):
                return evg0 + (s * NGP + gp) * EVG_PER_GP + n + 1

            def lda_s(s):
                return lda0 + (s + 1) * 5 * 16

            def stg_s(s):
                return stg0 + (s + 1) * 16

            def lin_gp(s, gp):
                return s * NGP + gp

            def lin_n(s, gp, n):
                return (s * NGP + gp) * 16 + n

            def from_lin_n(ln):
                return ln // (NGP * 16), (ln // 16) % NGP, ln % 16

            @blk.sync
            def _(s_eng):
                s_eng.wait_ge(self.sems["stg"], stg_m1_done)
                for s in range(NSUB):
                    par = s % 2
                    c0 = s * SUB
                    if s >= 1:
                        s_eng.wait_ge(self.sems["lda"], lda_s(s - 1))
                    if s >= 2:
                        s_eng.wait_ge(self.sems["evd"], evd_yg(s - 2, NGP - 1))
                    self.inc(s_eng.dma_start(
                        out=dtb2[:, par * NGP * SUB:(par + 1) * NGP * SUB].rearrange(
                            "p (a t) -> p a t", a=8),
                        in_=bzs["dt_b"].ap()[:, c0:c0 + SUB].rearrange(
                            "(a p) t -> p a t", p=128)), "lda", 16)
                    if s >= 1:
                        s_eng.wait_ge(self.sems["evd"], evd_yg(s - 1, NGP - 1))
                    for dram, buf, nblk in ((bzs["sz_b"], szb2, 8),
                                            (bzs["cr_b"], crb2, 8),
                                            (bzs["br_b"], brb2, 2),
                                            (bzs["xr_b"], xrb2, 2)):
                        self.inc(s_eng.dma_start(
                            out=buf[:, 0:nblk * SUB].rearrange(
                                "p (a t) -> p a t", a=nblk),
                            in_=dram.ap()[:, c0:c0 + SUB].rearrange(
                                "(a p) t -> p a t", p=128)), "lda", 16)

            @blk.tensor
            def _(t):
                for s in range(NSUB):
                    t.wait_ge(self.sems["lda"], lda_s(s))
                    par = s % 2
                    for half in range(2):
                        for n in range(16):
                            i = s * 16 + n
                            if i >= 2:
                                ps2, n2 = (i - 2) // 16, (i - 2) % 16
                                if half == 0:
                                    t.wait_ge(self.sems["evd"],
                                              evd_brepev(ps2, n2))
                                else:
                                    t.wait_ge(self.sems["eva"],
                                              eva_crepev(ps2, n2))
                            if half == 0:
                                self.pe_mm(
                                    ps_rb[:, (n % 2) * 512:(n % 2) * 512 + 512],
                                    self.rb_sb[:, n * 128:(n + 1) * 128],
                                    brb2[:, (n // 8) * SUB:
                                         (n // 8 + 1) * SUB],
                                    start=True, stop=True)
                            else:
                                self.pe_mm(
                                    ps_rc[:, (n % 2) * 512:(n % 2) * 512 + 512],
                                    self.rc_sb[:, (n % 2) * 128:
                                               (n % 2) * 128 + 128],
                                    crb2[:, (n // 2) * SUB:
                                         (n // 2 + 1) * SUB],
                                    start=True, stop=True)
                            self.pe_drain()
                    for gp in range(NGP):
                        i = lin_gp(s, gp)
                        if i >= 2:
                            ps2, pgp = (i - 2) // NGP, (i - 2) % NGP
                            t.wait_ge(self.sems["evd"], evd_xexpev(ps2, pgp))
                        for j in range(2):
                            self.pe_mm(ps_x[:, (gp % 2) * 512:(gp % 2) * 512 + 512],
                                       self.rx_sb[:, gp * 256 + j * 128:
                                                  gp * 256 + (j + 1) * 128],
                                       xrb2[:, j * SUB:(j + 1) * SUB],
                                       start=(j == 0), stop=(j == 1))
                        self.pe_drain()

            @blk.scalar
            def _(sc):
                for s in range(NSUB):
                    par = s % 2
                    for n in range(16):
                        sc.wait_ge(self.sems["pe"], pe_crep(s, n))
                        if s >= 2:
                            sc.wait_ge(self.sems["evd"],
                                       evd_scan(s - 2, NGP - 1, n))
                        self.inc(sc.activation(
                            crep[:, n * SUB:(n + 1) * SUB],
                            ps_rc[:, (n % 2) * 512:(n % 2) * 512 + 512],
                            AF.Copy), "eva", 1)
                    for gp in range(NGP):
                        sc.wait_ge(self.sems["evd"], evd_conv(s, gp))
                        self.inc(sc.activation(
                            xcb[:, (gp % 2) * SUB:(gp % 2) * SUB + SUB],
                            xcb[:, (gp % 2) * SUB:(gp % 2) * SUB + SUB],
                            AF.Silu, bias=self.convb_sb[:, gp:gp + 1]), "eva", 1)
                        for n in range(16):
                            pn = lin_n(s, gp, n) - 4
                            if pn >= 0:
                                sc.wait_ge(self.sems["evd"],
                                           evd_scan(*from_lin_n(pn)))
                            self.inc(sc.activation(
                                dab[:, (n % 4) * SUB:(n % 4 + 1) * SUB],
                                dtb2[:, par * NGP * SUB + gp * SUB:
                                     par * NGP * SUB + (gp + 1) * SUB],
                                AF.Exp,
                                scale=self.a_sb[:, gp * 16 + n:gp * 16 + n + 1]),
                                "eva", 1)

            @blk.gpsimd
            def _(g):
                for s in range(NSUB):
                    for gp in range(NGP):
                        g.wait_ge(self.sems["evd"], evd_dtx(s, gp))
                        for n in range(16):
                            g.wait_ge(self.sems["evd"], evd_brepev(s, n))
                            pn = lin_n(s, gp, n) - 4
                            if pn >= 0:
                                g.wait_ge(self.sems["evd"],
                                          evd_scan(*from_lin_n(pn)))
                            self.inc(g.tensor_mul(
                                ub[:, (n % 4) * SUB:(n % 4 + 1) * SUB],
                                dtxb[:, (gp % 2) * SUB:(gp % 2) * SUB + SUB],
                                brep[:, n * SUB:(n + 1) * SUB]), "evg", 1)
                    g.wait_ge(self.sems["evd"], evd_yg(s, NGP - 1))
                    self.inc(g.dma_start(
                        out=yg_dst[:, s * SUB:(s + 1) * SUB].rearrange(
                            "(a p) t -> p a t", p=128),
                        in_=ygst[:, (s % 2) * NGP * SUB:
                                 ((s % 2) + 1) * NGP * SUB].rearrange(
                            "p (a t) -> p a t", a=NGP)), "stg", 16)
                    if state_dbg is not None:
                        self.inc(g.dma_start(
                            out=state_dbg[:, s * 128:(s + 1) * 128],
                            in_=stateb[:, :]), "stg", 16)

            @blk.vector
            def _(v):
                for s in range(NSUB):
                    par = s % 2
                    v.wait_ge(self.sems["lda"], lda_s(s))
                    if s == 0:
                        v.memset(stateb[:, :], 0.0)
                        v.memset(xtail[:, :], 0.0)
                    for n in range(16):
                        v.wait_ge(self.sems["pe"], pe_brep(s, n))
                        if s >= 2:
                            v.wait_ge(self.sems["evg"],
                                      evg_u(s - 2, NGP - 1, n))
                        self.inc(v.tensor_copy(
                            brep[:, n * SUB:(n + 1) * SUB],
                            ps_rb[:, (n % 2) * 512:(n % 2) * 512 + 512]), "evd", 1)
                    for gp in range(NGP):
                        gx = (gp % 2) * (SUB + 3)
                        v.wait_ge(self.sems["pe"], pe_xexp(s, gp))
                        self.inc(v.tensor_copy(
                            xexp[:, gx + 3:gx + 3 + SUB],
                            ps_x[:, (gp % 2) * 512:(gp % 2) * 512 + 512]), "evd", 1)
                        v.tensor_copy(xexp[:, gx:gx + 3],
                                      xtail[:, gp * 3:(gp + 1) * 3])
                        pgl = lin_gp(s, gp) - 2
                        if pgl >= 0:
                            psx, pgp = pgl // NGP, pgl % NGP
                            v.wait_ge(self.sems["eva"], eva_silu(psx, pgp))
                        v.tensor_scalar_mul(
                            xcb[:, (gp % 2) * SUB:(gp % 2 + 1) * SUB],
                            xexp[:, gx:gx + SUB],
                            self.convw_sb[:, 4 * gp:4 * gp + 1])
                        for tau in range(1, 4):
                            v.scalar_tensor_tensor(
                                xcb[:, (gp % 2) * SUB:(gp % 2 + 1) * SUB],
                                xexp[:, gx + tau:gx + tau + SUB],
                                self.convw_sb[:, 4 * gp + tau:4 * gp + tau + 1],
                                xcb[:, (gp % 2) * SUB:(gp % 2 + 1) * SUB],
                                ALU.mult, ALU.add)
                        self.inc(v.tensor_copy(xtail[:, gp * 3:(gp + 1) * 3],
                                               xexp[:, gx + SUB:gx + SUB + 3]),
                                 "evd", 1)
                        if pgl >= 0:
                            psx, pgp = pgl // NGP, pgl % NGP
                            v.wait_ge(self.sems["evg"], evg_u(psx, pgp, 15))
                        v.wait_ge(self.sems["eva"], eva_silu(s, gp))
                        self.inc(v.tensor_mul(
                            dtxb[:, (gp % 2) * SUB:(gp % 2 + 1) * SUB],
                            dtb2[:, par * NGP * SUB + gp * SUB:
                                 par * NGP * SUB + (gp + 1) * SUB],
                            xcb[:, (gp % 2) * SUB:(gp % 2 + 1) * SUB]), "evd", 1)
                        for n in range(16):
                            v.wait_ge(self.sems["eva"], eva_da(s, gp, n))
                            v.wait_ge(self.sems["evg"], evg_u(s, gp, n))
                            col = gp * 16 + n
                            v.tensor_tensor_scan(
                                hb[:, (n % 4) * SUB:(n % 4 + 1) * SUB],
                                dab[:, (n % 4) * SUB:(n % 4 + 1) * SUB],
                                ub[:, (n % 4) * SUB:(n % 4 + 1) * SUB],
                                stateb[:, col:col + 1], ALU.mult, ALU.add)
                            v.wait_ge(self.sems["eva"], eva_crepev(s, n))
                            self.inc(v.tensor_mul(
                                tmpb[:, n * SUB:(n + 1) * SUB],
                                hb[:, (n % 4) * SUB:(n % 4 + 1) * SUB],
                                crep[:, n * SUB:(n + 1) * SUB]), "evd", 1)
                            v.tensor_copy(
                                stateb[:, col:col + 1],
                                hb[:, (n % 4 + 1) * SUB - 1:(n % 4 + 1) * SUB])
                        v.tensor_reduce(
                            yb[:, SUB:2 * SUB],
                            tmpb[:, :].rearrange("p (n t) -> p t n", n=16),
                            AX.X, ALU.add)
                        v.scalar_tensor_tensor(
                            yb[:, 0:SUB],
                            xcb[:, (gp % 2) * SUB:(gp % 2 + 1) * SUB],
                            self.dpar_sb[:, gp:gp + 1],
                            yb[:, SUB:2 * SUB], ALU.mult, ALU.add)
                        if s >= 2:
                            v.wait_ge(self.sems["stg"], stg_s(s - 2))
                        self.inc(v.tensor_mul(
                            ygst[:, par * NGP * SUB + gp * SUB:
                                 par * NGP * SUB + (gp + 1) * SUB],
                            yb[:, 0:SUB],
                            szb2[:, gp * SUB:(gp + 1) * SUB]), "evd", 1)


def _host_prep_mixer(inputs, c):
    w_in = np.asarray(inputs["L0_attn_in_proj_w"])
    conv_w = np.asarray(inputs["L0_attn_conv_w"])
    conv_b = np.asarray(inputs["L0_attn_conv_b"])
    dtw = np.asarray(inputs["L0_attn_dt_proj_w"])
    dtb = np.asarray(inputs["L0_attn_dt_proj_b"])
    a_log = np.asarray(inputs["L0_attn_A_log"])
    d_par = np.asarray(inputs["L0_attn_D_param"])
    ln1 = np.asarray(inputs["L0_attn_ln1_w"])
    ln2 = np.asarray(inputs["L0_attn_ln2_w"])
    q = c % 4
    d = {}
    rows = []
    for gp in range(NGP):
        rows.append(1024 * q + _dlocal(gp))
    for j in range(2):
        p = np.arange(128)
        rows.append(4096 + 256 * q + 16 * (p % 16) + 8 * j + p // 16)
    for j in range(2):
        p = np.arange(128)
        rows.append(5120 + 256 * q + 16 * (p % 16) + 8 * j + p // 16)
    for j in range(8):
        p = np.arange(128)
        rows.append(6144 + 1024 * q + 16 * (p % 64) + 2 * j + p // 64)
    rows.append(10240 + np.arange(128))
    rows = np.concatenate(rows)
    w_c = w_in[rows, :]
    d["w_inproj"] = np.ascontiguousarray(
        w_c.reshape(M_INPROJ, 128, 2048).transpose(0, 2, 1))
    d["w_dtproj"] = np.ascontiguousarray(
        np.stack([dtw[1024 * q + _dlocal(gp), :].T for gp in range(NGP)]))
    d["dt_bias"] = np.ascontiguousarray(
        np.stack([dtb[1024 * q + _dlocal(gp)] for gp in range(NGP)], axis=1))
    d["a_log_p"] = np.ascontiguousarray(np.concatenate(
        [a_log[1024 * q + _dlocal(gp), :] for gp in range(NGP)], axis=1))
    d["conv_w_p"] = np.ascontiguousarray(np.concatenate(
        [conv_w[1024 * q + _dlocal(gp), 0, :] for gp in range(NGP)], axis=1))
    d["conv_b_p"] = np.ascontiguousarray(
        np.stack([conv_b[1024 * q + _dlocal(gp)] for gp in range(NGP)], axis=1))
    d["d_par_p"] = np.ascontiguousarray(
        np.stack([d_par[1024 * q + _dlocal(gp)] for gp in range(NGP)], axis=1))
    d["ln1_w"] = np.ascontiguousarray(ln1.reshape(16, 128).T)
    d["ln2_w"] = np.ascontiguousarray(ln2.reshape(16, 128).T)
    return d


MIXER_IN_NAMES = ["w_inproj", "w_dtproj", "dt_bias", "a_log_p", "conv_w_p",
                  "conv_b_p", "d_par_p", "ln1_w", "ln2_w"]
MIXER_IN_SHAPES = {
    "w_inproj": ([M_INPROJ, 2048, 128], F32),
    "w_dtproj": ([NGP, 128, 128], F32),
    "dt_bias": ([128, NGP], F32),
    "a_log_p": ([128, NGP * 16], F32),
    "conv_w_p": ([128, NGP * 4], F32),
    "conv_b_p": ([128, NGP], F32),
    "d_par_p": ([128, NGP], F32),
    "ln1_w": ([128, 16], F32),
    "ln2_w": ([128, 16], F32),
}


def _declare_mixer_inputs(nc):
    t_in = {}
    for name in MIXER_IN_NAMES:
        shp, dt = MIXER_IN_SHAPES[name]
        t_in[name] = nc.dram_tensor(name, shp, dt, kind="ExternalInput").ap()
    return t_in


def _mixer_bounces(nc, kind="Internal"):
    return {
        "sz_b": nc.dram_tensor("sz_b", [1024, SEQ], BF16, kind=kind),
        "xr_b": nc.dram_tensor("xr_b", [256, SEQ], BF16, kind=kind),
        "br_b": nc.dram_tensor("br_b", [256, SEQ], BF16, kind=kind),
        "cr_b": nc.dram_tensor("cr_b", [1024, SEQ], BF16, kind=kind),
        "dt_b": nc.dram_tensor("dt_b", [1024, SEQ], F32, kind=kind),
    }


def _alloc_repl(cp, es):
    nc = cp.nc
    E = es.enter_context
    cp.rb_sb = E(nc.sbuf_tensor("rb_sb", [128, 16 * 128], BF16, side="right"))
    cp.rc_sb = E(nc.sbuf_tensor("rc_sb", [128, 2 * 128], BF16, side="right"))
    cp.rx_sb = E(nc.sbuf_tensor("rx_sb", [128, 16 * 128], BF16, side="right"))


def build_mixer_test(phases="all"):
    import contextlib
    nc = bass.Bass("TRN2", target_bir_lowering=False, debug=False,
                   detect_race_conditions=False)
    cp = CoreProgram(nc)
    cp.sem_init()
    _alloc_repl(cp, contextlib.ExitStack())
    t_in = _declare_mixer_inputs(nc)
    v_src = nc.dram_tensor("v_src", [4 * 2048, 1024], BF16,
                           kind="ExternalInput").ap()
    yg_out = nc.dram_tensor("yg_out", [1024, SEQ], BF16,
                            kind="ExternalOutput").ap()
    import os as _ost
    bzs = _mixer_bounces(nc, "ExternalOutput" if _ost.environ.get("DBG_BOUNCE") == "1" else "Internal")
    cp.build_p0(t_in)
    if phases in ("p0m1", "all"):
        cp.build_m1(t_in, v_src, bzs)
    if phases == "all":
        sdbg = None
        if _ost.environ.get("DBG_STATE") == "1":
            sdbg = nc.dram_tensor("state_dbg", [128, NSUB * 128], F32,
                                  kind="ExternalOutput").ap()
        cp.build_m2(bzs, yg_out, state_dbg=sdbg)
    else:
        with (nc.sbuf_tensor("zf", [128, SEQ], BF16) as zf, nc.Block() as blk):
            @blk.vector
            def _(v):
                cp.inc(v.memset(zf[:, :], 0.0), "evd", 1)

            @blk.gpsimd
            def _(g):
                g.wait_ge(cp.sems["evd"], cp.cnt["evd"])
                for j in range(8):
                    cp.inc(g.dma_start(out=yg_out[128 * j:128 * (j + 1), :],
                                       in_=zf[:, :]), "stg", 16)
    with nc.Block() as blk:
        @blk.sync
        def _(s):
            s.wait_ge(cp.sems["stg"], cp.cnt["stg"])
    return nc, cp


def kernel(**inputs):
    raise NotImplementedError("full kernel assembled in a later stage")


def emit_norm512(cp, tb, x_tile, out_tile, mode, eps, nk=16,
                 w_col=None, add_tile=None):
    eps_col = {EPS_LN: 0, EPS_FFN: 1}[eps]
    nc = cp.nc
    ones_col, ones_row = tb["ones_col"], tb["ones_row"]
    sq16, rows, rows16 = tb["sq16"], tb["rows"], tb["rows16"]
    rep, tmpf, psR = tb["rep"], tb["tmpf"], tb["psR"]
    need_mean = (mode == "ln")

    with nc.Block() as blk:
        eva0 = cp.cnt["eva"]
        evd0 = cp.cnt["evd"]
        pe0b = cp.pe_issued
        mk_sq = pe0b + nk + 1
        mk_sums = pe0b + (nk + 1) * (2 if need_mean else 1)
        mk_bcast = mk_sums + (3 if need_mean else 2)

        @blk.scalar
        def _(sc):
            for k in range(nk):
                cp.inc(sc.activation(sq16[:, k * 512:(k + 1) * 512],
                                     x_tile(k), AF.Square), "eva", 1)
            if need_mean:
                sc.wait_ge(cp.sems["pe"], mk_sq)
                for k in range(nk):
                    cp.inc(sc.activation(
                        sq16[:, k * 512:(k + 1) * 512],
                        x_tile(k), AF.Copy), "eva", 1)
            sc.wait_ge(cp.sems["evd"], evd0 + 1)
            sc.activation(rows[0:1, 0:512], rows[0:1, 0:512], AF.Ln,
                          bias=cp.eps_sb[0:1, eps_col:eps_col + 1])
            cp.inc(sc.activation(rows[0:1, 0:512], rows[0:1, 0:512],
                                 AF.Exp, scale=cp.eps_sb[0:1, 2:3]), "eva", 1)

        @blk.tensor
        def _(t):
            t.wait_ge(cp.sems["eva"], eva0 + nk)
            for k in range(nk):
                cp.pe_mm(psR[0:1, 0:512], ones_col[:, 0:1],
                         sq16[:, k * 512:(k + 1) * 512],
                         start=(k == 0), stop=(k == nk - 1))
            cp.pe_drain()
            if need_mean:
                t.wait_ge(cp.sems["eva"], eva0 + 2 * nk)
                for k in range(nk):
                    cp.pe_mm(psR[0:1, 512:1024], ones_col[:, 0:1],
                             sq16[:, k * 512:(k + 1) * 512],
                             start=(k == 0), stop=(k == nk - 1))
                cp.pe_drain()
            t.wait_ge(cp.sems["evd"], evd0 + 2)
            cp.pe_mm(psR[:, 0:512], ones_row[0:1, 0:128],
                     rows16[0:1, 0:512], start=True, stop=True)
            if need_mean:
                cp.pe_mm(psR[:, 512:1024], ones_row[0:1, 0:128],
                         rows16[0:1, 512:1024], start=True, stop=True)
            cp.pe_drain()

        @blk.vector
        def _(v):
            D = 128 * nk
            v.wait_ge(cp.sems["pe"], mk_sums)
            v.tensor_scalar_mul(rows[0:1, 0:512], psR[0:1, 0:512], 1.0 / D)
            if need_mean:
                v.tensor_scalar_mul(rows[0:1, 512:1024],
                                    psR[0:1, 512:1024], 1.0 / D)
                v.tensor_mul(rows[0:1, 1024:1536], rows[0:1, 512:1024],
                             rows[0:1, 512:1024])
                cp.inc(v.tensor_sub(rows[0:1, 0:512], rows[0:1, 0:512],
                                    rows[0:1, 1024:1536]), "evd", 1)
            else:
                cp.inc(v.engine_nop(), "evd", 1)
            v.wait_ge(cp.sems["eva"], eva0 + nk * (2 if need_mean else 1) + 1)
            v.tensor_copy(rows16[0:1, 0:512], rows[0:1, 0:512])
            if need_mean:
                v.tensor_mul(rows[0:1, 1024:1536], rows[0:1, 512:1024],
                             rows[0:1, 0:512])
                v.tensor_scalar_mul(rows[0:1, 1024:1536],
                                    rows[0:1, 1024:1536], -1.0)
                cp.inc(v.tensor_copy(rows16[0:1, 512:1024],
                                     rows[0:1, 1024:1536]), "evd", 1)
            else:
                cp.inc(v.tensor_copy(rows16[0:1, 512:1024],
                                     rows16[0:1, 0:512]), "evd", 1)
            v.wait_ge(cp.sems["pe"], mk_bcast)
            v.tensor_copy(rep[:, 0:512], psR[:, 0:512])
            if need_mean:
                v.tensor_copy(rep[:, 512:1024], psR[:, 512:1024])
            for k in range(nk):
                if mode == "ln":
                    v.tensor_mul(tmpf[:, 0:512], x_tile(k), rep[:, 0:512])
                    if add_tile is not None:
                        v.tensor_add(tmpf[:, 0:512], tmpf[:, 0:512],
                                     rep[:, 512:1024])
                        cp.inc(v.tensor_add(out_tile(k), tmpf[:, 0:512],
                                            add_tile(k)), "evd", 1)
                    else:
                        cp.inc(v.tensor_add(out_tile(k), tmpf[:, 0:512],
                                            rep[:, 512:1024]), "evd", 1)
                elif w_col is not None:
                    v.tensor_mul(tmpf[:, 0:512], x_tile(k), rep[:, 0:512])
                    cp.inc(v.tensor_scalar_mul(out_tile(k), tmpf[:, 0:512],
                                               w_col(k)), "evd", 1)
                else:
                    cp.inc(v.tensor_mul(out_tile(k), x_tile(k),
                                        rep[:, 0:512]), "evd", 1)


_NRM_SEQ = [0]


def _norm_scratch(nc, es, sq16=None):
    E = es.enter_context
    _NRM_SEQ[0] += 1
    sfx = f"_{_NRM_SEQ[0]}"
    tb = {}
    tb["sq16"] = sq16 if sq16 is not None else E(
        nc.sbuf_tensor("nrm_sq" + sfx, [128, 16 * 512], BF16))
    tb["rows"] = E(nc.sbuf_tensor("nrm_rows" + sfx, [1, 3 * 512], F32))
    tb["rows16"] = E(nc.sbuf_tensor("nrm_r16" + sfx, [1, 2 * 512], BF16))
    tb["rep"] = E(nc.sbuf_tensor("nrm_rep" + sfx, [128, 3 * 512], F32))
    tb["tmpf"] = tb["rep"][:, 1024:1536]
    tb["psR"] = E(nc.psum_tensor("nrm_psr" + sfx, [128, 2 * 512], F32))
    return tb


def build_consts(cp):
    nc = cp.nc
    oc = nc.inline_tensor(np.ones((128, 1), np.float32), name="ones_col").ap()
    orw = nc.inline_tensor(np.ones((1, 128), np.float32), name="ones_row").ap()
    ident = nc.inline_tensor(np.eye(128, dtype=np.float32), name="ident").ap()
    epsv = np.tile(np.array([[1e-5, 1e-6, -0.5, 0.0]], np.float32), (128, 1))
    epsd = nc.inline_tensor(epsv, name="eps_c").ap()
    cp.ones_col = cp.open(nc.sbuf_tensor("ones_col_sb", [128, 1], BF16))
    cp.ones_row = cp.open(nc.sbuf_tensor("ones_row_sb", [1, 128], BF16))
    cp.ident_sb = cp.open(nc.sbuf_tensor("ident_sb", [128, 128], BF16))
    cp.eps_sb = cp.open(nc.sbuf_tensor("eps_sb", [128, 4], F32))
    with nc.Block() as blk:
        @blk.gpsimd
        def _(g):
            cp.inc(g.dma_start(out=cp.ones_col[:, :], in_=oc[:, :]), "ldw", 16)
            cp.inc(g.dma_start(out=cp.ones_row[:, :], in_=orw[:, :]), "ldw", 16)
            cp.inc(g.dma_start(out=cp.ident_sb[:, :], in_=ident[:, :]), "ldw", 16)
            cp.inc(g.dma_start(out=cp.eps_sb[:, :], in_=epsd[:, :]), "ldw", 16)

        @blk.sync
        def _(s):
            s.wait_ge(cp.sems["ldw"], cp.cnt["ldw"])


def build_p1(cp, xT, v_mine):
    nc = cp.nc
    import contextlib
    with contextlib.ExitStack() as es:
        E = es.enter_context
        xb = E(nc.sbuf_tensor("p1_x", [128, 16 * 1024], F32))
        u0 = E(nc.sbuf_tensor("p1_u0", [128, 16 * 512], F32))
        vst = E(nc.sbuf_tensor("p1_v", [128, 16 * 512], BF16))
        tb = _norm_scratch(nc, es)
        tb["ones_col"] = cp.ones_col
        tb["ones_row"] = cp.ones_row

        with nc.Block() as blk:
            @blk.sync
            def _(s):
                cp.inc(s.dma_start(
                    out=xb[:, :].rearrange("p (k t) -> p k t", k=16),
                    in_=xT.rearrange("(k p) t -> p k t", p=128)), "lda", 16)

            @blk.vector
            def _(v):
                v.wait_ge(cp.sems["lda"], cp.cnt["lda"])
                cp.inc(v.engine_nop(), "evd", 1)

        for h in range(2):
            emit_norm512(cp, tb,
                         lambda k, h=h: xb[:, k * 1024 + h * 512:
                                           k * 1024 + h * 512 + 512],
                         lambda k: u0[:, k * 512:(k + 1) * 512],
                         "ln", EPS_LN)
            emit_norm512(cp, tb,
                         lambda k: u0[:, k * 512:(k + 1) * 512],
                         lambda k: vst[:, k * 512:(k + 1) * 512],
                         "rms", EPS_RMS,
                         w_col=lambda k: cp.ln1_sb[:, k:k + 1])
            with nc.Block() as blk:
                @blk.gpsimd
                def _(g):
                    g.wait_ge(cp.sems["evd"], cp.cnt["evd"])
                    cp.inc(g.dma_start(
                        out=v_mine.ap()[:, h * 512:(h + 1) * 512].rearrange(
                            "(k p) t -> p k t", p=128),
                        in_=vst[:, :].rearrange("p (k t) -> p k t", k=16)),
                        "stg", 16)
                    g.wait_ge(cp.sems["stg"], cp.cnt["stg"])


def build_dequant(cp, t_in, fp4_mine):
    nc = cp.nc
    import contextlib
    with contextlib.ExitStack() as es:
        E = es.enter_context
        pk = E(nc.sbuf_tensor("dq_pk", [128, 11 * 256 + 2 * 704], I32))
        sc_gu = E(nc.sbuf_tensor("dq_sgu", [128, 11 * 16], F32))
        sc_dn = E(nc.sbuf_tensor("dq_sdn", [128, 2 * 44], F32))
        nib = E(nc.sbuf_tensor("dq_nib", [128, 5632], I32))
        fx = E(nc.sbuf_tensor("dq_fx", [128, 5632], F32))
        t1 = E(nc.sbuf_tensor("dq_t1", [128, 5632], F32))
        t2 = E(nc.sbuf_tensor("dq_t2", [128, 5632], F32))
        dqb = E(nc.sbuf_tensor("dq_b", [128, 5632], BF16))
        stage = E(nc.sbuf_tensor("dq_st", [128, 44 * 128], BF16))
        pst = E(nc.psum_tensor("dq_ps", [128, 1024], BF16))
        blk = E(nc.Block())

        eva0 = cp.cnt["eva"]
        evd0 = cp.cnt["evd"]
        lda0 = cp.cnt["lda"]
        ldw0 = cp.cnt["ldw"]
        stg0 = cp.cnt["stg"]
        pe0 = cp.pe_issued

        tiles = [("gu", ti, 256, 2048, 16) for ti in range(11)] + \
                [("dn", ti, 704, 5632, 44) for ti in range(2)]

        def evd_tile(i):
            return evd0 + i + 1

        def eva_tile(i):
            kcs = [t[4] for t in tiles]
            return eva0 + sum(kcs[:i + 1])

        def pe_tile_t(i):
            kcs = [t[4] for t in tiles]
            return pe0 + sum(2 * k for k in kcs[:i + 1])

        @blk.sync
        def _(s):
            cp.inc(s.dma_start(out=pk[:, 0:11 * 256].rearrange(
                "p (a w) -> p a w", a=11),
                in_=t_in["gu_packed"].rearrange("(a p) w -> p a w", p=128)),
                "lda", 16)
            cp.inc(s.dma_start(out=pk[:, 11 * 256:].rearrange(
                "p (a w) -> p a w", a=2),
                in_=t_in["dn_packed"].rearrange("(a p) w -> p a w", p=128)),
                "lda", 16)

        @blk.gpsimd
        def _(g):
            cp.inc(g.dma_start(out=sc_gu[:, :], in_=t_in["gu_scales"]),
                   "ldw", 16)
            cp.inc(g.dma_start(out=sc_dn[:, :], in_=t_in["dn_scales"]),
                   "ldw", 16)
            for i, (kind, ti, wlen, inlen, kc) in enumerate(tiles):
                g.wait_ge(cp.sems["eva"], eva_tile(i))
                mofs = (ti if kind == "gu" else 11 * 16 * 128 // (kc * 128) * 0
                        ) * 0
                if kind == "gu":
                    dst = fp4_mine.ap()[ti * 16 * 128:(ti + 1) * 16 * 128, :]
                else:
                    base = 11 * 16 * 128
                    dst = fp4_mine.ap()[base + ti * 44 * 128:
                                        base + (ti + 1) * 44 * 128, :]
                cp.inc(g.dma_start(
                    out=dst.rearrange("(k p) c -> p k c", p=128),
                    in_=stage[:, 0:kc * 128].rearrange(
                        "p (k c) -> p k c", k=kc)), "stg", 16)

        @blk.vector
        def _(v):
            v.wait_ge(cp.sems["lda"], lda0 + 32)
            v.wait_ge(cp.sems["ldw"], ldw0 + 32)
            for i, (kind, ti, wlen, inlen, kc) in enumerate(tiles):
                if i >= 1:
                    v.wait_ge(cp.sems["eva"], eva_tile(i - 1))
                woff = ti * 256 if kind == "gu" else 11 * 256 + ti * 704
                wsl = pk[:, woff:woff + wlen]
                for kn in range(8):
                    v.tensor_scalar(
                        nib[:, 0:inlen].rearrange("p (w e) -> p e w",
                                                  e=8)[:, kn, :],
                        wsl, 4 * kn, 15,
                        ALU.logical_shift_right, ALU.bitwise_and)
                v.tensor_copy(fx[:, 0:inlen], nib[:, 0:inlen])
                v.tensor_single_scalar(t1[:, 0:inlen], fx[:, 0:inlen], 8.0,
                                       ALU.is_ge)
                v.scalar_tensor_tensor(fx[:, 0:inlen], t1[:, 0:inlen], -8.0,
                                       fx[:, 0:inlen], ALU.mult, ALU.add)
                v.tensor_scalar(t1[:, 0:inlen], t1[:, 0:inlen], -2.0, 1.0,
                                ALU.mult, ALU.add)
                v.tensor_scalar(t2[:, 0:inlen], fx[:, 0:inlen], -4.0, 0.0,
                                ALU.add, ALU.max)
                v.tensor_add(t2[:, 0:inlen], t2[:, 0:inlen], fx[:, 0:inlen])
                v.tensor_scalar(fx[:, 0:inlen], fx[:, 0:inlen], -6.0, 0.0,
                                ALU.add, ALU.max)
                v.scalar_tensor_tensor(t2[:, 0:inlen], t2[:, 0:inlen], 0.5,
                                       fx[:, 0:inlen], ALU.mult, ALU.add)
                v.tensor_mul(t2[:, 0:inlen], t2[:, 0:inlen], t1[:, 0:inlen])
                scs = sc_gu if kind == "gu" else sc_dn
                ng = inlen // 128
                for gidx in range(ng):
                    v.tensor_scalar_mul(
                        dqb[:, gidx * 128:(gidx + 1) * 128],
                        t2[:, gidx * 128:(gidx + 1) * 128],
                        scs[:, ti * ng + gidx:ti * ng + gidx + 1])
                cp.inc(v.engine_nop(), "evd", 1)

        @blk.tensor
        def _(t):
            for i, (kind, ti, wlen, inlen, kc) in enumerate(tiles):
                t.wait_ge(cp.sems["evd"], evd_tile(i))
                for k in range(kc):
                    if i * 1000 + k >= 2:
                        pass
                    tot_prev = sum(tt[4] for tt in tiles[:i]) + k
                    if tot_prev >= 2:
                        t.wait_ge(cp.sems["eva"], eva0 + tot_prev - 1)
                    cp.pe_transpose(
                        pst[:, (k % 2) * 128:(k % 2) * 128 + 128],
                        dqb[:, k * 128:(k + 1) * 128])
                    cp.pe_drain()

        @blk.scalar
        def _(sc):
            for i, (kind, ti, wlen, inlen, kc) in enumerate(tiles):
                if i >= 1:
                    sc.wait_ge(cp.sems["stg"], stg0 + i * 16)
                for k in range(kc):
                    tot = sum(tt[4] for tt in tiles[:i]) + k
                    sc.wait_ge(cp.sems["pe"], pe0 + 2 * (tot + 1))
                    cp.inc(sc.activation(stage[:, k * 128:(k + 1) * 128],
                                         pst[:, (k % 2) * 128:
                                             (k % 2) * 128 + 128],
                                         AF.Copy), "eva", 1)


def build_token(cp, t_in, yg_all, xT, fp4_all, out_d):
    nc = cp.nc
    import contextlib
    with contextlib.ExitStack() as es:
        E = es.enter_context
        AF1 = E(nc.sbuf_tensor("tk_af1", [128, 16 * 512], F32))
        AF2 = E(nc.sbuf_tensor("tk_af2", [128, 16 * 512], F32))
        AF3 = E(nc.sbuf_tensor("tk_af3", [128, 16 * 512], F32))
        AB1 = E(nc.sbuf_tensor("tk_ab1", [128, 64 * 512], BF16))
        AB2 = E(nc.sbuf_tensor("tk_ab2", [128, 16 * 512], BF16))
        wb1 = E(nc.sbuf_tensor("tk_wb1", [128, 2 * 16 * 128], BF16))
        sgt = E(nc.sbuf_tensor("tk_sgt", [128, 2 * 512], BF16))
        psA = E(nc.psum_tensor("tk_psa", [128, 2 * 512], F32))
        psD = E(nc.psum_tensor("tk_psd", [128, 2 * 512], F32))
        tb = _norm_scratch(nc, es, sq16=AB1[:, 44 * 512:60 * 512])
        tb["ones_col"] = cp.ones_col
        tb["ones_row"] = cp.ones_row

        def t16(buf):
            return lambda k: buf[:, k * 512:(k + 1) * 512]

        for w in range(2):
            my_t0 = 512 * w

            with nc.Block() as blk:
                @blk.sync
                def _(s):
                    cp.inc(s.dma_start(
                        out=AF1[:, :].rearrange("p (k t) -> p k t", k=16),
                        in_=xT[:, my_t0:my_t0 + 512].rearrange(
                            "(k p) t -> p k t", p=128)), "lda", 16)

                @blk.vector
                def _(v):
                    v.wait_ge(cp.sems["lda"], cp.cnt["lda"])
                    cp.inc(v.engine_nop(), "evd", 1)
            emit_norm512(cp, tb, t16(AF1), t16(AF2), "ln", EPS_LN)

            _mm_stage(cp, nc, wb1, psD, nm=16, nkc=32,
                      wsrc=lambda m, h, kw: t_in["w_outproj"][
                          m, h * kw:(h + 1) * kw].rearrange("k p c -> p k c"),
                      load_yg=(yg_all, my_t0), ygbuf=AB1,
                      moving=lambda k: AB1[:, k * 512:(k + 1) * 512],
                      evac=("dve_add", t16(AF2), t16(AF2)))

            emit_norm512(cp, tb, t16(AF2), t16(AB2), "rms", EPS_RMS,
                         w_col=lambda k: cp.ln2_sb[:, k:k + 1])

            _mlp_gateup(cp, nc, wb1, psA, psD, sgt, AB2, AB1,
                        t_in["w_gate"], t_in["w_up"], 64)

            _mm_stage(cp, nc, wb1, psD, nm=16, nkc=64,
                      wsrc=lambda m, h, kw: t_in["w_down"][
                          m, h * kw:(h + 1) * kw].rearrange("k p c -> p k c"),
                      load_yg=None, ygbuf=None,
                      moving=lambda k: AB1[:, k * 512:(k + 1) * 512],
                      evac=("dve_add", t16(AF2), t16(AF3)))
            with nc.Block() as blk:
                @blk.sync
                def _(s):
                    cp.inc(s.dma_start(
                        out=AF1[:, :].rearrange("p (k t) -> p k t", k=16),
                        in_=xT[:, my_t0:my_t0 + 512].rearrange(
                            "(k p) t -> p k t", p=128)), "lda", 16)

                @blk.vector
                def _(v):
                    v.wait_ge(cp.sems["lda"], cp.cnt["lda"])
                    cp.inc(v.engine_nop(), "evd", 1)
            emit_norm512(cp, tb, t16(AF3), t16(AF3), "ln", EPS_LN,
                         add_tile=t16(AF1))
            emit_norm512(cp, tb, t16(AF3), t16(AB2), "ln", EPS_LN)

            _ffn_gateup(cp, nc, wb1, psA, psD, sgt, AB2, AB1, fp4_all, 44)

            _ffn_down(cp, nc, wb1, psA, AB1, fp4_all, AF1)
            emit_norm512(cp, tb, t16(AF1), t16(AF1), "rmsdiv", EPS_FFN)
            emit_norm512(cp, tb, t16(AF1), t16(AF2), "ln", EPS_LN,
                         add_tile=t16(AF3))
            with nc.Block() as blk:
                @blk.gpsimd
                def _(g):
                    g.wait_ge(cp.sems["evd"], cp.cnt["evd"])
                    cp.inc(g.dma_start(
                        out=out_d[:, my_t0:my_t0 + 512].rearrange(
                            "(k p) t -> p k t", p=128),
                        in_=AF2[:, :].rearrange("p (k t) -> p k t", k=16)),
                        "sto", 16)


def _mm_stage(cp, nc, wbuf, ps, nm, nkc, wsrc, load_yg, ygbuf, moving, evac,
              kw=16):
    nchunk = nkc // kw
    with nc.Block() as blk:
        pe0 = cp.pe_issued
        ldw0 = cp.cnt["ldw"]
        lda0 = cp.cnt["lda"]
        evd0 = cp.cnt["evd"]

        def pe_m(m):
            return pe0 + (m + 1) * (nkc + 1)

        def pe_chunk(m, h):
            return pe0 + m * (nkc + 1) + (h + 1) * kw

        @blk.gpsimd
        def _(g):
            for m in range(nm):
                for h in range(nchunk):
                    gidx = m * nchunk + h
                    if gidx >= 2:
                        pm, ph = (gidx - 2) // nchunk, (gidx - 2) % nchunk
                        g.wait_ge(cp.sems["pe"], pe_chunk(pm, ph))
                    off = (gidx % 2) * kw * 128
                    cp.inc(g.dma_start(
                        out=wbuf[:, off:off + kw * 128].rearrange(
                            "p (k c) -> p k c", k=kw),
                        in_=wsrc(m, h, kw)), "ldw", 16)

        @blk.sync
        def _(s):
            if load_yg is not None:
                yg_all_ap, my_t0 = load_yg
                pid = nc.partition_id()
                off = (pid % 4) * 1024 + my_t0
                src_ap = bass.AP(yg_all_ap.tensor, off,
                                 [[4096, 128], [128 * 4096, 32], [1, 512]])
                cp.inc(s.dma_start(
                    out=ygbuf[:, 0:32 * 512].rearrange(
                        "p (k t) -> p k t", k=32),
                    in_=src_ap), "lda", 16)

        @blk.tensor
        def _(t):
            if load_yg is not None:
                t.wait_ge(cp.sems["lda"], lda0 + 16)
            for m in range(nm):
                if m >= 2:
                    t.wait_ge(cp.sems["evd"], evd0 + m - 1)
                for h in range(nchunk):
                    gidx = m * nchunk + h
                    t.wait_ge(cp.sems["ldw"], ldw0 + (gidx + 1) * 16)
                    off = (gidx % 2) * kw * 128
                    for k in range(kw):
                        cp.pe_mm(ps[:, (m % 2) * 512:(m % 2) * 512 + 512],
                                 wbuf[:, off + k * 128:off + (k + 1) * 128],
                                 moving(h * kw + k),
                                 start=(h == 0 and k == 0),
                                 stop=(h == nchunk - 1 and k == kw - 1))
                cp.pe_drain()

        @blk.vector
        def _(v):
            kind, a_t, o_t = evac
            for m in range(nm):
                v.wait_ge(cp.sems["pe"], pe_m(m))
                cp.inc(v.tensor_add(o_t(m), a_t(m),
                                    ps[:, (m % 2) * 512:(m % 2) * 512 + 512]),
                       "evd", 1)


def _mlp_gateup(cp, nc, wbuf, psA, psD, sgt, h2, hh, wg, wu, nm):
    with nc.Block() as blk:
        pe0 = cp.pe_issued
        ldw0 = cp.cnt["ldw"]
        eva0 = cp.cnt["eva"]
        evd0 = cp.cnt["evd"]

        def pe_pair(m):
            return pe0 + (m + 1) * 34

        @blk.gpsimd
        def _(g):
            for m in range(nm):
                for half, w in ((0, wg), (1, wu)):
                    if m >= 1:
                        g.wait_ge(cp.sems["pe"],
                                  pe0 + (m - 1) * 34 + (half + 1) * 17)
                    cp.inc(g.dma_start(
                        out=wbuf[:, half * 16 * 128:(half + 1) * 16 * 128
                                 ].rearrange("p (k c) -> p k c", k=16),
                        in_=w[m].rearrange("k p c -> p k c")), "ldw", 16)

        @blk.tensor
        def _(t):
            for m in range(nm):
                t.wait_ge(cp.sems["ldw"], ldw0 + (2 * m + 1) * 16)
                if m >= 2:
                    t.wait_ge(cp.sems["eva"], eva0 + m - 1)
                for k in range(16):
                    cp.pe_mm(psA[:, (m % 2) * 512:(m % 2) * 512 + 512],
                             wbuf[:, k * 128:(k + 1) * 128],
                             h2[:, k * 512:(k + 1) * 512],
                             start=(k == 0), stop=(k == 15))
                cp.pe_drain()
                t.wait_ge(cp.sems["ldw"], ldw0 + (2 * m + 2) * 16)
                if m >= 2:
                    t.wait_ge(cp.sems["evd"], evd0 + m - 1)
                for k in range(16):
                    cp.pe_mm(psD[:, (m % 2) * 512:(m % 2) * 512 + 512],
                             wbuf[:, 16 * 128 + k * 128:
                                  16 * 128 + (k + 1) * 128],
                             h2[:, k * 512:(k + 1) * 512],
                             start=(k == 0), stop=(k == 15))
                cp.pe_drain()

        @blk.scalar
        def _(sc):
            for m in range(nm):
                sc.wait_ge(cp.sems["pe"], pe0 + m * 34 + 17)
                if m >= 2:
                    sc.wait_ge(cp.sems["evd"], evd0 + m - 1)
                cp.inc(sc.activation(sgt[:, (m % 2) * 512:(m % 2) * 512 + 512],
                                     psA[:, (m % 2) * 512:(m % 2) * 512 + 512],
                                     AF.Silu), "eva", 1)

        @blk.vector
        def _(v):
            for m in range(nm):
                v.wait_ge(cp.sems["pe"], pe_pair(m))
                v.wait_ge(cp.sems["eva"], eva0 + m + 1)
                cp.inc(v.tensor_mul(hh[:, m * 512:(m + 1) * 512],
                                    sgt[:, (m % 2) * 512:(m % 2) * 512 + 512],
                                    psD[:, (m % 2) * 512:(m % 2) * 512 + 512]),
                       "evd", 1)


def _ffn_gateup(cp, nc, wbuf, psA, psD, sgt, u1, hh2, fp4_all, nm):
    RANKSZ = (11 * 16 + 2 * 44) * 128

    def gu_tile(j):
        rank, loc = j // 11, j % 11
        return rank * RANKSZ + loc * 16 * 128

    with nc.Block() as blk:
        pe0 = cp.pe_issued
        ldw0 = cp.cnt["ldw"]
        eva0 = cp.cnt["eva"]
        evd0 = cp.cnt["evd"]

        @blk.sync
        def _(s):
            for m in range(nm):
                for half in range(2):
                    j = m if half == 0 else 44 + m
                    if m >= 1:
                        s.wait_ge(cp.sems["pe"],
                                  pe0 + (m - 1) * 34 + (half + 1) * 17)
                    cp.inc(s.dma_start(
                        out=wbuf[:, half * 16 * 128:(half + 1) * 16 * 128
                                 ].rearrange("p (k c) -> p k c", k=16),
                        in_=fp4_all[gu_tile(j):gu_tile(j) + 16 * 128,
                                    :].rearrange("(k p) c -> p k c", p=128)),
                        "lda", 16)

        lda0 = cp.cnt["lda"] - 2 * nm * 16

        @blk.tensor
        def _(t):
            for m in range(nm):
                t.wait_ge(cp.sems["lda"], lda0 + (2 * m + 1) * 16)
                if m >= 2:
                    t.wait_ge(cp.sems["eva"], eva0 + m - 1)
                for k in range(16):
                    cp.pe_mm(psA[:, (m % 2) * 512:(m % 2) * 512 + 512],
                             wbuf[:, k * 128:(k + 1) * 128],
                             u1[:, k * 512:(k + 1) * 512],
                             start=(k == 0), stop=(k == 15))
                cp.pe_drain()
                t.wait_ge(cp.sems["lda"], lda0 + (2 * m + 2) * 16)
                if m >= 2:
                    t.wait_ge(cp.sems["evd"], evd0 + m - 1)
                for k in range(16):
                    cp.pe_mm(psD[:, (m % 2) * 512:(m % 2) * 512 + 512],
                             wbuf[:, 16 * 128 + k * 128:
                                  16 * 128 + (k + 1) * 128],
                             u1[:, k * 512:(k + 1) * 512],
                             start=(k == 0), stop=(k == 15))
                cp.pe_drain()

        @blk.scalar
        def _(sc):
            for m in range(nm):
                sc.wait_ge(cp.sems["pe"], pe0 + m * 34 + 17)
                if m >= 2:
                    sc.wait_ge(cp.sems["evd"], evd0 + m - 1)
                cp.inc(sc.activation(sgt[:, (m % 2) * 512:(m % 2) * 512 + 512],
                                     psA[:, (m % 2) * 512:(m % 2) * 512 + 512],
                                     AF.Silu), "eva", 1)

        @blk.vector
        def _(v):
            for m in range(nm):
                v.wait_ge(cp.sems["pe"], pe0 + (m + 1) * 34)
                v.wait_ge(cp.sems["eva"], eva0 + m + 1)
                cp.inc(v.tensor_mul(hh2[:, m * 512:(m + 1) * 512],
                                    sgt[:, (m % 2) * 512:(m % 2) * 512 + 512],
                                    psD[:, (m % 2) * 512:(m % 2) * 512 + 512]),
                       "evd", 1)


def _ffn_down(cp, nc, wbuf, psA, hh2, fp4_all, yout, kw=11):
    RANKSZ = (11 * 16 + 2 * 44) * 128
    GUSZ = 11 * 16 * 128
    nchunk = 44 // kw

    def dn_tile(m):
        return (m // 2) * RANKSZ + GUSZ + (m % 2) * 44 * 128

    with nc.Block() as blk:
        pe0 = cp.pe_issued
        lda0 = cp.cnt["lda"]
        eva0 = cp.cnt["eva"]

        def pe_chunk(m, h):
            return pe0 + m * 45 + (h + 1) * kw

        @blk.sync
        def _(s):
            for m in range(16):
                for h in range(nchunk):
                    gidx = m * nchunk + h
                    if gidx >= 2:
                        pm, ph = (gidx - 2) // nchunk, (gidx - 2) % nchunk
                        s.wait_ge(cp.sems["pe"], pe_chunk(pm, ph))
                    off = (gidx % 2) * kw * 128
                    r0 = dn_tile(m) + h * kw * 128
                    cp.inc(s.dma_start(
                        out=wbuf[:, off:off + kw * 128].rearrange(
                            "p (k c) -> p k c", k=kw),
                        in_=fp4_all[r0:r0 + kw * 128, :].rearrange(
                            "(k p) c -> p k c", p=128)), "lda", 16)

        @blk.tensor
        def _(t):
            for m in range(16):
                if m >= 2:
                    t.wait_ge(cp.sems["eva"], eva0 + m - 1)
                for h in range(nchunk):
                    gidx = m * nchunk + h
                    t.wait_ge(cp.sems["lda"], lda0 + (gidx + 1) * 16)
                    off = (gidx % 2) * kw * 128
                    for k in range(kw):
                        cp.pe_mm(psA[:, (m % 2) * 512:(m % 2) * 512 + 512],
                                 wbuf[:, off + k * 128:off + (k + 1) * 128],
                                 hh2[:, (h * kw + k) * 512:
                                     (h * kw + k + 1) * 512],
                                 start=(h == 0 and k == 0),
                                 stop=(h == nchunk - 1 and k == kw - 1))
                cp.pe_drain()

        @blk.scalar
        def _(sc):
            for m in range(16):
                sc.wait_ge(cp.sems["pe"], pe0 + (m + 1) * 45)
                cp.inc(sc.activation(yout[:, m * 512:(m + 1) * 512],
                                     psA[:, (m % 2) * 512:(m % 2) * 512 + 512],
                                     AF.Copy), "eva", 1)


TOKEN_IN_SHAPES = {
    "xT": ([2048, 1024], F32),
    "w_outproj": ([16, 32, 128, 128], F32),
    "w_gate": ([64, 16, 128, 128], F32),
    "w_up": ([64, 16, 128, 128], F32),
    "w_down": ([16, 64, 128, 128], F32),
    "gu_packed": ([1408, 256], I32),
    "gu_scales": ([128, 11 * 16], F16),
    "dn_packed": ([256, 704], I32),
    "dn_scales": ([128, 2 * 44], F16),
}
RANKSZ_FP4 = (11 * 16 + 2 * 44) * 128


def build_full():
    import contextlib
    nc = bass.Bass("TRN2", target_bir_lowering=False, debug=False,
                   detect_race_conditions=False)
    cp = CoreProgram(nc)
    cp.sem_init()
    t_in = _declare_mixer_inputs(nc)
    for name, (shp, dt) in TOKEN_IN_SHAPES.items():
        t_in[name] = nc.dram_tensor(name, shp, dt, kind="ExternalInput").ap()
    out_d = nc.dram_tensor("out", [2048, 1024], F32, kind="ExternalOutput").ap()

    bzs = _mixer_bounces(nc)
    v_mine = nc.dram_tensor("v_mine", [2048, 1024], BF16)
    v_all = nc.dram_tensor("v_all", [4 * 2048, 1024], BF16)
    yg_mine = nc.dram_tensor("yg_mine", [1024, SEQ], BF16)
    yg_all = nc.dram_tensor("yg_all", [4 * 1024, SEQ], BF16)
    fp4_mine = nc.dram_tensor("fp4_mine", [RANKSZ_FP4, 128], BF16)
    fp4_all = nc.dram_tensor("fp4_all", [8 * RANKSZ_FP4, 128], BF16)

    build_consts(cp)
    import contextlib as _ctxlib
    mix_es = _ctxlib.ExitStack()
    _alloc_repl(cp, mix_es)
    cp.build_p0(t_in)
    build_dequant(cp, t_in, fp4_mine)
    with nc.Block() as blk:
        @blk.gpsimd
        def _(g):
            g.wait_ge(cp.sems["stg"], cp.cnt["stg"])
            cp.inc(g.collective_compute(
                "AllGather", ALU.bypass,
                replica_groups=[list(range(8))],
                ins=[fp4_mine.ap().opt()],
                outs=[fp4_all.ap().opt()]), "cc", 1)
    build_p1(cp, t_in["xT"], v_mine)
    with nc.Block() as blk:
        @blk.gpsimd
        def _(g):
            g.wait_ge(cp.sems["stg"], cp.cnt["stg"])
            cp.inc(g.collective_compute(
                "AllGather", ALU.bypass,
                replica_groups=[[0, 1, 2, 3], [4, 5, 6, 7]],
                ins=[v_mine.ap().opt()],
                outs=[v_all.ap().opt()]), "cc", 1)
    cp.build_m1(t_in, v_all.ap(), bzs, wait_cc=2)
    cp.build_m2(bzs, yg_mine.ap())
    mix_es.close()
    with nc.Block() as blk:
        @blk.gpsimd
        def _(g):
            g.wait_ge(cp.sems["stg"], cp.cnt["stg"])
            cp.inc(g.collective_compute(
                "AllGather", ALU.bypass,
                replica_groups=[[0, 1, 2, 3], [4, 5, 6, 7]],
                ins=[yg_mine.ap().opt()],
                outs=[yg_all.ap().opt()]), "cc", 1)
    with nc.Block() as blk:
        @blk.sync
        def _(s):
            s.wait_ge(cp.sems["cc"], 3)
    build_token(cp, t_in, yg_all.ap(), t_in["xT"], fp4_all.ap(), out_d)
    with nc.Block() as blk:
        @blk.sync
        def _(s):
            s.wait_ge(cp.sems["sto"], cp.cnt["sto"])
    return nc, cp


def _host_prep_token(inputs, c):
    d = {}
    x = np.asarray(inputs["L0_attn_hidden_states"])
    b, q = c // 4, c % 4
    d["xT"] = np.ascontiguousarray(x[b, TOK * q:TOK * (q + 1), :].T)

    dperm_rank = np.concatenate([_dlocal(gp) for gp in range(NGP)])
    yg_dglobal = np.concatenate([1024 * r + dperm_rank for r in range(4)])
    w_out = np.asarray(inputs["L0_attn_out_proj_w"])
    w_out_lhsT = np.ascontiguousarray(w_out.T[yg_dglobal, :])
    d["w_outproj"] = np.ascontiguousarray(
        w_out_lhsT.reshape(32, 128, 16, 128).transpose(2, 0, 1, 3))

    def mk_lhsT(w):
        out_f, in_f = w.shape
        lt = np.ascontiguousarray(w.T)
        return np.ascontiguousarray(
            lt.reshape(in_f // 128, 128, out_f // 128, 128).transpose(2, 0, 1, 3))

    d["w_gate"] = mk_lhsT(np.asarray(inputs["L0_attn_gate_w"]))
    d["w_up"] = mk_lhsT(np.asarray(inputs["L0_attn_up_w"]))
    d["w_down"] = mk_lhsT(np.asarray(inputs["L0_attn_down_w"]))

    gu_p = np.asarray(inputs["L0_ffn_gate_up_packed"])
    gu_s = np.asarray(inputs["L0_ffn_gate_up_scales"])
    dn_p = np.asarray(inputs["L0_ffn_down_packed"])
    dn_s = np.asarray(inputs["L0_ffn_down_scales"])
    d["gu_packed"] = np.ascontiguousarray(gu_p[1408 * c:1408 * (c + 1), :])
    s = gu_s[:, 1408 * c:1408 * (c + 1)]
    d["gu_scales"] = np.ascontiguousarray(
        s.T.reshape(11, 128, 16).transpose(1, 0, 2).reshape(128, 11 * 16))
    d["dn_packed"] = np.ascontiguousarray(dn_p[256 * c:256 * (c + 1), :])
    s2 = dn_s[:, 256 * c:256 * (c + 1)]
    d["dn_scales"] = np.ascontiguousarray(
        s2.T.reshape(2, 128, 44).transpose(1, 0, 2).reshape(128, 2 * 44))
    return d


def kernel(**inputs):
    per_core = []
    for c in range(NCORE):
        d = _host_prep_mixer(inputs, c)
        d.update(_host_prep_token(inputs, c))
        per_core.append(d)
    nc, cp = build_full()
    res = run_bass_kernel_spmd(nc, per_core, core_ids=list(range(NCORE)))
    x = np.asarray(inputs["L0_attn_hidden_states"])
    out = np.empty_like(x)
    for c in range(NCORE):
        b, q = c // 4, c % 4
        out[b, TOK * q:TOK * (q + 1), :] = res.results[c]["out"].T
    return out


# version 03
# speedup vs baseline: 1.0539x; 1.0539x over previous
import numpy as np
import concourse.bass as bass
import concourse.mybir as mybir
from concourse.bass_utils import run_bass_kernel_spmd

F32 = mybir.dt.float32
BF16 = mybir.dt.bfloat16
F16 = mybir.dt.float16
I32 = mybir.dt.int32
AF = mybir.ActivationFunctionType
ALU = mybir.AluOpType
AX = mybir.AxisListType

D_MODEL = 2048
D_INNER = 4096
D_STATE = 16
DT_RANK = 128
INTER = 8192
FFN_I = 5632
BATCH, SEQ = 2, 4096
EPS_LN = 1e-5
EPS_RMS = 1e-5
EPS_FFN = 1e-6

NCORE = 8
TOK = 1024
NGP = 8
SUB = 512
NSUB = SEQ // SUB
M_INPROJ = 21
PE_WINDOW = 3


def _dlocal(gp):
    p = np.arange(128)
    return 16 * (p % 64) + (2 * gp + p // 64)


def _repl_matrices():
    RB = np.zeros((16, 128, 128), np.float32)
    for n in range(16):
        for p in range(128):
            RB[n, (n % 8) * 16 + (p % 64) // 4, p] = 1.0
    RC = np.zeros((2, 128, 128), np.float32)
    for par in range(2):
        for p in range(128):
            RC[par, 64 * par + p % 64, p] = 1.0
    RX = np.zeros((8, 2, 128, 128), np.float32)
    for gp in range(8):
        for p in range(128):
            g = 2 * gp + p // 64
            RX[gp, g // 8, (g % 8) * 16 + (p % 64) // 4, p] = 1.0
    return RB, RC, RX


class CoreProgram:
    def __init__(self, nc):
        self.nc = nc
        self._sem_names = ["pe", "eva", "evd", "evg", "ldw", "lda", "sto",
                          "cc", "ldc", "stg"]
        self.sems = {}
        self.cnt = {}
        self.pe_issued = 0
        self.pe_waited = 0
        self._keep = []

    def open(self, ctx):
        v = ctx.__enter__()
        self._keep.append(ctx)
        return v

    def sem_init(self):
        for n in self._sem_names:
            self.sems[n] = self.open(self.nc.semaphore(n))
            self.cnt[n] = 0

    def inc(self, inst, sem, n=1):
        inst.then_inc(self.sems[sem], n)
        self.cnt[sem] += n
        return self.cnt[sem]

    def wait(self, eng, sem, val):
        eng.wait_ge(self.sems[sem], val)

    def pe_mm(self, out, lhsT, rhs, start, stop):
        t = self.nc.tensor
        if self.pe_issued - self.pe_waited >= PE_WINDOW:
            need = self.pe_issued - PE_WINDOW + 1
            t.wait_ge(self.sems["pe"], need)
            self.pe_waited = need
        t.matmul(out, lhsT, rhs, start=start, stop=stop).then_inc(self.sems["pe"], 1)
        self.pe_issued += 1
        self.cnt["pe"] += 1
        return self.pe_issued

    def pe_transpose(self, out, in_):
        t = self.nc.tensor
        if self.pe_issued - self.pe_waited >= PE_WINDOW:
            need = self.pe_issued - PE_WINDOW + 1
            t.wait_ge(self.sems["pe"], need)
            self.pe_waited = need
        t.transpose(out, in_, self.ident_sb[:, :]).then_inc(self.sems["pe"], 1)
        self.pe_issued += 1
        self.cnt["pe"] += 1
        return self.pe_issued

    def pe_drain(self):
        t = self.nc.tensor
        if self.pe_issued - self.pe_waited >= PE_WINDOW:
            need = self.pe_issued - PE_WINDOW + 1
            t.wait_ge(self.sems["pe"], need)
            self.pe_waited = need
        t.drain().then_inc(self.sems["pe"], 1)
        self.pe_issued += 1
        self.cnt["pe"] += 1
        return self.pe_issued

    def build_p0(self, t_in):
        nc = self.nc
        op = self.open
        self.dtb_sb = op(nc.sbuf_tensor("dtb_sb", [128, NGP], F32))
        self.a_sb = op(nc.sbuf_tensor("a_sb", [128, NGP * 16], F32))
        self.convw_sb = op(nc.sbuf_tensor("convw_sb", [128, NGP * 4], F32))
        self.convb_sb = op(nc.sbuf_tensor("convb_sb", [128, NGP], F32))
        self.dpar_sb = op(nc.sbuf_tensor("dpar_sb", [128, NGP], F32))
        self.ln1_sb = op(nc.sbuf_tensor("ln1_sb", [128, 16], F32))
        self.ln2_sb = op(nc.sbuf_tensor("ln2_sb", [128, 16], F32))

        RB, RC, RX = _repl_matrices()
        rb_d = nc.inline_tensor(
            np.ascontiguousarray(RB.transpose(1, 0, 2).reshape(128, 16 * 128)),
            name="rb_c").ap()
        rc_d = nc.inline_tensor(
            np.ascontiguousarray(RC.transpose(1, 0, 2).reshape(128, 2 * 128)),
            name="rc_c").ap()
        rx_d = nc.inline_tensor(
            np.ascontiguousarray(RX.transpose(2, 0, 1, 3).reshape(128, 16 * 128)),
            name="rx_c").ap()

        import os as _osp
        if _osp.environ.get("P0SKIP") == "1":
            return
        with nc.Block() as blk:
            @blk.gpsimd
            def _(g):
                self.inc(g.dma_start(out=self.rb_sb[:, :], in_=rb_d[:, :]), "ldw", 16)
                self.inc(g.dma_start(out=self.rc_sb[:, :], in_=rc_d[:, :]), "ldw", 16)
                self.inc(g.dma_start(out=self.rx_sb[:, :], in_=rx_d[:, :]), "ldw", 16)

            @blk.sync
            def _(s):
                for name, sb in (("dt_bias", self.dtb_sb), ("a_log_p", self.a_sb),
                                 ("conv_w_p", self.convw_sb),
                                 ("conv_b_p", self.convb_sb),
                                 ("d_par_p", self.dpar_sb), ("ln1_w", self.ln1_sb),
                                 ("ln2_w", self.ln2_sb)):
                    self.inc(s.dma_start(out=sb[:, :], in_=t_in[name]), "ldc", 16)

            @blk.scalar
            def _(sc):
                self.wait(sc, "ldc", self.cnt["ldc"])
                self.inc(sc.activation(self.a_sb[:, :], self.a_sb[:, :], AF.Exp),
                         "eva", 1)

            @blk.vector
            def _(v):
                self.wait(v, "eva", self.cnt["eva"])
                self.inc(v.tensor_scalar_mul(self.a_sb[:, :], self.a_sb[:, :], -1.0),
                         "evd", 1)

    def build_m1(self, t_in, v_src, bzs, wait_cc=None):
        nc = self.nc
        NTC = 8
        PERTC_PE = M_INPROJ * 17 + NGP * 2

        import contextlib
        with contextlib.ExitStack() as es:
            E = es.enter_context
            w_in_sb = E(nc.sbuf_tensor("w_in_sb", [128, M_INPROJ * 2048], BF16))
            w_dt_sb = E(nc.sbuf_tensor("w_dt_sb", [128, NGP * 128], BF16))
            vbuf = E(nc.sbuf_tensor("vbuf", [128, 2 * 16 * 512], BF16))
            zst = E(nc.sbuf_tensor("zst", [128, 8 * 512], BF16))
            xbst = E(nc.sbuf_tensor("xbst", [128, 2 * 512], BF16))
            bst = E(nc.sbuf_tensor("bst", [128, 2 * 512], BF16))
            cst = E(nc.sbuf_tensor("cst", [128, 8 * 512], BF16))
            dtst = E(nc.sbuf_tensor("dtst", [128, 8 * 512], F32))
            dtraw = E(nc.sbuf_tensor("dtraw", [128, 2 * 512], BF16))
            sp1 = E(nc.sbuf_tensor("sp1", [128, 2 * 512], F32))
            sp2 = E(nc.sbuf_tensor("sp2", [128, 2 * 512], F32))
            ps_z = E(nc.psum_tensor("ps_z", [128, 2 * 512], F32))
            ps_o = E(nc.psum_tensor("ps_o", [128, 2 * 512], F32))
            ps_dt = E(nc.psum_tensor("ps_dt", [128, 2 * 512], F32))
            blk = E(nc.Block())
            pe0 = self.pe_issued
            eva0 = self.cnt["eva"]
            evd0 = self.cnt["evd"]
            lda0 = self.cnt["lda"]
            stg0 = self.cnt["stg"]

            def pe_tile(tc, m):
                return pe0 + tc * PERTC_PE + (m + 1) * 17

            def pe_dt(tc, gp):
                return pe0 + tc * PERTC_PE + M_INPROJ * 17 + (gp + 1) * 2

            def eva_silu(tc, zi):
                return eva0 + tc * 16 + zi + 1

            def eva_u3(tc, gp):
                return eva0 + tc * 16 + 8 + gp + 1

            def evd_copy(tc, m):
                return evd0 + tc * 21 + (m - 8) + 1

            def evd_dtfin(tc, gp):
                return evd0 + tc * 21 + 13 + gp + 1

            def stg_tc(tc):
                return stg0 + (tc + 1) * 5 * 16

            def lda_tc(tc):
                return lda0 + (tc + 1) * 16 * 16

            ldw_need = self.cnt["ldw"] + 22 * 16

            def o_idx(tc, m):
                return tc * 13 + (m - 8)

            def o_from_idx(i):
                return i // 13, i % 13 + 8

            @blk.tensor
            def _(t):
                t.wait_ge(self.sems["ldw"], ldw_need)
                for tc in range(NTC):
                    t.wait_ge(self.sems["lda"], lda_tc(tc))
                    voff = (tc % 2) * 16 * 512
                    for m in range(M_INPROJ):
                        if m < 8:
                            if tc * 8 + m >= 2:
                                pi = tc * 8 + m - 2
                                t.wait_ge(self.sems["eva"],
                                          eva_silu(pi // 8, pi % 8))
                            pb = ps_z[:, (m % 2) * 512:(m % 2) * 512 + 512]
                        else:
                            oi = o_idx(tc, m)
                            if oi >= 2:
                                ptc, pm = o_from_idx(oi - 2)
                                t.wait_ge(self.sems["evd"], evd_copy(ptc, pm))
                            pb = ps_o[:, (oi % 2) * 512:(oi % 2) * 512 + 512]
                        for k in range(16):
                            self.pe_mm(
                                pb,
                                w_in_sb[:, m * 2048 + k * 128:
                                        m * 2048 + (k + 1) * 128],
                                vbuf[:, voff + k * 512:voff + (k + 1) * 512],
                                start=(k == 0), stop=(k == 15))
                        self.pe_drain()
                    t.wait_ge(self.sems["evd"], evd_copy(tc, 20))
                    for gp in range(NGP):
                        i = tc * NGP + gp
                        if i >= 2:
                            ptc, pgp = (i - 2) // NGP, (i - 2) % NGP
                            t.wait_ge(self.sems["evd"], evd_dtfin(ptc, pgp))
                        self.pe_mm(ps_dt[:, (gp % 2) * 512:(gp % 2) * 512 + 512],
                                   w_dt_sb[:, gp * 128:(gp + 1) * 128],
                                   dtraw[:, (tc % 2) * 512:(tc % 2) * 512 + 512],
                                   start=True, stop=True)
                        self.pe_drain()

            @blk.sync
            def _(s):
                if wait_cc is not None:
                    self.wait(s, "cc", wait_cc)
                for tc in range(NTC):
                    if tc >= 1:
                        s.wait_ge(self.sems["lda"], lda_tc(tc - 1))
                    if tc >= 2:
                        s.wait_ge(self.sems["pe"], pe_tile(tc - 2, M_INPROJ - 1))
                    r, cb = tc // 2, (tc % 2) * 512
                    for k in range(16):
                        self.inc(s.dma_start(
                            out=vbuf[:, (tc % 2) * 16 * 512 + k * 512:
                                     (tc % 2) * 16 * 512 + (k + 1) * 512],
                            in_=v_src[2048 * r + 128 * k:2048 * r + 128 * (k + 1),
                                      cb:cb + 512]), "lda", 16)

            @blk.scalar
            def _(sc):
                for tc in range(NTC):
                    if tc >= 1:
                        self.wait(sc, "stg", stg_tc(tc - 1))
                    for zi in range(8):
                        sc.wait_ge(self.sems["pe"], pe_tile(tc, zi))
                        self.inc(sc.activation(
                            zst[:, zi * 512:(zi + 1) * 512],
                            ps_z[:, (zi % 2) * 512:(zi % 2) * 512 + 512],
                            AF.Silu), "eva", 1)
                    for gp in range(NGP):
                        i = tc * NGP + gp
                        if i >= 2:
                            ptc, pgp = (i - 2) // NGP, (i - 2) % NGP
                            sc.wait_ge(self.sems["evd"], evd_dtfin(ptc, pgp))
                        sc.wait_ge(self.sems["pe"], pe_dt(tc, gp))
                        h = (gp % 2) * 512
                        sc.activation(sp2[:, h:h + 512], ps_dt[:, h:h + 512],
                                      AF.Relu, bias=self.dtb_sb[:, gp:gp + 1])
                        sc.activation(sp1[:, h:h + 512], ps_dt[:, h:h + 512],
                                      AF.Abs, bias=self.dtb_sb[:, gp:gp + 1])
                        sc.activation(sp1[:, h:h + 512], sp1[:, h:h + 512],
                                      AF.Exp, scale=-1.0)
                        self.inc(sc.activation(sp1[:, h:h + 512], sp1[:, h:h + 512],
                                               AF.Ln, bias=1.0), "eva", 1)

            @blk.vector
            def _(v):
                for tc in range(NTC):
                    if tc >= 1:
                        self.wait(v, "stg", stg_tc(tc - 1))
                    for m in range(8, M_INPROJ):
                        v.wait_ge(self.sems["pe"], pe_tile(tc, m))
                        if m == 20 and tc >= 2:
                            v.wait_ge(self.sems["pe"], pe_dt(tc - 2, NGP - 1))
                        oi = o_idx(tc, m)
                        srcp = ps_o[:, (oi % 2) * 512:(oi % 2) * 512 + 512]
                        if m < 10:
                            dst = xbst[:, (m - 8) * 512:(m - 7) * 512]
                        elif m < 12:
                            dst = bst[:, (m - 10) * 512:(m - 9) * 512]
                        elif m < 20:
                            dst = cst[:, (m - 12) * 512:(m - 11) * 512]
                        else:
                            dst = dtraw[:, (tc % 2) * 512:(tc % 2) * 512 + 512]
                        self.inc(v.tensor_copy(dst, srcp), "evd", 1)
                    for gp in range(NGP):
                        h = (gp % 2) * 512
                        v.wait_ge(self.sems["eva"], eva_u3(tc, gp))
                        self.inc(v.tensor_add(dtst[:, gp * 512:(gp + 1) * 512],
                                              sp2[:, h:h + 512],
                                              sp1[:, h:h + 512]), "evd", 1)

            @blk.gpsimd
            def _(g):
                for m in range(M_INPROJ):
                    self.inc(g.dma_start(
                        out=w_in_sb[:, m * 2048:(m + 1) * 2048].rearrange(
                            "p (k c) -> p k c", k=16),
                        in_=t_in["w_inproj"][m]), "ldw", 16)
                self.inc(g.dma_start(
                    out=w_dt_sb[:, :].rearrange("p (g c) -> p g c", g=NGP),
                    in_=t_in["w_dtproj"].rearrange("g p c -> p g c")), "ldw", 16)
                for tc in range(NTC):
                    g.wait_ge(self.sems["eva"], eva0 + (tc + 1) * 16)
                    g.wait_ge(self.sems["evd"], evd0 + (tc + 1) * 21)
                    c0 = 512 * tc
                    for dram, sb, nblk in ((bzs["sz_b"], zst, 8),
                                           (bzs["xr_b"], xbst, 2),
                                           (bzs["br_b"], bst, 2),
                                           (bzs["cr_b"], cst, 8),
                                           (bzs["dt_b"], dtst, 8)):
                        self.inc(g.dma_start(
                            out=dram.ap()[:, c0:c0 + 512].rearrange(
                                "(a p) t -> p a t", p=128),
                            in_=sb[:, :].rearrange("p (a t) -> p a t", a=nblk)),
                            "stg", 16)

    def build_m2(self, bzs, yg_dst, state_dbg=None):
        nc = self.nc
        stg_m1_done = self.cnt["stg"]
        import contextlib
        with contextlib.ExitStack() as es:
            E = es.enter_context
            dtb2 = E(nc.sbuf_tensor("dtb2", [128, 2 * NGP * SUB], F32))
            szb2 = E(nc.sbuf_tensor("szb2", [128, NGP * SUB], BF16))
            crb2 = E(nc.sbuf_tensor("crb2", [128, NGP * SUB], BF16))
            brb2 = E(nc.sbuf_tensor("brb2", [128, 2 * SUB], BF16))
            xrb2 = E(nc.sbuf_tensor("xrb2", [128, 2 * SUB], BF16))
            brep = E(nc.sbuf_tensor("brep", [128, 16 * SUB], BF16))
            crep = E(nc.sbuf_tensor("crep", [128, 16 * SUB], BF16))
            xexp = E(nc.sbuf_tensor("xexp", [128, 2 * (SUB + 3)], F32))
            xtail = E(nc.sbuf_tensor("xtail", [128, NGP * 3], F32))
            xcb = E(nc.sbuf_tensor("xcb", [128, 2 * SUB], F32))
            dtxb = E(nc.sbuf_tensor("dtxb", [128, 2 * SUB], F32))
            dab = E(nc.sbuf_tensor("dab", [128, 4 * SUB], F32))
            ub = E(nc.sbuf_tensor("ub", [128, 4 * SUB], F32))
            hb = E(nc.sbuf_tensor("hb", [128, 4 * SUB], BF16))
            tmpb = E(nc.sbuf_tensor("tmpb", [128, 16 * SUB], BF16))
            yb = E(nc.sbuf_tensor("yb", [128, 2 * SUB], F32))
            ygst = E(nc.sbuf_tensor("ygst", [128, 2 * NGP * SUB], BF16))
            stateb = E(nc.sbuf_tensor("stateb", [128, 128], F32))
            ps_rb = E(nc.psum_tensor("ps_rb", [128, 2 * 512], F32))
            ps_rc = E(nc.psum_tensor("ps_rc", [128, 2 * 512], F32))
            ps_x = E(nc.psum_tensor("ps_x", [128, 2 * 512], F32))
            blk = E(nc.Block())
            pe0 = self.pe_issued
            eva0 = self.cnt["eva"]
            evd0 = self.cnt["evd"]
            evg0 = self.cnt["evg"]
            lda0 = self.cnt["lda"]
            stg0 = self.cnt["stg"]

            def pe_brep(s, n):
                return pe0 + s * 88 + 2 * (n + 1)

            def pe_crep(s, n):
                return pe0 + s * 88 + 32 + 2 * (n + 1)

            def pe_xexp(s, gp):
                return pe0 + s * 88 + 64 + 3 * (gp + 1)

            EVD_PER_GP = 1 + 1 + 1 + 16 + 1
            EVD_PER_S = 16 + NGP * EVD_PER_GP

            def evd_brepev(s, n):
                return evd0 + s * EVD_PER_S + n + 1

            def evd_gp_base(s, gp):
                return evd0 + s * EVD_PER_S + 16 + gp * EVD_PER_GP

            def evd_xexpev(s, gp):
                return evd_gp_base(s, gp) + 1

            def evd_conv(s, gp):
                return evd_gp_base(s, gp) + 2

            def evd_dtx(s, gp):
                return evd_gp_base(s, gp) + 3

            def evd_scan(s, gp, n):
                return evd_gp_base(s, gp) + 4 + n

            def evd_yg(s, gp):
                return evd_gp_base(s, gp) + 4 + 16

            EVA_PER_GP = 1 + 16
            EVA_PER_S = 16 + NGP * EVA_PER_GP

            def eva_crepev(s, n):
                return eva0 + s * EVA_PER_S + n + 1

            def eva_silu(s, gp):
                return eva0 + s * EVA_PER_S + 16 + gp * EVA_PER_GP + 1

            def eva_da(s, gp, n):
                return eva0 + s * EVA_PER_S + 16 + gp * EVA_PER_GP + 1 + n + 1

            EVG_PER_GP = 16

            def evg_u(s, gp, n):
                return evg0 + (s * NGP + gp) * EVG_PER_GP + n + 1

            def lda_s(s):
                return lda0 + (s + 1) * 5 * 16

            def stg_s(s):
                return stg0 + (s + 1) * 16

            def lin_gp(s, gp):
                return s * NGP + gp

            def lin_n(s, gp, n):
                return (s * NGP + gp) * 16 + n

            def from_lin_n(ln):
                return ln // (NGP * 16), (ln // 16) % NGP, ln % 16

            @blk.sync
            def _(s_eng):
                s_eng.wait_ge(self.sems["stg"], stg_m1_done)
                for s in range(NSUB):
                    par = s % 2
                    c0 = s * SUB
                    if s >= 1:
                        s_eng.wait_ge(self.sems["lda"], lda_s(s - 1))
                    if s >= 2:
                        s_eng.wait_ge(self.sems["evd"], evd_yg(s - 2, NGP - 1))
                    self.inc(s_eng.dma_start(
                        out=dtb2[:, par * NGP * SUB:(par + 1) * NGP * SUB].rearrange(
                            "p (a t) -> p a t", a=8),
                        in_=bzs["dt_b"].ap()[:, c0:c0 + SUB].rearrange(
                            "(a p) t -> p a t", p=128)), "lda", 16)
                    if s >= 1:
                        s_eng.wait_ge(self.sems["evd"], evd_yg(s - 1, NGP - 1))
                    for dram, buf, nblk in ((bzs["sz_b"], szb2, 8),
                                            (bzs["cr_b"], crb2, 8),
                                            (bzs["br_b"], brb2, 2),
                                            (bzs["xr_b"], xrb2, 2)):
                        self.inc(s_eng.dma_start(
                            out=buf[:, 0:nblk * SUB].rearrange(
                                "p (a t) -> p a t", a=nblk),
                            in_=dram.ap()[:, c0:c0 + SUB].rearrange(
                                "(a p) t -> p a t", p=128)), "lda", 16)

            @blk.tensor
            def _(t):
                for s in range(NSUB):
                    t.wait_ge(self.sems["lda"], lda_s(s))
                    par = s % 2
                    for half in range(2):
                        for n in range(16):
                            i = s * 16 + n
                            if i >= 2:
                                ps2, n2 = (i - 2) // 16, (i - 2) % 16
                                if half == 0:
                                    t.wait_ge(self.sems["evd"],
                                              evd_brepev(ps2, n2))
                                else:
                                    t.wait_ge(self.sems["eva"],
                                              eva_crepev(ps2, n2))
                            if half == 0:
                                self.pe_mm(
                                    ps_rb[:, (n % 2) * 512:(n % 2) * 512 + 512],
                                    self.rb_sb[:, n * 128:(n + 1) * 128],
                                    brb2[:, (n // 8) * SUB:
                                         (n // 8 + 1) * SUB],
                                    start=True, stop=True)
                            else:
                                self.pe_mm(
                                    ps_rc[:, (n % 2) * 512:(n % 2) * 512 + 512],
                                    self.rc_sb[:, (n % 2) * 128:
                                               (n % 2) * 128 + 128],
                                    crb2[:, (n // 2) * SUB:
                                         (n // 2 + 1) * SUB],
                                    start=True, stop=True)
                            self.pe_drain()
                    for gp in range(NGP):
                        i = lin_gp(s, gp)
                        if i >= 2:
                            ps2, pgp = (i - 2) // NGP, (i - 2) % NGP
                            t.wait_ge(self.sems["evd"], evd_xexpev(ps2, pgp))
                        for j in range(2):
                            self.pe_mm(ps_x[:, (gp % 2) * 512:(gp % 2) * 512 + 512],
                                       self.rx_sb[:, gp * 256 + j * 128:
                                                  gp * 256 + (j + 1) * 128],
                                       xrb2[:, j * SUB:(j + 1) * SUB],
                                       start=(j == 0), stop=(j == 1))
                        self.pe_drain()

            @blk.scalar
            def _(sc):
                for s in range(NSUB):
                    par = s % 2
                    for n in range(16):
                        sc.wait_ge(self.sems["pe"], pe_crep(s, n))
                        if s >= 2:
                            sc.wait_ge(self.sems["evd"],
                                       evd_scan(s - 2, NGP - 1, n))
                        self.inc(sc.activation(
                            crep[:, n * SUB:(n + 1) * SUB],
                            ps_rc[:, (n % 2) * 512:(n % 2) * 512 + 512],
                            AF.Copy), "eva", 1)
                    for gp in range(NGP):
                        sc.wait_ge(self.sems["evd"], evd_conv(s, gp))
                        self.inc(sc.activation(
                            xcb[:, (gp % 2) * SUB:(gp % 2) * SUB + SUB],
                            xcb[:, (gp % 2) * SUB:(gp % 2) * SUB + SUB],
                            AF.Silu, bias=self.convb_sb[:, gp:gp + 1]), "eva", 1)
                        for n in range(16):
                            pn = lin_n(s, gp, n) - 4
                            if pn >= 0:
                                sc.wait_ge(self.sems["evd"],
                                           evd_scan(*from_lin_n(pn)))
                            self.inc(sc.activation(
                                dab[:, (n % 4) * SUB:(n % 4 + 1) * SUB],
                                dtb2[:, par * NGP * SUB + gp * SUB:
                                     par * NGP * SUB + (gp + 1) * SUB],
                                AF.Exp,
                                scale=self.a_sb[:, gp * 16 + n:gp * 16 + n + 1]),
                                "eva", 1)

            @blk.gpsimd
            def _(g):
                for s in range(NSUB):
                    for gp in range(NGP):
                        g.wait_ge(self.sems["evd"], evd_dtx(s, gp))
                        for n in range(16):
                            g.wait_ge(self.sems["evd"], evd_brepev(s, n))
                            pn = lin_n(s, gp, n) - 4
                            if pn >= 0:
                                g.wait_ge(self.sems["evd"],
                                          evd_scan(*from_lin_n(pn)))
                            self.inc(g.tensor_mul(
                                ub[:, (n % 4) * SUB:(n % 4 + 1) * SUB],
                                dtxb[:, (gp % 2) * SUB:(gp % 2) * SUB + SUB],
                                brep[:, n * SUB:(n + 1) * SUB]), "evg", 1)
                    g.wait_ge(self.sems["evd"], evd_yg(s, NGP - 1))
                    self.inc(g.dma_start(
                        out=yg_dst[:, s * SUB:(s + 1) * SUB].rearrange(
                            "(a p) t -> p a t", p=128),
                        in_=ygst[:, (s % 2) * NGP * SUB:
                                 ((s % 2) + 1) * NGP * SUB].rearrange(
                            "p (a t) -> p a t", a=NGP)), "stg", 16)
                    if state_dbg is not None:
                        self.inc(g.dma_start(
                            out=state_dbg[:, s * 128:(s + 1) * 128],
                            in_=stateb[:, :]), "stg", 16)

            @blk.vector
            def _(v):
                for s in range(NSUB):
                    par = s % 2
                    v.wait_ge(self.sems["lda"], lda_s(s))
                    if s == 0:
                        v.memset(stateb[:, :], 0.0)
                        v.memset(xtail[:, :], 0.0)
                    for n in range(16):
                        v.wait_ge(self.sems["pe"], pe_brep(s, n))
                        if s >= 2:
                            v.wait_ge(self.sems["evg"],
                                      evg_u(s - 2, NGP - 1, n))
                        self.inc(v.tensor_copy(
                            brep[:, n * SUB:(n + 1) * SUB],
                            ps_rb[:, (n % 2) * 512:(n % 2) * 512 + 512]), "evd", 1)
                    for gp in range(NGP):
                        gx = (gp % 2) * (SUB + 3)
                        v.wait_ge(self.sems["pe"], pe_xexp(s, gp))
                        self.inc(v.tensor_copy(
                            xexp[:, gx + 3:gx + 3 + SUB],
                            ps_x[:, (gp % 2) * 512:(gp % 2) * 512 + 512]), "evd", 1)
                        v.tensor_copy(xexp[:, gx:gx + 3],
                                      xtail[:, gp * 3:(gp + 1) * 3])
                        pgl = lin_gp(s, gp) - 2
                        if pgl >= 0:
                            psx, pgp = pgl // NGP, pgl % NGP
                            v.wait_ge(self.sems["eva"], eva_silu(psx, pgp))
                        v.tensor_scalar_mul(
                            xcb[:, (gp % 2) * SUB:(gp % 2 + 1) * SUB],
                            xexp[:, gx:gx + SUB],
                            self.convw_sb[:, 4 * gp:4 * gp + 1])
                        for tau in range(1, 4):
                            v.scalar_tensor_tensor(
                                xcb[:, (gp % 2) * SUB:(gp % 2 + 1) * SUB],
                                xexp[:, gx + tau:gx + tau + SUB],
                                self.convw_sb[:, 4 * gp + tau:4 * gp + tau + 1],
                                xcb[:, (gp % 2) * SUB:(gp % 2 + 1) * SUB],
                                ALU.mult, ALU.add)
                        self.inc(v.tensor_copy(xtail[:, gp * 3:(gp + 1) * 3],
                                               xexp[:, gx + SUB:gx + SUB + 3]),
                                 "evd", 1)
                        if pgl >= 0:
                            psx, pgp = pgl // NGP, pgl % NGP
                            v.wait_ge(self.sems["evg"], evg_u(psx, pgp, 15))
                        v.wait_ge(self.sems["eva"], eva_silu(s, gp))
                        self.inc(v.tensor_mul(
                            dtxb[:, (gp % 2) * SUB:(gp % 2 + 1) * SUB],
                            dtb2[:, par * NGP * SUB + gp * SUB:
                                 par * NGP * SUB + (gp + 1) * SUB],
                            xcb[:, (gp % 2) * SUB:(gp % 2 + 1) * SUB]), "evd", 1)
                        for n in range(16):
                            v.wait_ge(self.sems["eva"], eva_da(s, gp, n))
                            v.wait_ge(self.sems["evg"], evg_u(s, gp, n))
                            col = gp * 16 + n
                            v.tensor_tensor_scan(
                                hb[:, (n % 4) * SUB:(n % 4 + 1) * SUB],
                                dab[:, (n % 4) * SUB:(n % 4 + 1) * SUB],
                                ub[:, (n % 4) * SUB:(n % 4 + 1) * SUB],
                                stateb[:, col:col + 1], ALU.mult, ALU.add)
                            v.wait_ge(self.sems["eva"], eva_crepev(s, n))
                            self.inc(v.tensor_mul(
                                tmpb[:, n * SUB:(n + 1) * SUB],
                                hb[:, (n % 4) * SUB:(n % 4 + 1) * SUB],
                                crep[:, n * SUB:(n + 1) * SUB]), "evd", 1)
                            v.tensor_copy(
                                stateb[:, col:col + 1],
                                hb[:, (n % 4 + 1) * SUB - 1:(n % 4 + 1) * SUB])
                        v.tensor_reduce(
                            yb[:, SUB:2 * SUB],
                            tmpb[:, :].rearrange("p (n t) -> p t n", n=16),
                            AX.X, ALU.add)
                        v.scalar_tensor_tensor(
                            yb[:, 0:SUB],
                            xcb[:, (gp % 2) * SUB:(gp % 2 + 1) * SUB],
                            self.dpar_sb[:, gp:gp + 1],
                            yb[:, SUB:2 * SUB], ALU.mult, ALU.add)
                        if s >= 2:
                            v.wait_ge(self.sems["stg"], stg_s(s - 2))
                        self.inc(v.tensor_mul(
                            ygst[:, par * NGP * SUB + gp * SUB:
                                 par * NGP * SUB + (gp + 1) * SUB],
                            yb[:, 0:SUB],
                            szb2[:, gp * SUB:(gp + 1) * SUB]), "evd", 1)


def _host_prep_mixer(inputs, c):
    w_in = np.asarray(inputs["L0_attn_in_proj_w"])
    conv_w = np.asarray(inputs["L0_attn_conv_w"])
    conv_b = np.asarray(inputs["L0_attn_conv_b"])
    dtw = np.asarray(inputs["L0_attn_dt_proj_w"])
    dtb = np.asarray(inputs["L0_attn_dt_proj_b"])
    a_log = np.asarray(inputs["L0_attn_A_log"])
    d_par = np.asarray(inputs["L0_attn_D_param"])
    ln1 = np.asarray(inputs["L0_attn_ln1_w"])
    ln2 = np.asarray(inputs["L0_attn_ln2_w"])
    q = c % 4
    d = {}
    rows = []
    for gp in range(NGP):
        rows.append(1024 * q + _dlocal(gp))
    for j in range(2):
        p = np.arange(128)
        rows.append(4096 + 256 * q + 16 * (p % 16) + 8 * j + p // 16)
    for j in range(2):
        p = np.arange(128)
        rows.append(5120 + 256 * q + 16 * (p % 16) + 8 * j + p // 16)
    for j in range(8):
        p = np.arange(128)
        rows.append(6144 + 1024 * q + 16 * (p % 64) + 2 * j + p // 64)
    rows.append(10240 + np.arange(128))
    rows = np.concatenate(rows)
    w_c = w_in[rows, :]
    d["w_inproj"] = np.ascontiguousarray(
        w_c.reshape(M_INPROJ, 128, 16, 128).transpose(0, 3, 2, 1))
    d["w_dtproj"] = np.ascontiguousarray(
        np.stack([dtw[1024 * q + _dlocal(gp), :].T for gp in range(NGP)]))
    d["dt_bias"] = np.ascontiguousarray(
        np.stack([dtb[1024 * q + _dlocal(gp)] for gp in range(NGP)], axis=1))
    d["a_log_p"] = np.ascontiguousarray(np.concatenate(
        [a_log[1024 * q + _dlocal(gp), :] for gp in range(NGP)], axis=1))
    d["conv_w_p"] = np.ascontiguousarray(np.concatenate(
        [conv_w[1024 * q + _dlocal(gp), 0, :] for gp in range(NGP)], axis=1))
    d["conv_b_p"] = np.ascontiguousarray(
        np.stack([conv_b[1024 * q + _dlocal(gp)] for gp in range(NGP)], axis=1))
    d["d_par_p"] = np.ascontiguousarray(
        np.stack([d_par[1024 * q + _dlocal(gp)] for gp in range(NGP)], axis=1))
    d["ln1_w"] = np.ascontiguousarray(ln1.reshape(16, 128).T)
    d["ln2_w"] = np.ascontiguousarray(ln2.reshape(16, 128).T)
    return d


MIXER_IN_NAMES = ["w_inproj", "w_dtproj", "dt_bias", "a_log_p", "conv_w_p",
                  "conv_b_p", "d_par_p", "ln1_w", "ln2_w"]
MIXER_IN_SHAPES = {
    "w_inproj": ([M_INPROJ, 128, 16, 128], F32),
    "w_dtproj": ([NGP, 128, 128], F32),
    "dt_bias": ([128, NGP], F32),
    "a_log_p": ([128, NGP * 16], F32),
    "conv_w_p": ([128, NGP * 4], F32),
    "conv_b_p": ([128, NGP], F32),
    "d_par_p": ([128, NGP], F32),
    "ln1_w": ([128, 16], F32),
    "ln2_w": ([128, 16], F32),
}


def _declare_mixer_inputs(nc):
    t_in = {}
    for name in MIXER_IN_NAMES:
        shp, dt = MIXER_IN_SHAPES[name]
        t_in[name] = nc.dram_tensor(name, shp, dt, kind="ExternalInput").ap()
    return t_in


def _mixer_bounces(nc, kind="Internal"):
    return {
        "sz_b": nc.dram_tensor("sz_b", [1024, SEQ], BF16, kind=kind),
        "xr_b": nc.dram_tensor("xr_b", [256, SEQ], BF16, kind=kind),
        "br_b": nc.dram_tensor("br_b", [256, SEQ], BF16, kind=kind),
        "cr_b": nc.dram_tensor("cr_b", [1024, SEQ], BF16, kind=kind),
        "dt_b": nc.dram_tensor("dt_b", [1024, SEQ], F32, kind=kind),
    }


def _alloc_repl(cp, es):
    nc = cp.nc
    E = es.enter_context
    cp.rb_sb = E(nc.sbuf_tensor("rb_sb", [128, 16 * 128], BF16, side="right"))
    cp.rc_sb = E(nc.sbuf_tensor("rc_sb", [128, 2 * 128], BF16, side="right"))
    cp.rx_sb = E(nc.sbuf_tensor("rx_sb", [128, 16 * 128], BF16, side="right"))


def build_mixer_test(phases="all"):
    import contextlib
    nc = bass.Bass("TRN2", target_bir_lowering=False, debug=False,
                   detect_race_conditions=False)
    cp = CoreProgram(nc)
    cp.sem_init()
    _alloc_repl(cp, contextlib.ExitStack())
    t_in = _declare_mixer_inputs(nc)
    v_src = nc.dram_tensor("v_src", [4 * 2048, 1024], BF16,
                           kind="ExternalInput").ap()
    yg_out = nc.dram_tensor("yg_out", [1024, SEQ], BF16,
                            kind="ExternalOutput").ap()
    import os as _ost
    bzs = _mixer_bounces(nc, "ExternalOutput" if _ost.environ.get("DBG_BOUNCE") == "1" else "Internal")
    cp.build_p0(t_in)
    if phases in ("p0m1", "all"):
        cp.build_m1(t_in, v_src, bzs)
    if phases == "all":
        sdbg = None
        if _ost.environ.get("DBG_STATE") == "1":
            sdbg = nc.dram_tensor("state_dbg", [128, NSUB * 128], F32,
                                  kind="ExternalOutput").ap()
        cp.build_m2(bzs, yg_out, state_dbg=sdbg)
    else:
        with (nc.sbuf_tensor("zf", [128, SEQ], BF16) as zf, nc.Block() as blk):
            @blk.vector
            def _(v):
                cp.inc(v.memset(zf[:, :], 0.0), "evd", 1)

            @blk.gpsimd
            def _(g):
                g.wait_ge(cp.sems["evd"], cp.cnt["evd"])
                for j in range(8):
                    cp.inc(g.dma_start(out=yg_out[128 * j:128 * (j + 1), :],
                                       in_=zf[:, :]), "stg", 16)
    with nc.Block() as blk:
        @blk.sync
        def _(s):
            s.wait_ge(cp.sems["stg"], cp.cnt["stg"])
    return nc, cp


def kernel(**inputs):
    raise NotImplementedError("full kernel assembled in a later stage")


def emit_norm512(cp, tb, x_tile, out_tile, mode, eps, nk=16,
                 w_col=None, add_tile=None):
    eps_col = {EPS_LN: 0, EPS_FFN: 1}[eps]
    nc = cp.nc
    ones_col, ones_row = tb["ones_col"], tb["ones_row"]
    sq16, rows, rows16 = tb["sq16"], tb["rows"], tb["rows16"]
    rep, tmpf, psR = tb["rep"], tb["tmpf"], tb["psR"]
    need_mean = (mode == "ln")

    with nc.Block() as blk:
        eva0 = cp.cnt["eva"]
        evd0 = cp.cnt["evd"]
        pe0b = cp.pe_issued
        mk_sq = pe0b + nk + 1
        mk_sums = pe0b + (nk + 1) * (2 if need_mean else 1)
        mk_bcast = mk_sums + (3 if need_mean else 2)

        @blk.scalar
        def _(sc):
            for k in range(nk):
                cp.inc(sc.activation(sq16[:, k * 512:(k + 1) * 512],
                                     x_tile(k), AF.Square), "eva", 1)
            if need_mean:
                sc.wait_ge(cp.sems["pe"], mk_sq)
                for k in range(nk):
                    cp.inc(sc.activation(
                        sq16[:, k * 512:(k + 1) * 512],
                        x_tile(k), AF.Copy), "eva", 1)
            sc.wait_ge(cp.sems["evd"], evd0 + 1)
            sc.activation(rows[0:1, 0:512], rows[0:1, 0:512], AF.Ln,
                          bias=cp.eps_sb[0:1, eps_col:eps_col + 1])
            cp.inc(sc.activation(rows[0:1, 0:512], rows[0:1, 0:512],
                                 AF.Exp, scale=cp.eps_sb[0:1, 2:3]), "eva", 1)

        @blk.tensor
        def _(t):
            t.wait_ge(cp.sems["eva"], eva0 + nk)
            for k in range(nk):
                cp.pe_mm(psR[0:1, 0:512], ones_col[:, 0:1],
                         sq16[:, k * 512:(k + 1) * 512],
                         start=(k == 0), stop=(k == nk - 1))
            cp.pe_drain()
            if need_mean:
                t.wait_ge(cp.sems["eva"], eva0 + 2 * nk)
                for k in range(nk):
                    cp.pe_mm(psR[0:1, 512:1024], ones_col[:, 0:1],
                             sq16[:, k * 512:(k + 1) * 512],
                             start=(k == 0), stop=(k == nk - 1))
                cp.pe_drain()
            t.wait_ge(cp.sems["evd"], evd0 + 2)
            cp.pe_mm(psR[:, 0:512], ones_row[0:1, 0:128],
                     rows16[0:1, 0:512], start=True, stop=True)
            if need_mean:
                cp.pe_mm(psR[:, 512:1024], ones_row[0:1, 0:128],
                         rows16[0:1, 512:1024], start=True, stop=True)
            cp.pe_drain()

        @blk.vector
        def _(v):
            D = 128 * nk
            v.wait_ge(cp.sems["pe"], mk_sums)
            v.tensor_scalar_mul(rows[0:1, 0:512], psR[0:1, 0:512], 1.0 / D)
            if need_mean:
                v.tensor_scalar_mul(rows[0:1, 512:1024],
                                    psR[0:1, 512:1024], 1.0 / D)
                v.tensor_mul(rows[0:1, 1024:1536], rows[0:1, 512:1024],
                             rows[0:1, 512:1024])
                cp.inc(v.tensor_sub(rows[0:1, 0:512], rows[0:1, 0:512],
                                    rows[0:1, 1024:1536]), "evd", 1)
            else:
                cp.inc(v.engine_nop(), "evd", 1)
            v.wait_ge(cp.sems["eva"], eva0 + nk * (2 if need_mean else 1) + 1)
            v.tensor_copy(rows16[0:1, 0:512], rows[0:1, 0:512])
            if need_mean:
                v.tensor_mul(rows[0:1, 1024:1536], rows[0:1, 512:1024],
                             rows[0:1, 0:512])
                v.tensor_scalar_mul(rows[0:1, 1024:1536],
                                    rows[0:1, 1024:1536], -1.0)
                cp.inc(v.tensor_copy(rows16[0:1, 512:1024],
                                     rows[0:1, 1024:1536]), "evd", 1)
            else:
                cp.inc(v.tensor_copy(rows16[0:1, 512:1024],
                                     rows16[0:1, 0:512]), "evd", 1)
            v.wait_ge(cp.sems["pe"], mk_bcast)
            v.tensor_copy(rep[:, 0:512], psR[:, 0:512])
            if need_mean:
                v.tensor_copy(rep[:, 512:1024], psR[:, 512:1024])
            for k in range(nk):
                if mode == "ln":
                    v.tensor_mul(tmpf[:, 0:512], x_tile(k), rep[:, 0:512])
                    if add_tile is not None:
                        v.tensor_add(tmpf[:, 0:512], tmpf[:, 0:512],
                                     rep[:, 512:1024])
                        cp.inc(v.tensor_add(out_tile(k), tmpf[:, 0:512],
                                            add_tile(k)), "evd", 1)
                    else:
                        cp.inc(v.tensor_add(out_tile(k), tmpf[:, 0:512],
                                            rep[:, 512:1024]), "evd", 1)
                elif w_col is not None:
                    v.tensor_mul(tmpf[:, 0:512], x_tile(k), rep[:, 0:512])
                    cp.inc(v.tensor_scalar_mul(out_tile(k), tmpf[:, 0:512],
                                               w_col(k)), "evd", 1)
                else:
                    cp.inc(v.tensor_mul(out_tile(k), x_tile(k),
                                        rep[:, 0:512]), "evd", 1)


_NRM_SEQ = [0]


def _norm_scratch(nc, es, sq16=None):
    E = es.enter_context
    _NRM_SEQ[0] += 1
    sfx = f"_{_NRM_SEQ[0]}"
    tb = {}
    tb["sq16"] = sq16 if sq16 is not None else E(
        nc.sbuf_tensor("nrm_sq" + sfx, [128, 16 * 512], BF16))
    tb["rows"] = E(nc.sbuf_tensor("nrm_rows" + sfx, [1, 3 * 512], F32))
    tb["rows16"] = E(nc.sbuf_tensor("nrm_r16" + sfx, [1, 2 * 512], BF16))
    tb["rep"] = E(nc.sbuf_tensor("nrm_rep" + sfx, [128, 3 * 512], F32))
    tb["tmpf"] = tb["rep"][:, 1024:1536]
    tb["psR"] = E(nc.psum_tensor("nrm_psr" + sfx, [128, 2 * 512], F32))
    return tb


def build_consts(cp):
    nc = cp.nc
    oc = nc.inline_tensor(np.ones((128, 1), np.float32), name="ones_col").ap()
    orw = nc.inline_tensor(np.ones((1, 128), np.float32), name="ones_row").ap()
    ident = nc.inline_tensor(np.eye(128, dtype=np.float32), name="ident").ap()
    epsv = np.tile(np.array([[1e-5, 1e-6, -0.5, 0.0]], np.float32), (128, 1))
    epsd = nc.inline_tensor(epsv, name="eps_c").ap()
    cp.ones_col = cp.open(nc.sbuf_tensor("ones_col_sb", [128, 1], BF16))
    cp.ones_row = cp.open(nc.sbuf_tensor("ones_row_sb", [1, 128], BF16))
    cp.ident_sb = cp.open(nc.sbuf_tensor("ident_sb", [128, 128], BF16))
    cp.eps_sb = cp.open(nc.sbuf_tensor("eps_sb", [128, 4], F32))
    with nc.Block() as blk:
        @blk.gpsimd
        def _(g):
            cp.inc(g.dma_start(out=cp.ones_col[:, :], in_=oc[:, :]), "ldw", 16)
            cp.inc(g.dma_start(out=cp.ones_row[:, :], in_=orw[:, :]), "ldw", 16)
            cp.inc(g.dma_start(out=cp.ident_sb[:, :], in_=ident[:, :]), "ldw", 16)
            cp.inc(g.dma_start(out=cp.eps_sb[:, :], in_=epsd[:, :]), "ldw", 16)

        @blk.sync
        def _(s):
            s.wait_ge(cp.sems["ldw"], cp.cnt["ldw"])


def build_p1(cp, xT, v_mine):
    nc = cp.nc
    import contextlib
    with contextlib.ExitStack() as es:
        E = es.enter_context
        xb = E(nc.sbuf_tensor("p1_x", [128, 16 * 1024], F32))
        u0 = E(nc.sbuf_tensor("p1_u0", [128, 16 * 512], F32))
        vst = E(nc.sbuf_tensor("p1_v", [128, 16 * 512], BF16))
        tb = _norm_scratch(nc, es)
        tb["ones_col"] = cp.ones_col
        tb["ones_row"] = cp.ones_row

        with nc.Block() as blk:
            @blk.sync
            def _(s):
                cp.inc(s.dma_start(
                    out=xb[:, :].rearrange("p (k t) -> p k t", k=16),
                    in_=xT.rearrange("(k p) t -> p k t", p=128)), "lda", 16)

            @blk.vector
            def _(v):
                v.wait_ge(cp.sems["lda"], cp.cnt["lda"])
                cp.inc(v.engine_nop(), "evd", 1)

        for h in range(2):
            emit_norm512(cp, tb,
                         lambda k, h=h: xb[:, k * 1024 + h * 512:
                                           k * 1024 + h * 512 + 512],
                         lambda k: u0[:, k * 512:(k + 1) * 512],
                         "ln", EPS_LN)
            emit_norm512(cp, tb,
                         lambda k: u0[:, k * 512:(k + 1) * 512],
                         lambda k: vst[:, k * 512:(k + 1) * 512],
                         "rms", EPS_RMS,
                         w_col=lambda k: cp.ln1_sb[:, k:k + 1])
            with nc.Block() as blk:
                @blk.gpsimd
                def _(g):
                    g.wait_ge(cp.sems["evd"], cp.cnt["evd"])
                    cp.inc(g.dma_start(
                        out=v_mine.ap()[:, h * 512:(h + 1) * 512].rearrange(
                            "(k p) t -> p k t", p=128),
                        in_=vst[:, :].rearrange("p (k t) -> p k t", k=16)),
                        "stg", 16)
                    g.wait_ge(cp.sems["stg"], cp.cnt["stg"])


def build_dequant(cp, t_in, fp4_mine):
    fp4g_mine, fp4d_mine = fp4_mine
    nc = cp.nc
    import contextlib
    with contextlib.ExitStack() as es:
        E = es.enter_context
        pk = E(nc.sbuf_tensor("dq_pk", [128, 11 * 256 + 2 * 704], I32))
        sc_gu = E(nc.sbuf_tensor("dq_sgu", [128, 11 * 16], F32))
        sc_dn = E(nc.sbuf_tensor("dq_sdn", [128, 2 * 44], F32))
        nib = E(nc.sbuf_tensor("dq_nib", [128, 5632], I32))
        fx = E(nc.sbuf_tensor("dq_fx", [128, 5632], F32))
        t1 = E(nc.sbuf_tensor("dq_t1", [128, 5632], F32))
        t2 = E(nc.sbuf_tensor("dq_t2", [128, 5632], F32))
        dqb = E(nc.sbuf_tensor("dq_b", [128, 5632], BF16))
        stage = E(nc.sbuf_tensor("dq_st", [128, 44 * 128], BF16))
        pst = E(nc.psum_tensor("dq_ps", [128, 1024], BF16))
        blk = E(nc.Block())

        eva0 = cp.cnt["eva"]
        evd0 = cp.cnt["evd"]
        lda0 = cp.cnt["lda"]
        ldw0 = cp.cnt["ldw"]
        stg0 = cp.cnt["stg"]
        pe0 = cp.pe_issued

        tiles = [("gu", ti, 256, 2048, 16) for ti in range(11)] + \
                [("dn", ti, 704, 5632, 44) for ti in range(2)]

        def evd_tile(i):
            return evd0 + i + 1

        def eva_tile(i):
            kcs = [t[4] for t in tiles]
            return eva0 + sum(kcs[:i + 1])

        def pe_tile_t(i):
            kcs = [t[4] for t in tiles]
            return pe0 + sum(2 * k for k in kcs[:i + 1])

        @blk.sync
        def _(s):
            cp.inc(s.dma_start(out=pk[:, 0:11 * 256].rearrange(
                "p (a w) -> p a w", a=11),
                in_=t_in["gu_packed"].rearrange("(a p) w -> p a w", p=128)),
                "lda", 16)
            cp.inc(s.dma_start(out=pk[:, 11 * 256:].rearrange(
                "p (a w) -> p a w", a=2),
                in_=t_in["dn_packed"].rearrange("(a p) w -> p a w", p=128)),
                "lda", 16)

        @blk.gpsimd
        def _(g):
            cp.inc(g.dma_start(out=sc_gu[:, :], in_=t_in["gu_scales"]),
                   "ldw", 16)
            cp.inc(g.dma_start(out=sc_dn[:, :], in_=t_in["dn_scales"]),
                   "ldw", 16)
            for i, (kind, ti, wlen, inlen, kc) in enumerate(tiles):
                g.wait_ge(cp.sems["eva"], eva_tile(i))
                if kind == "gu":
                    dst = fp4g_mine.ap()[ti * 128:(ti + 1) * 128, :]
                else:
                    dst = fp4d_mine.ap()[ti * 128:(ti + 1) * 128, :]
                cp.inc(g.dma_start(out=dst, in_=stage[:, 0:kc * 128]),
                       "stg", 16)

        @blk.vector
        def _(v):
            v.wait_ge(cp.sems["lda"], lda0 + 32)
            v.wait_ge(cp.sems["ldw"], ldw0 + 32)
            for i, (kind, ti, wlen, inlen, kc) in enumerate(tiles):
                if i >= 1:
                    v.wait_ge(cp.sems["eva"], eva_tile(i - 1))
                woff = ti * 256 if kind == "gu" else 11 * 256 + ti * 704
                wsl = pk[:, woff:woff + wlen]
                for kn in range(8):
                    v.tensor_scalar(
                        nib[:, 0:inlen].rearrange("p (w e) -> p e w",
                                                  e=8)[:, kn, :],
                        wsl, 4 * kn, 15,
                        ALU.logical_shift_right, ALU.bitwise_and)
                v.tensor_copy(fx[:, 0:inlen], nib[:, 0:inlen])
                v.tensor_single_scalar(t1[:, 0:inlen], fx[:, 0:inlen], 8.0,
                                       ALU.is_ge)
                v.scalar_tensor_tensor(fx[:, 0:inlen], t1[:, 0:inlen], -8.0,
                                       fx[:, 0:inlen], ALU.mult, ALU.add)
                v.tensor_scalar(t1[:, 0:inlen], t1[:, 0:inlen], -2.0, 1.0,
                                ALU.mult, ALU.add)
                v.tensor_scalar(t2[:, 0:inlen], fx[:, 0:inlen], -4.0, 0.0,
                                ALU.add, ALU.max)
                v.tensor_add(t2[:, 0:inlen], t2[:, 0:inlen], fx[:, 0:inlen])
                v.tensor_scalar(fx[:, 0:inlen], fx[:, 0:inlen], -6.0, 0.0,
                                ALU.add, ALU.max)
                v.scalar_tensor_tensor(t2[:, 0:inlen], t2[:, 0:inlen], 0.5,
                                       fx[:, 0:inlen], ALU.mult, ALU.add)
                v.tensor_mul(t2[:, 0:inlen], t2[:, 0:inlen], t1[:, 0:inlen])
                scs = sc_gu if kind == "gu" else sc_dn
                ng = inlen // 128
                for gidx in range(ng):
                    v.tensor_scalar_mul(
                        dqb[:, gidx * 128:(gidx + 1) * 128],
                        t2[:, gidx * 128:(gidx + 1) * 128],
                        scs[:, ti * ng + gidx:ti * ng + gidx + 1])
                cp.inc(v.engine_nop(), "evd", 1)

        @blk.tensor
        def _(t):
            for i, (kind, ti, wlen, inlen, kc) in enumerate(tiles):
                t.wait_ge(cp.sems["evd"], evd_tile(i))
                for k in range(kc):
                    if i * 1000 + k >= 2:
                        pass
                    tot_prev = sum(tt[4] for tt in tiles[:i]) + k
                    if tot_prev >= 2:
                        t.wait_ge(cp.sems["eva"], eva0 + tot_prev - 1)
                    cp.pe_transpose(
                        pst[:, (k % 2) * 128:(k % 2) * 128 + 128],
                        dqb[:, k * 128:(k + 1) * 128])
                    cp.pe_drain()

        @blk.scalar
        def _(sc):
            for i, (kind, ti, wlen, inlen, kc) in enumerate(tiles):
                if i >= 1:
                    sc.wait_ge(cp.sems["stg"], stg0 + i * 16)
                for k in range(kc):
                    tot = sum(tt[4] for tt in tiles[:i]) + k
                    sc.wait_ge(cp.sems["pe"], pe0 + 2 * (tot + 1))
                    cp.inc(sc.activation(stage[:, k * 128:(k + 1) * 128],
                                         pst[:, (k % 2) * 128:
                                             (k % 2) * 128 + 128],
                                         AF.Copy), "eva", 1)


def build_token(cp, t_in, yg_all, xT, fp4_all, out_d):
    fp4g_all, fp4d_all = fp4_all
    nc = cp.nc
    import contextlib
    with contextlib.ExitStack() as es:
        E = es.enter_context
        AF1 = E(nc.sbuf_tensor("tk_af1", [128, 16 * 512], F32))
        AF2 = E(nc.sbuf_tensor("tk_af2", [128, 16 * 512], F32))
        AF3 = E(nc.sbuf_tensor("tk_af3", [128, 16 * 512], F32))
        AB1 = E(nc.sbuf_tensor("tk_ab1", [128, 64 * 512], BF16))
        AB2 = E(nc.sbuf_tensor("tk_ab2", [128, 16 * 512], BF16))
        wb1 = E(nc.sbuf_tensor("tk_wb1", [128, 2 * 16 * 128], BF16))
        sgt = E(nc.sbuf_tensor("tk_sgt", [128, 2 * 512], BF16))
        psA = E(nc.psum_tensor("tk_psa", [128, 2 * 512], F32))
        psD = E(nc.psum_tensor("tk_psd", [128, 2 * 512], F32))
        tb = _norm_scratch(nc, es, sq16=AB1[:, 44 * 512:60 * 512])
        tb["ones_col"] = cp.ones_col
        tb["ones_row"] = cp.ones_row

        def t16(buf):
            return lambda k: buf[:, k * 512:(k + 1) * 512]

        for w in range(2):
            my_t0 = 512 * w

            with nc.Block() as blk:
                @blk.sync
                def _(s):
                    cp.inc(s.dma_start(
                        out=AF1[:, :].rearrange("p (k t) -> p k t", k=16),
                        in_=xT[:, my_t0:my_t0 + 512].rearrange(
                            "(k p) t -> p k t", p=128)), "lda", 16)

                @blk.vector
                def _(v):
                    v.wait_ge(cp.sems["lda"], cp.cnt["lda"])
                    cp.inc(v.engine_nop(), "evd", 1)
            emit_norm512(cp, tb, t16(AF1), t16(AF2), "ln", EPS_LN)

            _mm_stage(cp, nc, wb1, psD, nm=16, nkc=32,
                      wsrc=lambda m, h, kw: t_in["w_outproj"][
                          m, :, h * kw:(h + 1) * kw, :],
                      load_yg=(yg_all, my_t0), ygbuf=AB1,
                      moving=lambda k: AB1[:, k * 512:(k + 1) * 512],
                      evac=("dve_add", t16(AF2), t16(AF2)))

            emit_norm512(cp, tb, t16(AF2), t16(AB2), "rms", EPS_RMS,
                         w_col=lambda k: cp.ln2_sb[:, k:k + 1])

            _mlp_gateup(cp, nc, wb1, psA, psD, sgt, AB2, AB1,
                        t_in["w_gate"], t_in["w_up"], 64)

            _mm_stage(cp, nc, wb1, psD, nm=16, nkc=64,
                      wsrc=lambda m, h, kw: t_in["w_down"][
                          m, :, h * kw:(h + 1) * kw, :],
                      load_yg=None, ygbuf=None,
                      moving=lambda k: AB1[:, k * 512:(k + 1) * 512],
                      evac=("dve_add", t16(AF2), t16(AF3)))
            with nc.Block() as blk:
                @blk.sync
                def _(s):
                    cp.inc(s.dma_start(
                        out=AF1[:, :].rearrange("p (k t) -> p k t", k=16),
                        in_=xT[:, my_t0:my_t0 + 512].rearrange(
                            "(k p) t -> p k t", p=128)), "lda", 16)

                @blk.vector
                def _(v):
                    v.wait_ge(cp.sems["lda"], cp.cnt["lda"])
                    cp.inc(v.engine_nop(), "evd", 1)
            emit_norm512(cp, tb, t16(AF3), t16(AF3), "ln", EPS_LN,
                         add_tile=t16(AF1))
            emit_norm512(cp, tb, t16(AF3), t16(AB2), "ln", EPS_LN)

            _ffn_gateup(cp, nc, wb1, psA, psD, sgt, AB2, AB1, fp4g_all, 44)

            _ffn_down(cp, nc, wb1, psA, AB1, fp4d_all, AF1)
            emit_norm512(cp, tb, t16(AF1), t16(AF1), "rmsdiv", EPS_FFN)
            emit_norm512(cp, tb, t16(AF1), t16(AF2), "ln", EPS_LN,
                         add_tile=t16(AF3))
            with nc.Block() as blk:
                @blk.gpsimd
                def _(g):
                    g.wait_ge(cp.sems["evd"], cp.cnt["evd"])
                    cp.inc(g.dma_start(
                        out=out_d[:, my_t0:my_t0 + 512].rearrange(
                            "(k p) t -> p k t", p=128),
                        in_=AF2[:, :].rearrange("p (k t) -> p k t", k=16)),
                        "sto", 16)


def _mm_stage(cp, nc, wbuf, ps, nm, nkc, wsrc, load_yg, ygbuf, moving, evac,
              kw=16):
    nchunk = nkc // kw
    with nc.Block() as blk:
        pe0 = cp.pe_issued
        ldw0 = cp.cnt["ldw"]
        lda0 = cp.cnt["lda"]
        evd0 = cp.cnt["evd"]

        def pe_m(m):
            return pe0 + (m + 1) * (nkc + 1)

        def pe_chunk(m, h):
            return pe0 + m * (nkc + 1) + (h + 1) * kw

        @blk.gpsimd
        def _(g):
            for m in range(nm):
                for h in range(nchunk):
                    gidx = m * nchunk + h
                    if gidx >= 2:
                        pm, ph = (gidx - 2) // nchunk, (gidx - 2) % nchunk
                        g.wait_ge(cp.sems["pe"], pe_chunk(pm, ph))
                    off = (gidx % 2) * kw * 128
                    cp.inc(g.dma_start(
                        out=wbuf[:, off:off + kw * 128].rearrange(
                            "p (k c) -> p k c", k=kw),
                        in_=wsrc(m, h, kw)), "ldw", 16)

        @blk.sync
        def _(s):
            if load_yg is not None:
                yg_all_ap, my_t0 = load_yg
                pid = nc.partition_id()
                off = (pid % 4) * 1024 + my_t0
                src_ap = bass.AP(yg_all_ap.tensor, off,
                                 [[4096, 128], [128 * 4096, 32], [1, 512]])
                cp.inc(s.dma_start(
                    out=ygbuf[:, 0:32 * 512].rearrange(
                        "p (k t) -> p k t", k=32),
                    in_=src_ap), "lda", 16)

        @blk.tensor
        def _(t):
            if load_yg is not None:
                t.wait_ge(cp.sems["lda"], lda0 + 16)
            for m in range(nm):
                if m >= 2:
                    t.wait_ge(cp.sems["evd"], evd0 + m - 1)
                for h in range(nchunk):
                    gidx = m * nchunk + h
                    t.wait_ge(cp.sems["ldw"], ldw0 + (gidx + 1) * 16)
                    off = (gidx % 2) * kw * 128
                    for k in range(kw):
                        cp.pe_mm(ps[:, (m % 2) * 512:(m % 2) * 512 + 512],
                                 wbuf[:, off + k * 128:off + (k + 1) * 128],
                                 moving(h * kw + k),
                                 start=(h == 0 and k == 0),
                                 stop=(h == nchunk - 1 and k == kw - 1))
                cp.pe_drain()

        @blk.vector
        def _(v):
            kind, a_t, o_t = evac
            for m in range(nm):
                v.wait_ge(cp.sems["pe"], pe_m(m))
                cp.inc(v.tensor_add(o_t(m), a_t(m),
                                    ps[:, (m % 2) * 512:(m % 2) * 512 + 512]),
                       "evd", 1)


def _mlp_gateup(cp, nc, wbuf, psA, psD, sgt, h2, hh, wg, wu, nm):
    with nc.Block() as blk:
        pe0 = cp.pe_issued
        ldw0 = cp.cnt["ldw"]
        eva0 = cp.cnt["eva"]
        evd0 = cp.cnt["evd"]

        def pe_pair(m):
            return pe0 + (m + 1) * 34

        @blk.gpsimd
        def _(g):
            for m in range(nm):
                for half, w in ((0, wg), (1, wu)):
                    if m >= 1:
                        g.wait_ge(cp.sems["pe"],
                                  pe0 + (m - 1) * 34 + (half + 1) * 17)
                    cp.inc(g.dma_start(
                        out=wbuf[:, half * 16 * 128:(half + 1) * 16 * 128
                                 ].rearrange("p (k c) -> p k c", k=16),
                        in_=w[m]), "ldw", 16)

        @blk.tensor
        def _(t):
            for m in range(nm):
                t.wait_ge(cp.sems["ldw"], ldw0 + (2 * m + 1) * 16)
                if m >= 2:
                    t.wait_ge(cp.sems["eva"], eva0 + m - 1)
                for k in range(16):
                    cp.pe_mm(psA[:, (m % 2) * 512:(m % 2) * 512 + 512],
                             wbuf[:, k * 128:(k + 1) * 128],
                             h2[:, k * 512:(k + 1) * 512],
                             start=(k == 0), stop=(k == 15))
                cp.pe_drain()
                t.wait_ge(cp.sems["ldw"], ldw0 + (2 * m + 2) * 16)
                if m >= 2:
                    t.wait_ge(cp.sems["evd"], evd0 + m - 1)
                for k in range(16):
                    cp.pe_mm(psD[:, (m % 2) * 512:(m % 2) * 512 + 512],
                             wbuf[:, 16 * 128 + k * 128:
                                  16 * 128 + (k + 1) * 128],
                             h2[:, k * 512:(k + 1) * 512],
                             start=(k == 0), stop=(k == 15))
                cp.pe_drain()

        @blk.scalar
        def _(sc):
            for m in range(nm):
                sc.wait_ge(cp.sems["pe"], pe0 + m * 34 + 17)
                if m >= 2:
                    sc.wait_ge(cp.sems["evd"], evd0 + m - 1)
                cp.inc(sc.activation(sgt[:, (m % 2) * 512:(m % 2) * 512 + 512],
                                     psA[:, (m % 2) * 512:(m % 2) * 512 + 512],
                                     AF.Silu), "eva", 1)

        @blk.vector
        def _(v):
            for m in range(nm):
                v.wait_ge(cp.sems["pe"], pe_pair(m))
                v.wait_ge(cp.sems["eva"], eva0 + m + 1)
                cp.inc(v.tensor_mul(hh[:, m * 512:(m + 1) * 512],
                                    sgt[:, (m % 2) * 512:(m % 2) * 512 + 512],
                                    psD[:, (m % 2) * 512:(m % 2) * 512 + 512]),
                       "evd", 1)


def _ffn_gateup(cp, nc, wbuf, psA, psD, sgt, u1, hh2, fp4g_all, nm):

    with nc.Block() as blk:
        pe0 = cp.pe_issued
        ldw0 = cp.cnt["ldw"]
        eva0 = cp.cnt["eva"]
        evd0 = cp.cnt["evd"]

        @blk.sync
        def _(s):
            for m in range(nm):
                for half in range(2):
                    j = m if half == 0 else 44 + m
                    if m >= 1:
                        s.wait_ge(cp.sems["pe"],
                                  pe0 + (m - 1) * 34 + (half + 1) * 17)
                    cp.inc(s.dma_start(
                        out=wbuf[:, half * 16 * 128:(half + 1) * 16 * 128],
                        in_=fp4g_all[j * 128:(j + 1) * 128, :]), "lda", 16)

        lda0 = cp.cnt["lda"] - 2 * nm * 16

        @blk.tensor
        def _(t):
            for m in range(nm):
                t.wait_ge(cp.sems["lda"], lda0 + (2 * m + 1) * 16)
                if m >= 2:
                    t.wait_ge(cp.sems["eva"], eva0 + m - 1)
                for k in range(16):
                    cp.pe_mm(psA[:, (m % 2) * 512:(m % 2) * 512 + 512],
                             wbuf[:, k * 128:(k + 1) * 128],
                             u1[:, k * 512:(k + 1) * 512],
                             start=(k == 0), stop=(k == 15))
                cp.pe_drain()
                t.wait_ge(cp.sems["lda"], lda0 + (2 * m + 2) * 16)
                if m >= 2:
                    t.wait_ge(cp.sems["evd"], evd0 + m - 1)
                for k in range(16):
                    cp.pe_mm(psD[:, (m % 2) * 512:(m % 2) * 512 + 512],
                             wbuf[:, 16 * 128 + k * 128:
                                  16 * 128 + (k + 1) * 128],
                             u1[:, k * 512:(k + 1) * 512],
                             start=(k == 0), stop=(k == 15))
                cp.pe_drain()

        @blk.scalar
        def _(sc):
            for m in range(nm):
                sc.wait_ge(cp.sems["pe"], pe0 + m * 34 + 17)
                if m >= 2:
                    sc.wait_ge(cp.sems["evd"], evd0 + m - 1)
                cp.inc(sc.activation(sgt[:, (m % 2) * 512:(m % 2) * 512 + 512],
                                     psA[:, (m % 2) * 512:(m % 2) * 512 + 512],
                                     AF.Silu), "eva", 1)

        @blk.vector
        def _(v):
            for m in range(nm):
                v.wait_ge(cp.sems["pe"], pe0 + (m + 1) * 34)
                v.wait_ge(cp.sems["eva"], eva0 + m + 1)
                cp.inc(v.tensor_mul(hh2[:, m * 512:(m + 1) * 512],
                                    sgt[:, (m % 2) * 512:(m % 2) * 512 + 512],
                                    psD[:, (m % 2) * 512:(m % 2) * 512 + 512]),
                       "evd", 1)


def _ffn_down(cp, nc, wbuf, psA, hh2, fp4d_all, yout, kw=11):
    nchunk = 44 // kw

    with nc.Block() as blk:
        pe0 = cp.pe_issued
        lda0 = cp.cnt["lda"]
        eva0 = cp.cnt["eva"]

        def pe_chunk(m, h):
            return pe0 + m * 45 + (h + 1) * kw

        @blk.sync
        def _(s):
            for m in range(16):
                for h in range(nchunk):
                    gidx = m * nchunk + h
                    if gidx >= 2:
                        pm, ph = (gidx - 2) // nchunk, (gidx - 2) % nchunk
                        s.wait_ge(cp.sems["pe"], pe_chunk(pm, ph))
                    off = (gidx % 2) * kw * 128
                    cp.inc(s.dma_start(
                        out=wbuf[:, off:off + kw * 128],
                        in_=fp4d_all[m * 128:(m + 1) * 128,
                                     h * kw * 128:(h + 1) * kw * 128]),
                        "lda", 16)

        @blk.tensor
        def _(t):
            for m in range(16):
                if m >= 2:
                    t.wait_ge(cp.sems["eva"], eva0 + m - 1)
                for h in range(nchunk):
                    gidx = m * nchunk + h
                    t.wait_ge(cp.sems["lda"], lda0 + (gidx + 1) * 16)
                    off = (gidx % 2) * kw * 128
                    for k in range(kw):
                        cp.pe_mm(psA[:, (m % 2) * 512:(m % 2) * 512 + 512],
                                 wbuf[:, off + k * 128:off + (k + 1) * 128],
                                 hh2[:, (h * kw + k) * 512:
                                     (h * kw + k + 1) * 512],
                                 start=(h == 0 and k == 0),
                                 stop=(h == nchunk - 1 and k == kw - 1))
                cp.pe_drain()

        @blk.scalar
        def _(sc):
            for m in range(16):
                sc.wait_ge(cp.sems["pe"], pe0 + (m + 1) * 45)
                cp.inc(sc.activation(yout[:, m * 512:(m + 1) * 512],
                                     psA[:, (m % 2) * 512:(m % 2) * 512 + 512],
                                     AF.Copy), "eva", 1)


TOKEN_IN_SHAPES = {
    "xT": ([2048, 1024], F32),
    "w_outproj": ([16, 128, 32, 128], F32),
    "w_gate": ([64, 128, 16, 128], F32),
    "w_up": ([64, 128, 16, 128], F32),
    "w_down": ([16, 128, 64, 128], F32),
    "gu_packed": ([1408, 256], I32),
    "gu_scales": ([128, 11 * 16], F16),
    "dn_packed": ([256, 704], I32),
    "dn_scales": ([128, 2 * 44], F16),
}
RANKSZ_FP4 = (11 * 16 + 2 * 44) * 128


def build_full():
    import contextlib
    nc = bass.Bass("TRN2", target_bir_lowering=False, debug=False,
                   detect_race_conditions=False)
    cp = CoreProgram(nc)
    cp.sem_init()
    t_in = _declare_mixer_inputs(nc)
    for name, (shp, dt) in TOKEN_IN_SHAPES.items():
        t_in[name] = nc.dram_tensor(name, shp, dt, kind="ExternalInput").ap()
    out_d = nc.dram_tensor("out", [2048, 1024], F32, kind="ExternalOutput").ap()

    bzs = _mixer_bounces(nc)
    v_mine = nc.dram_tensor("v_mine", [2048, 1024], BF16)
    v_all = nc.dram_tensor("v_all", [4 * 2048, 1024], BF16)
    yg_mine = nc.dram_tensor("yg_mine", [1024, SEQ], BF16)
    yg_all = nc.dram_tensor("yg_all", [4 * 1024, SEQ], BF16)
    fp4g_mine = nc.dram_tensor("fp4g_mine", [11 * 128, 2048], BF16)
    fp4d_mine = nc.dram_tensor("fp4d_mine", [2 * 128, 5632], BF16)
    fp4g_all = nc.dram_tensor("fp4g_all", [88 * 128, 2048], BF16)
    fp4d_all = nc.dram_tensor("fp4d_all", [16 * 128, 5632], BF16)

    build_consts(cp)
    import contextlib as _ctxlib
    mix_es = _ctxlib.ExitStack()
    _alloc_repl(cp, mix_es)
    cp.build_p0(t_in)
    build_dequant(cp, t_in, (fp4g_mine, fp4d_mine))
    with nc.Block() as blk:
        @blk.gpsimd
        def _(g):
            g.wait_ge(cp.sems["stg"], cp.cnt["stg"])
            cp.inc(g.collective_compute(
                "AllGather", ALU.bypass,
                replica_groups=[list(range(8))],
                ins=[fp4g_mine.ap().opt()],
                outs=[fp4g_all.ap().opt()]), "cc", 1)
            cp.inc(g.collective_compute(
                "AllGather", ALU.bypass,
                replica_groups=[list(range(8))],
                ins=[fp4d_mine.ap().opt()],
                outs=[fp4d_all.ap().opt()]), "cc", 1)
    build_p1(cp, t_in["xT"], v_mine)
    with nc.Block() as blk:
        @blk.gpsimd
        def _(g):
            g.wait_ge(cp.sems["stg"], cp.cnt["stg"])
            cp.inc(g.collective_compute(
                "AllGather", ALU.bypass,
                replica_groups=[[0, 1, 2, 3], [4, 5, 6, 7]],
                ins=[v_mine.ap().opt()],
                outs=[v_all.ap().opt()]), "cc", 1)
    cp.build_m1(t_in, v_all.ap(), bzs, wait_cc=3)
    cp.build_m2(bzs, yg_mine.ap())
    mix_es.close()
    with nc.Block() as blk:
        @blk.gpsimd
        def _(g):
            g.wait_ge(cp.sems["stg"], cp.cnt["stg"])
            cp.inc(g.collective_compute(
                "AllGather", ALU.bypass,
                replica_groups=[[0, 1, 2, 3], [4, 5, 6, 7]],
                ins=[yg_mine.ap().opt()],
                outs=[yg_all.ap().opt()]), "cc", 1)
    with nc.Block() as blk:
        @blk.sync
        def _(s):
            s.wait_ge(cp.sems["cc"], 4)
    build_token(cp, t_in, yg_all.ap(), t_in["xT"],
                (fp4g_all.ap(), fp4d_all.ap()), out_d)
    with nc.Block() as blk:
        @blk.sync
        def _(s):
            s.wait_ge(cp.sems["sto"], cp.cnt["sto"])
    return nc, cp


def _host_prep_token(inputs, c):
    d = {}
    x = np.asarray(inputs["L0_attn_hidden_states"])
    b, q = c // 4, c % 4
    d["xT"] = np.ascontiguousarray(x[b, TOK * q:TOK * (q + 1), :].T)

    dperm_rank = np.concatenate([_dlocal(gp) for gp in range(NGP)])
    yg_dglobal = np.concatenate([1024 * r + dperm_rank for r in range(4)])
    w_out = np.asarray(inputs["L0_attn_out_proj_w"])
    w_out_lhsT = np.ascontiguousarray(w_out.T[yg_dglobal, :])
    d["w_outproj"] = np.ascontiguousarray(
        w_out_lhsT.reshape(32, 128, 16, 128).transpose(2, 1, 0, 3))

    def mk_lhsT(w):
        out_f, in_f = w.shape
        lt = np.ascontiguousarray(w.T)
        return np.ascontiguousarray(
            lt.reshape(in_f // 128, 128, out_f // 128, 128).transpose(2, 1, 0, 3))

    d["w_gate"] = mk_lhsT(np.asarray(inputs["L0_attn_gate_w"]))
    d["w_up"] = mk_lhsT(np.asarray(inputs["L0_attn_up_w"]))
    d["w_down"] = mk_lhsT(np.asarray(inputs["L0_attn_down_w"]))

    gu_p = np.asarray(inputs["L0_ffn_gate_up_packed"])
    gu_s = np.asarray(inputs["L0_ffn_gate_up_scales"])
    dn_p = np.asarray(inputs["L0_ffn_down_packed"])
    dn_s = np.asarray(inputs["L0_ffn_down_scales"])
    d["gu_packed"] = np.ascontiguousarray(gu_p[1408 * c:1408 * (c + 1), :])
    s = gu_s[:, 1408 * c:1408 * (c + 1)]
    d["gu_scales"] = np.ascontiguousarray(
        s.T.reshape(11, 128, 16).transpose(1, 0, 2).reshape(128, 11 * 16))
    d["dn_packed"] = np.ascontiguousarray(dn_p[256 * c:256 * (c + 1), :])
    s2 = dn_s[:, 256 * c:256 * (c + 1)]
    d["dn_scales"] = np.ascontiguousarray(
        s2.T.reshape(2, 128, 44).transpose(1, 0, 2).reshape(128, 2 * 44))
    return d


def kernel(**inputs):
    per_core = []
    for c in range(NCORE):
        d = _host_prep_mixer(inputs, c)
        d.update(_host_prep_token(inputs, c))
        per_core.append(d)
    nc, cp = build_full()
    res = run_bass_kernel_spmd(nc, per_core, core_ids=list(range(NCORE)))
    x = np.asarray(inputs["L0_attn_hidden_states"])
    out = np.empty_like(x)
    for c in range(NCORE):
        b, q = c // 4, c % 4
        out[b, TOK * q:TOK * (q + 1), :] = res.results[c]["out"].T
    return out


# version 05 (final)
# speedup vs baseline: 1.1792x; 1.1190x over previous
import numpy as np
import concourse.bass as bass
import concourse.mybir as mybir
from concourse.bass_utils import run_bass_kernel_spmd

F32 = mybir.dt.float32
BF16 = mybir.dt.bfloat16
F16 = mybir.dt.float16
I32 = mybir.dt.int32
AF = mybir.ActivationFunctionType
ALU = mybir.AluOpType
AX = mybir.AxisListType

D_MODEL = 2048
D_INNER = 4096
D_STATE = 16
DT_RANK = 128
INTER = 8192
FFN_I = 5632
BATCH, SEQ = 2, 4096
EPS_LN = 1e-5
EPS_RMS = 1e-5
EPS_FFN = 1e-6

NCORE = 8
TOK = 1024
NGP = 8
SUB = 512
NSUB = SEQ // SUB
M_INPROJ = 21
PE_WINDOW = 3


def _dlocal(gp):
    p = np.arange(128)
    return 16 * (p % 64) + (2 * gp + p // 64)


def _repl_matrices():
    RB = np.zeros((16, 128, 128), np.float32)
    for n in range(16):
        for p in range(128):
            RB[n, (n % 8) * 16 + (p % 64) // 4, p] = 1.0
    RC = np.zeros((2, 128, 128), np.float32)
    for par in range(2):
        for p in range(128):
            RC[par, 64 * par + p % 64, p] = 1.0
    RX = np.zeros((8, 2, 128, 128), np.float32)
    for gp in range(8):
        for p in range(128):
            g = 2 * gp + p // 64
            RX[gp, g // 8, (g % 8) * 16 + (p % 64) // 4, p] = 1.0
    return RB, RC, RX


class CoreProgram:
    def __init__(self, nc):
        self.nc = nc
        self._sem_names = ["pe", "eva", "evd", "evg", "ldw", "lda", "sto",
                          "cc", "ldc", "stg"]
        self.sems = {}
        self.cnt = {}
        self.pe_issued = 0
        self.pe_waited = 0
        self._keep = []

    def open(self, ctx):
        v = ctx.__enter__()
        self._keep.append(ctx)
        return v

    def sem_init(self):
        for n in self._sem_names:
            self.sems[n] = self.open(self.nc.semaphore(n))
            self.cnt[n] = 0

    def inc(self, inst, sem, n=1):
        inst.then_inc(self.sems[sem], n)
        self.cnt[sem] += n
        return self.cnt[sem]

    def wait(self, eng, sem, val):
        eng.wait_ge(self.sems[sem], val)

    def _pe_window(self):
        t = self.nc.tensor
        if self.pe_issued - self.pe_waited >= 4:
            need = self.pe_issued - 2
            t.wait_ge(self.sems["pe"], need)
            self.pe_waited = need

    def pe_mm(self, out, lhsT, rhs, start, stop):
        self._pe_window()
        self.nc.tensor.matmul(out, lhsT, rhs, start=start,
                              stop=stop).then_inc(self.sems["pe"], 1)
        self.pe_issued += 1
        self.cnt["pe"] += 1
        return self.pe_issued

    def pe_transpose(self, out, in_):
        self._pe_window()
        self.nc.tensor.transpose(
            out, in_, self.ident_sb[:, :]).then_inc(self.sems["pe"], 1)
        self.pe_issued += 1
        self.cnt["pe"] += 1
        return self.pe_issued

    def pe_drain(self):
        self._pe_window()
        self.nc.tensor.drain().then_inc(self.sems["pe"], 1)
        self.pe_issued += 1
        self.cnt["pe"] += 1
        return self.pe_issued

    def build_p0(self, t_in):
        nc = self.nc
        op = self.open
        self.dtb_sb = op(nc.sbuf_tensor("dtb_sb", [128, NGP], F32))
        self.a_sb = op(nc.sbuf_tensor("a_sb", [128, NGP * 16], F32))
        self.convw_sb = op(nc.sbuf_tensor("convw_sb", [128, NGP * 4], F32))
        self.convb_sb = op(nc.sbuf_tensor("convb_sb", [128, NGP], F32))
        self.dpar_sb = op(nc.sbuf_tensor("dpar_sb", [128, NGP], F32))
        self.ln1_sb = op(nc.sbuf_tensor("ln1_sb", [128, 16], F32))
        self.ln2_sb = op(nc.sbuf_tensor("ln2_sb", [128, 16], F32))

        RB, RC, RX = _repl_matrices()
        rb_d = nc.inline_tensor(
            np.ascontiguousarray(RB.transpose(1, 0, 2).reshape(128, 16 * 128)),
            name="rb_c").ap()
        rc_d = nc.inline_tensor(
            np.ascontiguousarray(RC.transpose(1, 0, 2).reshape(128, 2 * 128)),
            name="rc_c").ap()
        rx_d = nc.inline_tensor(
            np.ascontiguousarray(RX.transpose(2, 0, 1, 3).reshape(128, 16 * 128)),
            name="rx_c").ap()

        import os as _osp
        if _osp.environ.get("P0SKIP") == "1":
            return
        with nc.Block() as blk:
            @blk.gpsimd
            def _(g):
                self.inc(g.dma_start(out=self.rb_sb[:, :], in_=rb_d[:, :]), "ldw", 16)
                self.inc(g.dma_start(out=self.rc_sb[:, :], in_=rc_d[:, :]), "ldw", 16)
                self.inc(g.dma_start(out=self.rx_sb[:, :], in_=rx_d[:, :]), "ldw", 16)

            @blk.sync
            def _(s):
                for name, sb in (("dt_bias", self.dtb_sb), ("a_log_p", self.a_sb),
                                 ("conv_w_p", self.convw_sb),
                                 ("conv_b_p", self.convb_sb),
                                 ("d_par_p", self.dpar_sb), ("ln1_w", self.ln1_sb),
                                 ("ln2_w", self.ln2_sb)):
                    self.inc(s.dma_start(out=sb[:, :], in_=t_in[name]), "ldc", 16)

            @blk.scalar
            def _(sc):
                self.wait(sc, "ldc", self.cnt["ldc"])
                self.inc(sc.activation(self.a_sb[:, :], self.a_sb[:, :], AF.Exp),
                         "eva", 1)

            @blk.vector
            def _(v):
                self.wait(v, "eva", self.cnt["eva"])
                self.inc(v.tensor_scalar_mul(self.a_sb[:, :], self.a_sb[:, :], -1.0),
                         "evd", 1)

    def build_m1(self, t_in, v_src, bzs, wait_cc=None):
        nc = self.nc
        NTC = 8
        PERTC_PE = M_INPROJ * 17 + NGP * 2

        import contextlib
        with contextlib.ExitStack() as es:
            E = es.enter_context
            w_in_sb = E(nc.sbuf_tensor("w_in_sb", [128, M_INPROJ * 2048], BF16))
            w_dt_sb = E(nc.sbuf_tensor("w_dt_sb", [128, NGP * 128], BF16))
            vbuf = E(nc.sbuf_tensor("vbuf", [128, 2 * 16 * 512], BF16))
            zst = E(nc.sbuf_tensor("zst", [128, 8 * 512], BF16))
            xbst = E(nc.sbuf_tensor("xbst", [128, 2 * 512], BF16))
            bst = E(nc.sbuf_tensor("bst", [128, 2 * 512], BF16))
            cst = E(nc.sbuf_tensor("cst", [128, 8 * 512], BF16))
            dtst = E(nc.sbuf_tensor("dtst", [128, 8 * 512], F32))
            dtraw = E(nc.sbuf_tensor("dtraw", [128, 2 * 512], BF16))
            sp1 = E(nc.sbuf_tensor("sp1", [128, 2 * 512], F32))
            sp2 = E(nc.sbuf_tensor("sp2", [128, 2 * 512], F32))
            ps_z = E(nc.psum_tensor("ps_z", [128, 2 * 512], F32))
            ps_o = E(nc.psum_tensor("ps_o", [128, 2 * 512], F32))
            ps_dt = E(nc.psum_tensor("ps_dt", [128, 2 * 512], F32))
            blk = E(nc.Block())
            pe0 = self.pe_issued
            eva0 = self.cnt["eva"]
            evd0 = self.cnt["evd"]
            lda0 = self.cnt["lda"]
            stg0 = self.cnt["stg"]

            def pe_tile(tc, m):
                return pe0 + tc * PERTC_PE + (m + 1) * 17

            def pe_dt(tc, gp):
                return pe0 + tc * PERTC_PE + M_INPROJ * 17 + (gp + 1) * 2

            def eva_silu(tc, zi):
                return eva0 + tc * 16 + zi + 1

            def eva_u3(tc, gp):
                return eva0 + tc * 16 + 8 + gp + 1

            def evd_copy(tc, m):
                return evd0 + tc * 21 + (m - 8) + 1

            def evd_dtfin(tc, gp):
                return evd0 + tc * 21 + 13 + gp + 1

            def stg_tc(tc):
                return stg0 + (tc + 1) * 5 * 16

            def lda_tc(tc):
                return lda0 + (tc + 1) * 16 * 16

            ldw_need = self.cnt["ldw"] + 22 * 16

            def o_idx(tc, m):
                return tc * 13 + (m - 8)

            def o_from_idx(i):
                return i // 13, i % 13 + 8

            @blk.tensor
            def _(t):
                t.wait_ge(self.sems["ldw"], ldw_need)
                for tc in range(NTC):
                    t.wait_ge(self.sems["lda"], lda_tc(tc))
                    voff = (tc % 2) * 16 * 512
                    for m in range(M_INPROJ):
                        if m < 8:
                            if tc * 8 + m >= 2:
                                pi = tc * 8 + m - 2
                                t.wait_ge(self.sems["eva"],
                                          eva_silu(pi // 8, pi % 8))
                            pb = ps_z[:, (m % 2) * 512:(m % 2) * 512 + 512]
                        else:
                            oi = o_idx(tc, m)
                            if oi >= 2:
                                ptc, pm = o_from_idx(oi - 2)
                                t.wait_ge(self.sems["evd"], evd_copy(ptc, pm))
                            pb = ps_o[:, (oi % 2) * 512:(oi % 2) * 512 + 512]
                        for k in range(16):
                            self.pe_mm(
                                pb,
                                w_in_sb[:, m * 2048 + k * 128:
                                        m * 2048 + (k + 1) * 128],
                                vbuf[:, voff + k * 512:voff + (k + 1) * 512],
                                start=(k == 0), stop=(k == 15))
                        self.pe_drain()
                    t.wait_ge(self.sems["evd"], evd_copy(tc, 20))
                    for gp in range(NGP):
                        i = tc * NGP + gp
                        if i >= 2:
                            ptc, pgp = (i - 2) // NGP, (i - 2) % NGP
                            t.wait_ge(self.sems["evd"], evd_dtfin(ptc, pgp))
                        self.pe_mm(ps_dt[:, (gp % 2) * 512:(gp % 2) * 512 + 512],
                                   w_dt_sb[:, gp * 128:(gp + 1) * 128],
                                   dtraw[:, (tc % 2) * 512:(tc % 2) * 512 + 512],
                                   start=True, stop=True)
                        self.pe_drain()

            @blk.sync
            def _(s):
                if wait_cc is not None:
                    self.wait(s, "cc", wait_cc)
                for tc in range(NTC):
                    if tc >= 1:
                        s.wait_ge(self.sems["lda"], lda_tc(tc - 1))
                    if tc >= 2:
                        s.wait_ge(self.sems["pe"], pe_tile(tc - 2, M_INPROJ - 1))
                    r, cb = tc // 2, (tc % 2) * 512
                    for k in range(16):
                        self.inc(s.dma_start(
                            out=vbuf[:, (tc % 2) * 16 * 512 + k * 512:
                                     (tc % 2) * 16 * 512 + (k + 1) * 512],
                            in_=v_src[2048 * r + 128 * k:2048 * r + 128 * (k + 1),
                                      cb:cb + 512]), "lda", 16)

            @blk.scalar
            def _(sc):
                for tc in range(NTC):
                    if tc >= 1:
                        self.wait(sc, "stg", stg_tc(tc - 1))
                    for zi in range(8):
                        sc.wait_ge(self.sems["pe"], pe_tile(tc, zi))
                        self.inc(sc.activation(
                            zst[:, zi * 512:(zi + 1) * 512],
                            ps_z[:, (zi % 2) * 512:(zi % 2) * 512 + 512],
                            AF.Silu), "eva", 1)
                    for gp in range(NGP):
                        i = tc * NGP + gp
                        if i >= 2:
                            ptc, pgp = (i - 2) // NGP, (i - 2) % NGP
                            sc.wait_ge(self.sems["evd"], evd_dtfin(ptc, pgp))
                        sc.wait_ge(self.sems["pe"], pe_dt(tc, gp))
                        h = (gp % 2) * 512
                        sc.activation(sp2[:, h:h + 512], ps_dt[:, h:h + 512],
                                      AF.Relu, bias=self.dtb_sb[:, gp:gp + 1])
                        sc.activation(sp1[:, h:h + 512], ps_dt[:, h:h + 512],
                                      AF.Abs, bias=self.dtb_sb[:, gp:gp + 1])
                        sc.activation(sp1[:, h:h + 512], sp1[:, h:h + 512],
                                      AF.Exp, scale=-1.0)
                        self.inc(sc.activation(sp1[:, h:h + 512], sp1[:, h:h + 512],
                                               AF.Ln, bias=1.0), "eva", 1)

            @blk.vector
            def _(v):
                for tc in range(NTC):
                    if tc >= 1:
                        self.wait(v, "stg", stg_tc(tc - 1))
                    for m in range(8, M_INPROJ):
                        v.wait_ge(self.sems["pe"], pe_tile(tc, m))
                        if m == 20 and tc >= 2:
                            v.wait_ge(self.sems["pe"], pe_dt(tc - 2, NGP - 1))
                        oi = o_idx(tc, m)
                        srcp = ps_o[:, (oi % 2) * 512:(oi % 2) * 512 + 512]
                        if m < 10:
                            dst = xbst[:, (m - 8) * 512:(m - 7) * 512]
                        elif m < 12:
                            dst = bst[:, (m - 10) * 512:(m - 9) * 512]
                        elif m < 20:
                            dst = cst[:, (m - 12) * 512:(m - 11) * 512]
                        else:
                            dst = dtraw[:, (tc % 2) * 512:(tc % 2) * 512 + 512]
                        self.inc(v.tensor_copy(dst, srcp), "evd", 1)
                    for gp in range(NGP):
                        h = (gp % 2) * 512
                        v.wait_ge(self.sems["eva"], eva_u3(tc, gp))
                        self.inc(v.tensor_add(dtst[:, gp * 512:(gp + 1) * 512],
                                              sp2[:, h:h + 512],
                                              sp1[:, h:h + 512]), "evd", 1)

            @blk.gpsimd
            def _(g):
                for m in range(M_INPROJ):
                    self.inc(g.dma_start(
                        out=w_in_sb[:, m * 2048:(m + 1) * 2048].rearrange(
                            "p (k c) -> p k c", k=16),
                        in_=t_in["w_inproj"][m]), "ldw", 16)
                self.inc(g.dma_start(
                    out=w_dt_sb[:, :].rearrange("p (g c) -> p g c", g=NGP),
                    in_=t_in["w_dtproj"].rearrange("g p c -> p g c")), "ldw", 16)
                for tc in range(NTC):
                    g.wait_ge(self.sems["eva"], eva0 + (tc + 1) * 16)
                    g.wait_ge(self.sems["evd"], evd0 + (tc + 1) * 21)
                    c0 = 512 * tc
                    for dram, sb, nblk in ((bzs["sz_b"], zst, 8),
                                           (bzs["xr_b"], xbst, 2),
                                           (bzs["br_b"], bst, 2),
                                           (bzs["cr_b"], cst, 8),
                                           (bzs["dt_b"], dtst, 8)):
                        self.inc(g.dma_start(
                            out=dram.ap()[:, c0:c0 + 512].rearrange(
                                "(a p) t -> p a t", p=128),
                            in_=sb[:, :].rearrange("p (a t) -> p a t", a=nblk)),
                            "stg", 16)

    def build_m2(self, bzs, yg_dst):
        nc = self.nc
        stg_m1_done = self.cnt["stg"]
        SU = 1024
        NS = SEQ // SU
        import contextlib
        with contextlib.ExitStack() as es:
            E = es.enter_context
            dtb2 = E(nc.sbuf_tensor("dtb2", [128, NGP * SU], F32))
            szb2 = E(nc.sbuf_tensor("szb2", [128, NGP * SU], BF16))
            crb2 = E(nc.sbuf_tensor("crb2", [128, NGP * SU], BF16))
            brb2 = E(nc.sbuf_tensor("brb2", [128, 2 * SU], BF16))
            xrb2 = E(nc.sbuf_tensor("xrb2", [128, 2 * SU], BF16))
            brep = E(nc.sbuf_tensor("brep", [128, 16 * SU], BF16))
            crep = E(nc.sbuf_tensor("crep", [128, 16 * SU], BF16))
            xexp = E(nc.sbuf_tensor("xexp", [128, 2 * (SU + 3)], F32))
            xtail = E(nc.sbuf_tensor("xtail", [128, NGP * 3], F32))
            xcb = E(nc.sbuf_tensor("xcb", [128, 2 * SU], F32))
            dtxb = E(nc.sbuf_tensor("dtxb", [128, 2 * SU], F32))
            dab = E(nc.sbuf_tensor("dab", [128, 2 * SU], F32))
            ub = E(nc.sbuf_tensor("ub", [128, 4 * SU], F32))
            yb = E(nc.sbuf_tensor("yb", [128, SU], F32))
            ygst = E(nc.sbuf_tensor("ygst", [128, 2 * SU], BF16))
            stateb = E(nc.sbuf_tensor("stateb", [128, 128], F32))
            ps_bc = E(nc.psum_tensor("ps_bc", [128, 2 * 1024], F32))
            ps_x = E(nc.psum_tensor("ps_x", [128, 1024], F32))
            ps_y = E(nc.psum_tensor("ps_y", [128, 1024], F32))
            blk = E(nc.Block())

            pe0 = self.pe_issued
            eva0 = self.cnt["eva"]
            evd0 = self.cnt["evd"]
            evg0 = self.cnt["evg"]
            lda0 = self.cnt["lda"]
            stg0 = self.cnt["stg"]

            def hb_ap(n):
                sl = (n % 4) * SU
                return crb2[:, sl:sl + SU]

            def tmp_ap(n):
                sl = (4 + n % 4) * SU
                return crb2[:, sl:sl + SU]

            def pe_plan():
                for n in range(16):
                    yield ("bmm", n, 0)
                    yield ("bmm", n, 1)
                    yield ("bdrain", n)
                for n in range(16):
                    yield ("cmm", n, 0)
                    yield ("cmm", n, 1)
                    yield ("cdrain", n)

                def xops(gp):
                    for ch in range(2):
                        for j in range(2):
                            yield ("xmm", gp, ch, j)
                    yield ("xdrain", gp)

                def sumops(gp):
                    for n in range(16):
                        yield ("smm", gp, n, 0)
                        yield ("smm", gp, n, 1)
                    yield ("sdrain", gp)

                yield from xops(0)
                yield from xops(1)
                for gp in range(6):
                    yield from sumops(gp)
                    yield from xops(gp + 2)
                yield from sumops(6)
                yield from sumops(7)

            PE_S = sum(1 for _ in pe_plan())
            assert PE_S == 400
            mk_brep, mk_crep, mk_xexp, mk_sum, mk_sumdr = {}, {}, {}, {}, {}
            c = 0
            for op in pe_plan():
                c += 1
                k, a = op[0], op[1]
                if k == "bdrain":
                    mk_brep[a] = c
                elif k == "cdrain":
                    mk_crep[a] = c
                elif k == "xdrain":
                    mk_xexp[a] = c
                elif k == "smm" and op[3] == 1:
                    mk_sum[(a, op[2])] = c
                elif k == "sdrain":
                    mk_sumdr[a] = c

            def pe_brep(s, n):
                return pe0 + s * PE_S + mk_brep[n]

            def pe_crep(s, n):
                return pe0 + s * PE_S + mk_crep[n]

            def pe_xexp(s, gp):
                return pe0 + s * PE_S + mk_xexp[gp]

            def pe_sum(s, gp, n):
                return pe0 + s * PE_S + mk_sum[(gp, n)]

            def pe_sumdrain(s, gp):
                return pe0 + s * PE_S + mk_sumdr[gp]

            EVA_S = 64 + NGP * 17

            def eva_brep(s, n):
                return eva0 + s * EVA_S + 2 * (n + 1)

            def eva_crep(s, n):
                return eva0 + s * EVA_S + 32 + 2 * (n + 1)

            def eva_silu(s, gp):
                return eva0 + s * EVA_S + 64 + 17 * gp + 1

            def eva_da(s, gp, n):
                return eva0 + s * EVA_S + 64 + 17 * gp + 2 + n

            EVD_GP = 21
            EVD_S = NGP * EVD_GP

            def evd_base(s, gp):
                return evd0 + s * EVD_S + gp * EVD_GP

            def evd_xe2(s, gp):
                return evd_base(s, gp) + 2

            def evd_conv(s, gp):
                return evd_base(s, gp) + 3

            def evd_dtx(s, gp):
                return evd_base(s, gp) + 4

            def evd_pair(s, gp, n):
                return evd_base(s, gp) + 5 + n

            def evd_yg(s, gp):
                return evd_base(s, gp) + EVD_GP

            EVG_GP = 17
            EVG_S = NGP * EVG_GP

            def evg_base(s, gp):
                return evg0 + s * EVG_S + gp * EVG_GP

            def evg_u(s, gp, n):
                return evg_base(s, gp) + (n + 1 if n <= 2 else n + 2)

            def evg_cmul0(s, gp):
                return evg_base(s, gp) + 4

            def lda_s(s):
                return lda0 + (s + 1) * 5 * 16

            def stg_m2(s, gp):
                return stg0 + 16 * (s * NGP + gp + 1)

            @blk.sync
            def _(s_eng):
                s_eng.wait_ge(self.sems["stg"], stg_m1_done)
                for s in range(NS):
                    c0 = s * SU
                    if s >= 1:
                        s_eng.wait_ge(self.sems["evd"], evd_yg(s - 1, NGP - 1))
                        s_eng.wait_ge(self.sems["eva"],
                                      eva_da(s - 1, NGP - 1, 15))
                        s_eng.wait_ge(self.sems["pe"], pe_xexp(s - 1, NGP - 1))
                    for dram, buf, nblk in ((bzs["dt_b"], dtb2, 8),
                                            (bzs["sz_b"], szb2, 8),
                                            (bzs["cr_b"], crb2, 8),
                                            (bzs["br_b"], brb2, 2),
                                            (bzs["xr_b"], xrb2, 2)):
                        self.inc(s_eng.dma_start(
                            out=buf[:, 0:nblk * SU].rearrange(
                                "p (a t) -> p a t", a=nblk),
                            in_=dram.ap()[:, c0:c0 + SU].rearrange(
                                "(a p) t -> p a t", p=128)), "lda", 16)

            @blk.tensor
            def _(t):
                for s in range(NS):
                    t.wait_ge(self.sems["lda"], lda_s(s))
                    for op in pe_plan():
                        k = op[0]
                        if k == "bmm":
                            n, ch = op[1], op[2]
                            if ch == 0 and n >= 2:
                                t.wait_ge(self.sems["eva"],
                                          eva_brep(s, n - 2))
                            sl = (n % 2) * 1024 + ch * 512
                            self.pe_mm(
                                ps_bc[:, sl:sl + 512],
                                self.rb_sb[:, n * 128:(n + 1) * 128],
                                brb2[:, (n // 8) * SU + ch * 512:
                                     (n // 8) * SU + (ch + 1) * 512],
                                start=True, stop=True)
                        elif k == "cmm":
                            n, ch = op[1], op[2]
                            if ch == 0:
                                if n < 2:
                                    t.wait_ge(self.sems["eva"],
                                              eva_brep(s, 14 + n))
                                else:
                                    t.wait_ge(self.sems["eva"],
                                              eva_crep(s, n - 2))
                            sl = (n % 2) * 1024 + ch * 512
                            self.pe_mm(
                                ps_bc[:, sl:sl + 512],
                                self.rc_sb[:, (n % 2) * 128:
                                           (n % 2) * 128 + 128],
                                crb2[:, (n // 2) * SU + ch * 512:
                                     (n // 2) * SU + (ch + 1) * 512],
                                start=True, stop=True)
                        elif k == "xmm":
                            gp, ch, j = op[1], op[2], op[3]
                            if ch == 0 and j == 0:
                                lin = s * NGP + gp - 1
                                if lin >= 0:
                                    t.wait_ge(self.sems["evd"],
                                              evd_xe2(lin // NGP, lin % NGP))
                            self.pe_mm(
                                ps_x[:, ch * 512:(ch + 1) * 512],
                                self.rx_sb[:, gp * 256 + j * 128:
                                           gp * 256 + (j + 1) * 128],
                                xrb2[:, j * SU + ch * 512:
                                     j * SU + (ch + 1) * 512],
                                start=(j == 0), stop=(j == 1))
                        elif k == "smm":
                            gp, n, ch = op[1], op[2], op[3]
                            if ch == 0:
                                if n == 0:
                                    lin = s * NGP + gp - 1
                                    if lin >= 0:
                                        t.wait_ge(self.sems["evd"],
                                                  evd_yg(lin // NGP,
                                                         lin % NGP))
                                    t.wait_ge(self.sems["evg"],
                                              evg_cmul0(s, gp))
                                else:
                                    t.wait_ge(self.sems["evd"],
                                              evd_pair(s, gp, n))
                            self.pe_mm(
                                ps_y[:, ch * 512:(ch + 1) * 512],
                                self.ident_sb[:, :],
                                tmp_ap(n)[:, ch * 512:(ch + 1) * 512],
                                start=(n == 0), stop=(n == 15))
                        else:
                            self.pe_drain()

            @blk.scalar
            def _(sc):
                for s in range(NS):
                    sc.wait_ge(self.sems["lda"], lda_s(s))
                    for n in range(16):
                        sc.wait_ge(self.sems["pe"], pe_brep(s, n))
                        if s >= 1:
                            sc.wait_ge(self.sems["evg"],
                                       evg_u(s - 1, NGP - 1, n))
                        sl = (n % 2) * 1024
                        bb = n * SU
                        self.inc(sc.activation(brep[:, bb:bb + 512],
                                               ps_bc[:, sl:sl + 512],
                                               AF.Copy), "eva", 1)
                        self.inc(sc.activation(brep[:, bb + 512:bb + SU],
                                               ps_bc[:, sl + 512:sl + 1024],
                                               AF.Copy), "eva", 1)
                    for n in range(16):
                        sc.wait_ge(self.sems["pe"], pe_crep(s, n))
                        if s >= 1:
                            if n == 0:
                                sc.wait_ge(self.sems["evg"],
                                           evg_cmul0(s - 1, NGP - 1))
                            else:
                                sc.wait_ge(self.sems["evd"],
                                           evd_pair(s - 1, NGP - 1, n))
                        sl = (n % 2) * 1024
                        cc0 = n * SU
                        self.inc(sc.activation(crep[:, cc0:cc0 + 512],
                                               ps_bc[:, sl:sl + 512],
                                               AF.Copy), "eva", 1)
                        self.inc(sc.activation(crep[:, cc0 + 512:cc0 + SU],
                                               ps_bc[:, sl + 512:sl + 1024],
                                               AF.Copy), "eva", 1)
                    for gp in range(NGP):
                        sc.wait_ge(self.sems["evd"], evd_conv(s, gp))
                        self.inc(sc.activation(
                            xcb[:, (gp % 2) * SU:(gp % 2 + 1) * SU],
                            xcb[:, (gp % 2) * SU:(gp % 2 + 1) * SU],
                            AF.Silu, bias=self.convb_sb[:, gp:gp + 1]),
                            "eva", 1)
                        for n in range(16):
                            pl = (s * NGP + gp) * 16 + n - 2
                            if pl >= 0:
                                ps_, r = divmod(pl, NGP * 16)
                                sc.wait_ge(self.sems["evd"],
                                           evd_pair(ps_, r // 16, r % 16))
                            self.inc(sc.activation(
                                dab[:, (n % 2) * SU:(n % 2 + 1) * SU],
                                dtb2[:, gp * SU:(gp + 1) * SU],
                                AF.Exp,
                                scale=self.a_sb[:, gp * 16 + n:
                                                gp * 16 + n + 1]), "eva", 1)

            @blk.gpsimd
            def _(g):
                for s in range(NS):
                    for gp in range(NGP):
                        g.wait_ge(self.sems["evd"], evd_dtx(s, gp))

                        def u_op(n):
                            g.wait_ge(self.sems["eva"], eva_brep(s, n))
                            if n <= 3:
                                lin = s * NGP + gp - 1
                                if lin >= 0:
                                    g.wait_ge(self.sems["evd"],
                                              evd_pair(lin // NGP,
                                                       lin % NGP, 12 + n))
                            else:
                                g.wait_ge(self.sems["evd"],
                                          evd_pair(s, gp, n - 4))
                            self.inc(g.tensor_mul(
                                ub[:, (n % 4) * SU:(n % 4 + 1) * SU],
                                dtxb[:, (gp % 2) * SU:(gp % 2 + 1) * SU],
                                brep[:, n * SU:(n + 1) * SU]), "evg", 1)

                        for n in range(3):
                            u_op(n)
                        g.wait_ge(self.sems["evd"], evd_pair(s, gp, 0))
                        g.wait_ge(self.sems["eva"], eva_crep(s, 0))
                        lin = s * NGP + gp - 1
                        if lin >= 0:
                            g.wait_ge(self.sems["pe"],
                                      pe_sum(lin // NGP, lin % NGP, 12))
                        self.inc(g.tensor_mul(
                            tmp_ap(0), hb_ap(0), crep[:, 0:SU]), "evg", 1)
                        for n in range(3, 16):
                            u_op(n)
                        g.wait_ge(self.sems["evd"], evd_yg(s, gp))
                        self.inc(g.dma_start(
                            out=yg_dst[gp * 128:(gp + 1) * 128,
                                       s * SU:(s + 1) * SU],
                            in_=ygst[:, (gp % 2) * SU:(gp % 2 + 1) * SU]),
                            "stg", 16)

            @blk.vector
            def _(v):
                for s in range(NS):
                    if s == 0:
                        v.memset(stateb[:, :], 0.0)
                        v.memset(xtail[:, :], 0.0)
                    for gp in range(NGP):
                        lin = s * NGP + gp
                        gx = (gp % 2) * (SU + 3)
                        v.tensor_copy(xexp[:, gx:gx + 3],
                                      xtail[:, gp * 3:(gp + 1) * 3])
                        v.wait_ge(self.sems["pe"], pe_xexp(s, gp))
                        self.inc(v.tensor_copy(xexp[:, gx + 3:gx + 3 + 512],
                                               ps_x[:, 0:512]), "evd", 1)
                        self.inc(v.tensor_copy(
                            xexp[:, gx + 3 + 512:gx + 3 + SU],
                            ps_x[:, 512:1024]), "evd", 1)
                        v.tensor_scalar_mul(
                            xcb[:, (gp % 2) * SU:(gp % 2 + 1) * SU],
                            xexp[:, gx:gx + SU],
                            self.convw_sb[:, 4 * gp:4 * gp + 1])
                        for tau in range(1, 4):
                            i = v.scalar_tensor_tensor(
                                xcb[:, (gp % 2) * SU:(gp % 2 + 1) * SU],
                                xexp[:, gx + tau:gx + tau + SU],
                                self.convw_sb[:, 4 * gp + tau:
                                              4 * gp + tau + 1],
                                xcb[:, (gp % 2) * SU:(gp % 2 + 1) * SU],
                                ALU.mult, ALU.add)
                            if tau == 3:
                                self.inc(i, "evd", 1)
                        v.tensor_copy(xtail[:, gp * 3:(gp + 1) * 3],
                                      xexp[:, gx + SU:gx + SU + 3])
                        if lin >= 2:
                            pl = lin - 2
                            v.wait_ge(self.sems["evg"],
                                      evg_u(pl // NGP, pl % NGP, 15))
                        v.wait_ge(self.sems["eva"], eva_silu(s, gp))
                        self.inc(v.tensor_mul(
                            dtxb[:, (gp % 2) * SU:(gp % 2 + 1) * SU],
                            dtb2[:, gp * SU:(gp + 1) * SU],
                            xcb[:, (gp % 2) * SU:(gp % 2 + 1) * SU]),
                            "evd", 1)

                        def scan_op(n, inc):
                            v.wait_ge(self.sems["eva"], eva_da(s, gp, n))
                            v.wait_ge(self.sems["evg"], evg_u(s, gp, n))
                            col = gp * 16 + n
                            i = v.tensor_tensor_scan(
                                hb_ap(n),
                                dab[:, (n % 2) * SU:(n % 2 + 1) * SU],
                                ub[:, (n % 4) * SU:(n % 4 + 1) * SU],
                                stateb[:, col:col + 1], ALU.mult, ALU.add)
                            if inc:
                                self.inc(i, "evd", 1)

                        def cmul_op(n):
                            v.wait_ge(self.sems["eva"], eva_crep(s, n))
                            if n >= 4:
                                v.wait_ge(self.sems["pe"],
                                          pe_sum(s, gp, n - 4))
                            elif lin >= 1:
                                pl = lin - 1
                                v.wait_ge(self.sems["pe"],
                                          pe_sum(pl // NGP, pl % NGP,
                                                 12 + n))
                            self.inc(v.tensor_mul(
                                tmp_ap(n), hb_ap(n),
                                crep[:, n * SU:(n + 1) * SU]), "evd", 1)

                        def state_op(n):
                            col = gp * 16 + n
                            v.tensor_copy(stateb[:, col:col + 1],
                                          hb_ap(n)[:, SU - 1:SU])

                        scan_op(0, True)
                        scan_op(1, False)
                        state_op(0)
                        for n in range(1, 16):
                            cmul_op(n)
                            state_op(n)
                            if n < 15:
                                scan_op(n + 1, False)
                        v.wait_ge(self.sems["pe"], pe_sumdrain(s, gp))
                        v.scalar_tensor_tensor(
                            yb[:, :],
                            xcb[:, (gp % 2) * SU:(gp % 2 + 1) * SU],
                            self.dpar_sb[:, gp:gp + 1],
                            ps_y[:, :], ALU.mult, ALU.add)
                        if lin >= 2:
                            v.wait_ge(self.sems["stg"],
                                      stg_m2((lin - 2) // NGP,
                                             (lin - 2) % NGP))
                        self.inc(v.tensor_mul(
                            ygst[:, (gp % 2) * SU:(gp % 2 + 1) * SU],
                            yb[:, :],
                            szb2[:, gp * SU:(gp + 1) * SU]), "evd", 1)

        assert self.pe_issued == pe0 + NS * PE_S, \
            (self.pe_issued, pe0, NS * PE_S)
        assert self.cnt["eva"] == eva0 + NS * EVA_S
        assert self.cnt["evd"] == evd0 + NS * EVD_S
        assert self.cnt["evg"] == evg0 + NS * EVG_S
        assert self.cnt["stg"] == stg0 + NS * NGP * 16


def _host_prep_mixer(inputs, c):
    w_in = np.asarray(inputs["L0_attn_in_proj_w"])
    conv_w = np.asarray(inputs["L0_attn_conv_w"])
    conv_b = np.asarray(inputs["L0_attn_conv_b"])
    dtw = np.asarray(inputs["L0_attn_dt_proj_w"])
    dtb = np.asarray(inputs["L0_attn_dt_proj_b"])
    a_log = np.asarray(inputs["L0_attn_A_log"])
    d_par = np.asarray(inputs["L0_attn_D_param"])
    ln1 = np.asarray(inputs["L0_attn_ln1_w"])
    ln2 = np.asarray(inputs["L0_attn_ln2_w"])
    q = c % 4
    d = {}
    rows = []
    for gp in range(NGP):
        rows.append(1024 * q + _dlocal(gp))
    for j in range(2):
        p = np.arange(128)
        rows.append(4096 + 256 * q + 16 * (p % 16) + 8 * j + p // 16)
    for j in range(2):
        p = np.arange(128)
        rows.append(5120 + 256 * q + 16 * (p % 16) + 8 * j + p // 16)
    for j in range(8):
        p = np.arange(128)
        rows.append(6144 + 1024 * q + 16 * (p % 64) + 2 * j + p // 64)
    rows.append(10240 + np.arange(128))
    rows = np.concatenate(rows)
    w_c = w_in[rows, :]
    d["w_inproj"] = np.ascontiguousarray(
        w_c.reshape(M_INPROJ, 128, 16, 128).transpose(0, 3, 2, 1))
    d["w_dtproj"] = np.ascontiguousarray(
        np.stack([dtw[1024 * q + _dlocal(gp), :].T for gp in range(NGP)]))
    d["dt_bias"] = np.ascontiguousarray(
        np.stack([dtb[1024 * q + _dlocal(gp)] for gp in range(NGP)], axis=1))
    d["a_log_p"] = np.ascontiguousarray(np.concatenate(
        [a_log[1024 * q + _dlocal(gp), :] for gp in range(NGP)], axis=1))
    d["conv_w_p"] = np.ascontiguousarray(np.concatenate(
        [conv_w[1024 * q + _dlocal(gp), 0, :] for gp in range(NGP)], axis=1))
    d["conv_b_p"] = np.ascontiguousarray(
        np.stack([conv_b[1024 * q + _dlocal(gp)] for gp in range(NGP)], axis=1))
    d["d_par_p"] = np.ascontiguousarray(
        np.stack([d_par[1024 * q + _dlocal(gp)] for gp in range(NGP)], axis=1))
    d["ln1_w"] = np.ascontiguousarray(ln1.reshape(16, 128).T)
    d["ln2_w"] = np.ascontiguousarray(ln2.reshape(16, 128).T)
    return d


MIXER_IN_NAMES = ["w_inproj", "w_dtproj", "dt_bias", "a_log_p", "conv_w_p",
                  "conv_b_p", "d_par_p", "ln1_w", "ln2_w"]
MIXER_IN_SHAPES = {
    "w_inproj": ([M_INPROJ, 128, 16, 128], F32),
    "w_dtproj": ([NGP, 128, 128], F32),
    "dt_bias": ([128, NGP], F32),
    "a_log_p": ([128, NGP * 16], F32),
    "conv_w_p": ([128, NGP * 4], F32),
    "conv_b_p": ([128, NGP], F32),
    "d_par_p": ([128, NGP], F32),
    "ln1_w": ([128, 16], F32),
    "ln2_w": ([128, 16], F32),
}


def _declare_mixer_inputs(nc):
    t_in = {}
    for name in MIXER_IN_NAMES:
        shp, dt = MIXER_IN_SHAPES[name]
        t_in[name] = nc.dram_tensor(name, shp, dt, kind="ExternalInput").ap()
    return t_in


def _mixer_bounces(nc, kind="Internal"):
    return {
        "sz_b": nc.dram_tensor("sz_b", [1024, SEQ], BF16, kind=kind),
        "xr_b": nc.dram_tensor("xr_b", [256, SEQ], BF16, kind=kind),
        "br_b": nc.dram_tensor("br_b", [256, SEQ], BF16, kind=kind),
        "cr_b": nc.dram_tensor("cr_b", [1024, SEQ], BF16, kind=kind),
        "dt_b": nc.dram_tensor("dt_b", [1024, SEQ], F32, kind=kind),
    }


def _alloc_repl(cp, es):
    nc = cp.nc
    E = es.enter_context
    cp.rb_sb = E(nc.sbuf_tensor("rb_sb", [128, 16 * 128], BF16, side="right"))
    cp.rc_sb = E(nc.sbuf_tensor("rc_sb", [128, 2 * 128], BF16, side="right"))
    cp.rx_sb = E(nc.sbuf_tensor("rx_sb", [128, 16 * 128], BF16, side="right"))


def build_mixer_test(phases="all"):
    import contextlib
    nc = bass.Bass("TRN2", target_bir_lowering=False, debug=False,
                   detect_race_conditions=False)
    cp = CoreProgram(nc)
    cp.sem_init()
    build_consts(cp)
    _alloc_repl(cp, contextlib.ExitStack())
    t_in = _declare_mixer_inputs(nc)
    v_src = nc.dram_tensor("v_src", [4 * 2048, 1024], BF16,
                           kind="ExternalInput").ap()
    yg_out = nc.dram_tensor("yg_out", [1024, SEQ], BF16,
                            kind="ExternalOutput").ap()
    import os as _ost
    bzs = _mixer_bounces(nc, "ExternalOutput" if _ost.environ.get("DBG_BOUNCE") == "1" else "Internal")
    cp.build_p0(t_in)
    if phases in ("p0m1", "all"):
        cp.build_m1(t_in, v_src, bzs)
    if phases == "all":
        sdbg = None
        if _ost.environ.get("DBG_STATE") == "1":
            sdbg = nc.dram_tensor("state_dbg", [128, NSUB * 128], F32,
                                  kind="ExternalOutput").ap()
        cp.build_m2(bzs, yg_out)
    else:
        with (nc.sbuf_tensor("zf", [128, SEQ], BF16) as zf, nc.Block() as blk):
            @blk.vector
            def _(v):
                cp.inc(v.memset(zf[:, :], 0.0), "evd", 1)

            @blk.gpsimd
            def _(g):
                g.wait_ge(cp.sems["evd"], cp.cnt["evd"])
                for j in range(8):
                    cp.inc(g.dma_start(out=yg_out[128 * j:128 * (j + 1), :],
                                       in_=zf[:, :]), "stg", 16)
    with nc.Block() as blk:
        @blk.sync
        def _(s):
            s.wait_ge(cp.sems["stg"], cp.cnt["stg"])
    return nc, cp


def kernel(**inputs):
    raise NotImplementedError("full kernel assembled in a later stage")


def emit_norm512(cp, tb, x_tile, out_tile, mode, eps, nk=16,
                 w_col=None, add_tile=None):
    eps_col = {EPS_LN: 0, EPS_FFN: 1}[eps]
    nc = cp.nc
    ones_col, ones_row = tb["ones_col"], tb["ones_row"]
    sq16, rows, rows16 = tb["sq16"], tb["rows"], tb["rows16"]
    rep, tmpf, psR = tb["rep"], tb["tmpf"], tb["psR"]
    need_mean = (mode == "ln")

    with nc.Block() as blk:
        eva0 = cp.cnt["eva"]
        evd0 = cp.cnt["evd"]
        pe0b = cp.pe_issued
        mk_sq = pe0b + nk + 1
        mk_sums = pe0b + (nk + 1) * (2 if need_mean else 1)
        mk_bcast = mk_sums + (3 if need_mean else 2)

        @blk.scalar
        def _(sc):
            for k in range(nk):
                cp.inc(sc.activation(sq16[:, k * 512:(k + 1) * 512],
                                     x_tile(k), AF.Square), "eva", 1)
            if need_mean:
                sc.wait_ge(cp.sems["pe"], mk_sq)
                for k in range(nk):
                    cp.inc(sc.activation(
                        sq16[:, k * 512:(k + 1) * 512],
                        x_tile(k), AF.Copy), "eva", 1)
            sc.wait_ge(cp.sems["evd"], evd0 + 1)
            sc.activation(rows[0:1, 0:512], rows[0:1, 0:512], AF.Ln,
                          bias=cp.eps_sb[0:1, eps_col:eps_col + 1])
            cp.inc(sc.activation(rows[0:1, 0:512], rows[0:1, 0:512],
                                 AF.Exp, scale=cp.eps_sb[0:1, 2:3]), "eva", 1)

        @blk.tensor
        def _(t):
            t.wait_ge(cp.sems["eva"], eva0 + nk)
            for k in range(nk):
                cp.pe_mm(psR[0:1, 0:512], ones_col[:, 0:1],
                         sq16[:, k * 512:(k + 1) * 512],
                         start=(k == 0), stop=(k == nk - 1))
            cp.pe_drain()
            if need_mean:
                t.wait_ge(cp.sems["eva"], eva0 + 2 * nk)
                for k in range(nk):
                    cp.pe_mm(psR[0:1, 512:1024], ones_col[:, 0:1],
                             sq16[:, k * 512:(k + 1) * 512],
                             start=(k == 0), stop=(k == nk - 1))
                cp.pe_drain()
            t.wait_ge(cp.sems["evd"], evd0 + 2)
            cp.pe_mm(psR[:, 0:512], ones_row[0:1, 0:128],
                     rows16[0:1, 0:512], start=True, stop=True)
            if need_mean:
                cp.pe_mm(psR[:, 512:1024], ones_row[0:1, 0:128],
                         rows16[0:1, 512:1024], start=True, stop=True)
            cp.pe_drain()

        @blk.vector
        def _(v):
            D = 128 * nk
            v.wait_ge(cp.sems["pe"], mk_sums)
            v.tensor_scalar_mul(rows[0:1, 0:512], psR[0:1, 0:512], 1.0 / D)
            if need_mean:
                v.tensor_scalar_mul(rows[0:1, 512:1024],
                                    psR[0:1, 512:1024], 1.0 / D)
                v.tensor_mul(rows[0:1, 1024:1536], rows[0:1, 512:1024],
                             rows[0:1, 512:1024])
                cp.inc(v.tensor_sub(rows[0:1, 0:512], rows[0:1, 0:512],
                                    rows[0:1, 1024:1536]), "evd", 1)
            else:
                cp.inc(v.engine_nop(), "evd", 1)
            v.wait_ge(cp.sems["eva"], eva0 + nk * (2 if need_mean else 1) + 1)
            v.tensor_copy(rows16[0:1, 0:512], rows[0:1, 0:512])
            if need_mean:
                v.tensor_mul(rows[0:1, 1024:1536], rows[0:1, 512:1024],
                             rows[0:1, 0:512])
                v.tensor_scalar_mul(rows[0:1, 1024:1536],
                                    rows[0:1, 1024:1536], -1.0)
                cp.inc(v.tensor_copy(rows16[0:1, 512:1024],
                                     rows[0:1, 1024:1536]), "evd", 1)
            else:
                cp.inc(v.tensor_copy(rows16[0:1, 512:1024],
                                     rows16[0:1, 0:512]), "evd", 1)
            v.wait_ge(cp.sems["pe"], mk_bcast)
            v.tensor_copy(rep[:, 0:512], psR[:, 0:512])
            if need_mean:
                v.tensor_copy(rep[:, 512:1024], psR[:, 512:1024])
            for k in range(nk):
                if mode == "ln":
                    v.tensor_mul(tmpf[:, 0:512], x_tile(k), rep[:, 0:512])
                    if add_tile is not None:
                        v.tensor_add(tmpf[:, 0:512], tmpf[:, 0:512],
                                     rep[:, 512:1024])
                        cp.inc(v.tensor_add(out_tile(k), tmpf[:, 0:512],
                                            add_tile(k)), "evd", 1)
                    else:
                        cp.inc(v.tensor_add(out_tile(k), tmpf[:, 0:512],
                                            rep[:, 512:1024]), "evd", 1)
                elif w_col is not None:
                    v.tensor_mul(tmpf[:, 0:512], x_tile(k), rep[:, 0:512])
                    cp.inc(v.tensor_scalar_mul(out_tile(k), tmpf[:, 0:512],
                                               w_col(k)), "evd", 1)
                else:
                    cp.inc(v.tensor_mul(out_tile(k), x_tile(k),
                                        rep[:, 0:512]), "evd", 1)


_NRM_SEQ = [0]


def _norm_scratch(nc, es, sq16=None):
    E = es.enter_context
    _NRM_SEQ[0] += 1
    sfx = f"_{_NRM_SEQ[0]}"
    tb = {}
    tb["sq16"] = sq16 if sq16 is not None else E(
        nc.sbuf_tensor("nrm_sq" + sfx, [128, 16 * 512], BF16))
    tb["rows"] = E(nc.sbuf_tensor("nrm_rows" + sfx, [1, 3 * 512], F32))
    tb["rows16"] = E(nc.sbuf_tensor("nrm_r16" + sfx, [1, 2 * 512], BF16))
    tb["rep"] = E(nc.sbuf_tensor("nrm_rep" + sfx, [128, 3 * 512], F32))
    tb["tmpf"] = tb["rep"][:, 1024:1536]
    tb["psR"] = E(nc.psum_tensor("nrm_psr" + sfx, [128, 2 * 512], F32))
    return tb


def build_consts(cp):
    nc = cp.nc
    oc = nc.inline_tensor(np.ones((128, 1), np.float32), name="ones_col").ap()
    orw = nc.inline_tensor(np.ones((1, 128), np.float32), name="ones_row").ap()
    ident = nc.inline_tensor(np.eye(128, dtype=np.float32), name="ident").ap()
    epsv = np.tile(np.array([[1e-5, 1e-6, -0.5, 0.0]], np.float32), (128, 1))
    epsd = nc.inline_tensor(epsv, name="eps_c").ap()
    cp.ones_col = cp.open(nc.sbuf_tensor("ones_col_sb", [128, 1], BF16))
    cp.ones_row = cp.open(nc.sbuf_tensor("ones_row_sb", [1, 128], BF16))
    cp.ident_sb = cp.open(nc.sbuf_tensor("ident_sb", [128, 128], BF16))
    cp.eps_sb = cp.open(nc.sbuf_tensor("eps_sb", [128, 4], F32))
    with nc.Block() as blk:
        @blk.gpsimd
        def _(g):
            cp.inc(g.dma_start(out=cp.ones_col[:, :], in_=oc[:, :]), "ldw", 16)
            cp.inc(g.dma_start(out=cp.ones_row[:, :], in_=orw[:, :]), "ldw", 16)
            cp.inc(g.dma_start(out=cp.ident_sb[:, :], in_=ident[:, :]), "ldw", 16)
            cp.inc(g.dma_start(out=cp.eps_sb[:, :], in_=epsd[:, :]), "ldw", 16)

        @blk.sync
        def _(s):
            s.wait_ge(cp.sems["ldw"], cp.cnt["ldw"])


def build_p1(cp, xT, v_mine):
    nc = cp.nc
    import contextlib
    with contextlib.ExitStack() as es:
        E = es.enter_context
        xb = E(nc.sbuf_tensor("p1_x", [128, 16 * 1024], F32))
        u0 = E(nc.sbuf_tensor("p1_u0", [128, 16 * 512], F32))
        vst = E(nc.sbuf_tensor("p1_v", [128, 16 * 512], BF16))
        tb = _norm_scratch(nc, es)
        tb["ones_col"] = cp.ones_col
        tb["ones_row"] = cp.ones_row

        with nc.Block() as blk:
            @blk.sync
            def _(s):
                cp.inc(s.dma_start(
                    out=xb[:, :].rearrange("p (k t) -> p k t", k=16),
                    in_=xT.rearrange("(k p) t -> p k t", p=128)), "lda", 16)

            @blk.vector
            def _(v):
                v.wait_ge(cp.sems["lda"], cp.cnt["lda"])
                cp.inc(v.engine_nop(), "evd", 1)

        for h in range(2):
            emit_norm512(cp, tb,
                         lambda k, h=h: xb[:, k * 1024 + h * 512:
                                           k * 1024 + h * 512 + 512],
                         lambda k: u0[:, k * 512:(k + 1) * 512],
                         "ln", EPS_LN)
            emit_norm512(cp, tb,
                         lambda k: u0[:, k * 512:(k + 1) * 512],
                         lambda k: vst[:, k * 512:(k + 1) * 512],
                         "rms", EPS_RMS,
                         w_col=lambda k: cp.ln1_sb[:, k:k + 1])
            with nc.Block() as blk:
                @blk.gpsimd
                def _(g):
                    g.wait_ge(cp.sems["evd"], cp.cnt["evd"])
                    cp.inc(g.dma_start(
                        out=v_mine.ap()[:, h * 512:(h + 1) * 512].rearrange(
                            "(k p) t -> p k t", p=128),
                        in_=vst[:, :].rearrange("p (k t) -> p k t", k=16)),
                        "stg", 16)
                    g.wait_ge(cp.sems["stg"], cp.cnt["stg"])


def build_dequant(cp, t_in, fp4_mine):
    fp4g_mine, fp4d_mine = fp4_mine
    nc = cp.nc
    import contextlib
    with contextlib.ExitStack() as es:
        E = es.enter_context
        pk = E(nc.sbuf_tensor("dq_pk", [128, 11 * 256 + 2 * 704], I32))
        sc_gu = E(nc.sbuf_tensor("dq_sgu", [128, 11 * 16], F32))
        sc_dn = E(nc.sbuf_tensor("dq_sdn", [128, 2 * 44], F32))
        nib = E(nc.sbuf_tensor("dq_nib", [128, 5632], I32))
        fx = E(nc.sbuf_tensor("dq_fx", [128, 5632], F32))
        t1 = E(nc.sbuf_tensor("dq_t1", [128, 5632], F32))
        t2 = E(nc.sbuf_tensor("dq_t2", [128, 5632], F32))
        dqb = E(nc.sbuf_tensor("dq_b", [128, 5632], BF16))
        stage = E(nc.sbuf_tensor("dq_st", [128, 44 * 128], BF16))
        pst = E(nc.psum_tensor("dq_ps", [128, 1024], BF16))
        blk = E(nc.Block())

        eva0 = cp.cnt["eva"]
        evd0 = cp.cnt["evd"]
        lda0 = cp.cnt["lda"]
        ldw0 = cp.cnt["ldw"]
        stg0 = cp.cnt["stg"]
        pe0 = cp.pe_issued

        tiles = [("gu", ti, 256, 2048, 16) for ti in range(11)] + \
                [("dn", ti, 704, 5632, 44) for ti in range(2)]

        def evd_tile(i):
            return evd0 + i + 1

        def eva_tile(i):
            kcs = [t[4] for t in tiles]
            return eva0 + sum(kcs[:i + 1])

        def pe_tile_t(i):
            kcs = [t[4] for t in tiles]
            return pe0 + sum(2 * k for k in kcs[:i + 1])

        @blk.sync
        def _(s):
            cp.inc(s.dma_start(out=pk[:, 0:11 * 256].rearrange(
                "p (a w) -> p a w", a=11),
                in_=t_in["gu_packed"].rearrange("(a p) w -> p a w", p=128)),
                "lda", 16)
            cp.inc(s.dma_start(out=pk[:, 11 * 256:].rearrange(
                "p (a w) -> p a w", a=2),
                in_=t_in["dn_packed"].rearrange("(a p) w -> p a w", p=128)),
                "lda", 16)

        @blk.gpsimd
        def _(g):
            cp.inc(g.dma_start(out=sc_gu[:, :], in_=t_in["gu_scales"]),
                   "ldw", 16)
            cp.inc(g.dma_start(out=sc_dn[:, :], in_=t_in["dn_scales"]),
                   "ldw", 16)
            for i, (kind, ti, wlen, inlen, kc) in enumerate(tiles):
                g.wait_ge(cp.sems["eva"], eva_tile(i))
                if kind == "gu":
                    dst = fp4g_mine.ap()[ti * 128:(ti + 1) * 128, :]
                else:
                    dst = fp4d_mine.ap()[ti * 128:(ti + 1) * 128, :]
                cp.inc(g.dma_start(out=dst, in_=stage[:, 0:kc * 128]),
                       "stg", 16)

        @blk.vector
        def _(v):
            v.wait_ge(cp.sems["lda"], lda0 + 32)
            v.wait_ge(cp.sems["ldw"], ldw0 + 32)
            for i, (kind, ti, wlen, inlen, kc) in enumerate(tiles):
                if i >= 1:
                    v.wait_ge(cp.sems["eva"], eva_tile(i - 1))
                woff = ti * 256 if kind == "gu" else 11 * 256 + ti * 704
                wsl = pk[:, woff:woff + wlen]
                for kn in range(8):
                    v.tensor_scalar(
                        nib[:, 0:inlen].rearrange("p (w e) -> p e w",
                                                  e=8)[:, kn, :],
                        wsl, 4 * kn, 15,
                        ALU.logical_shift_right, ALU.bitwise_and)
                v.tensor_copy(fx[:, 0:inlen], nib[:, 0:inlen])
                v.tensor_single_scalar(t1[:, 0:inlen], fx[:, 0:inlen], 8.0,
                                       ALU.is_ge)
                v.scalar_tensor_tensor(fx[:, 0:inlen], t1[:, 0:inlen], -8.0,
                                       fx[:, 0:inlen], ALU.mult, ALU.add)
                v.tensor_scalar(t1[:, 0:inlen], t1[:, 0:inlen], -2.0, 1.0,
                                ALU.mult, ALU.add)
                v.tensor_scalar(t2[:, 0:inlen], fx[:, 0:inlen], -4.0, 0.0,
                                ALU.add, ALU.max)
                v.tensor_add(t2[:, 0:inlen], t2[:, 0:inlen], fx[:, 0:inlen])
                v.tensor_scalar(fx[:, 0:inlen], fx[:, 0:inlen], -6.0, 0.0,
                                ALU.add, ALU.max)
                v.scalar_tensor_tensor(t2[:, 0:inlen], t2[:, 0:inlen], 0.5,
                                       fx[:, 0:inlen], ALU.mult, ALU.add)
                v.tensor_mul(t2[:, 0:inlen], t2[:, 0:inlen], t1[:, 0:inlen])
                scs = sc_gu if kind == "gu" else sc_dn
                ng = inlen // 128
                for gidx in range(ng):
                    v.tensor_scalar_mul(
                        dqb[:, gidx * 128:(gidx + 1) * 128],
                        t2[:, gidx * 128:(gidx + 1) * 128],
                        scs[:, ti * ng + gidx:ti * ng + gidx + 1])
                cp.inc(v.engine_nop(), "evd", 1)

        @blk.tensor
        def _(t):
            for i, (kind, ti, wlen, inlen, kc) in enumerate(tiles):
                t.wait_ge(cp.sems["evd"], evd_tile(i))
                for k in range(kc):
                    if i * 1000 + k >= 2:
                        pass
                    tot_prev = sum(tt[4] for tt in tiles[:i]) + k
                    if tot_prev >= 2:
                        t.wait_ge(cp.sems["eva"], eva0 + tot_prev - 1)
                    cp.pe_transpose(
                        pst[:, (k % 2) * 128:(k % 2) * 128 + 128],
                        dqb[:, k * 128:(k + 1) * 128])
                    cp.pe_drain()

        @blk.scalar
        def _(sc):
            for i, (kind, ti, wlen, inlen, kc) in enumerate(tiles):
                if i >= 1:
                    sc.wait_ge(cp.sems["stg"], stg0 + i * 16)
                for k in range(kc):
                    tot = sum(tt[4] for tt in tiles[:i]) + k
                    sc.wait_ge(cp.sems["pe"], pe0 + 2 * (tot + 1))
                    cp.inc(sc.activation(stage[:, k * 128:(k + 1) * 128],
                                         pst[:, (k % 2) * 128:
                                             (k % 2) * 128 + 128],
                                         AF.Copy), "eva", 1)


def build_token(cp, t_in, yg_all, xT, fp4_all, out_d):
    fp4g_all, fp4d_all = fp4_all
    nc = cp.nc
    import contextlib
    with contextlib.ExitStack() as es:
        E = es.enter_context
        AF1 = E(nc.sbuf_tensor("tk_af1", [128, 16 * 512], F32))
        AF2 = E(nc.sbuf_tensor("tk_af2", [128, 16 * 512], F32))
        AF3 = E(nc.sbuf_tensor("tk_af3", [128, 16 * 512], F32))
        AB1 = E(nc.sbuf_tensor("tk_ab1", [128, 64 * 512], BF16))
        AB2 = E(nc.sbuf_tensor("tk_ab2", [128, 16 * 512], BF16))
        wb1 = E(nc.sbuf_tensor("tk_wb1", [128, 2 * 16 * 128], BF16))
        sgt = E(nc.sbuf_tensor("tk_sgt", [128, 2 * 512], BF16))
        psA = E(nc.psum_tensor("tk_psa", [128, 2 * 512], F32))
        psD = E(nc.psum_tensor("tk_psd", [128, 2 * 512], F32))
        tb = _norm_scratch(nc, es, sq16=AB1[:, 44 * 512:60 * 512])
        tb["ones_col"] = cp.ones_col
        tb["ones_row"] = cp.ones_row

        def t16(buf):
            return lambda k: buf[:, k * 512:(k + 1) * 512]

        for w in range(2):
            my_t0 = 512 * w

            with nc.Block() as blk:
                @blk.sync
                def _(s):
                    cp.inc(s.dma_start(
                        out=AF1[:, :].rearrange("p (k t) -> p k t", k=16),
                        in_=xT[:, my_t0:my_t0 + 512].rearrange(
                            "(k p) t -> p k t", p=128)), "lda", 16)

                @blk.vector
                def _(v):
                    v.wait_ge(cp.sems["lda"], cp.cnt["lda"])
                    cp.inc(v.engine_nop(), "evd", 1)
            emit_norm512(cp, tb, t16(AF1), t16(AF2), "ln", EPS_LN)

            _mm_stage(cp, nc, wb1, psD, nm=16, nkc=32,
                      wsrc=lambda m, h, kw: t_in["w_outproj"][
                          m, :, h * kw:(h + 1) * kw, :],
                      load_yg=(yg_all, my_t0), ygbuf=AB1,
                      moving=lambda k: AB1[:, k * 512:(k + 1) * 512],
                      evac=("dve_add", t16(AF2), t16(AF2)))

            emit_norm512(cp, tb, t16(AF2), t16(AB2), "rms", EPS_RMS,
                         w_col=lambda k: cp.ln2_sb[:, k:k + 1])

            _mlp_gateup(cp, nc, wb1, psA, psD, sgt, AB2, AB1,
                        t_in["w_gate"], t_in["w_up"], 64)

            _mm_stage(cp, nc, wb1, psD, nm=16, nkc=64,
                      wsrc=lambda m, h, kw: t_in["w_down"][
                          m, :, h * kw:(h + 1) * kw, :],
                      load_yg=None, ygbuf=None,
                      moving=lambda k: AB1[:, k * 512:(k + 1) * 512],
                      evac=("dve_add", t16(AF2), t16(AF3)))
            with nc.Block() as blk:
                @blk.sync
                def _(s):
                    cp.inc(s.dma_start(
                        out=AF1[:, :].rearrange("p (k t) -> p k t", k=16),
                        in_=xT[:, my_t0:my_t0 + 512].rearrange(
                            "(k p) t -> p k t", p=128)), "lda", 16)

                @blk.vector
                def _(v):
                    v.wait_ge(cp.sems["lda"], cp.cnt["lda"])
                    cp.inc(v.engine_nop(), "evd", 1)
            emit_norm512(cp, tb, t16(AF3), t16(AF3), "ln", EPS_LN,
                         add_tile=t16(AF1))
            emit_norm512(cp, tb, t16(AF3), t16(AB2), "ln", EPS_LN)

            _ffn_gateup(cp, nc, wb1, psA, psD, sgt, AB2, AB1, fp4g_all, 44)

            _ffn_down(cp, nc, wb1, psA, AB1, fp4d_all, AF1)
            emit_norm512(cp, tb, t16(AF1), t16(AF1), "rmsdiv", EPS_FFN)
            emit_norm512(cp, tb, t16(AF1), t16(AF2), "ln", EPS_LN,
                         add_tile=t16(AF3))
            with nc.Block() as blk:
                @blk.gpsimd
                def _(g):
                    g.wait_ge(cp.sems["evd"], cp.cnt["evd"])
                    cp.inc(g.dma_start(
                        out=out_d[:, my_t0:my_t0 + 512].rearrange(
                            "(k p) t -> p k t", p=128),
                        in_=AF2[:, :].rearrange("p (k t) -> p k t", k=16)),
                        "sto", 16)


def _mm_stage(cp, nc, wbuf, ps, nm, nkc, wsrc, load_yg, ygbuf, moving, evac,
              kw=16):
    nchunk = nkc // kw
    with nc.Block() as blk:
        pe0 = cp.pe_issued
        ldw0 = cp.cnt["ldw"]
        lda0 = cp.cnt["lda"]
        evd0 = cp.cnt["evd"]

        def pe_m(m):
            return pe0 + (m + 1) * (nkc + 1)

        def pe_chunk(m, h):
            return pe0 + m * (nkc + 1) + (h + 1) * kw

        @blk.gpsimd
        def _(g):
            for m in range(nm):
                for h in range(nchunk):
                    gidx = m * nchunk + h
                    if gidx >= 2:
                        pm, ph = (gidx - 2) // nchunk, (gidx - 2) % nchunk
                        g.wait_ge(cp.sems["pe"], pe_chunk(pm, ph))
                    off = (gidx % 2) * kw * 128
                    cp.inc(g.dma_start(
                        out=wbuf[:, off:off + kw * 128].rearrange(
                            "p (k c) -> p k c", k=kw),
                        in_=wsrc(m, h, kw)), "ldw", 16)

        @blk.sync
        def _(s):
            if load_yg is not None:
                yg_all_ap, my_t0 = load_yg
                pid = nc.partition_id()
                off = (pid % 4) * 1024 + my_t0
                src_ap = bass.AP(yg_all_ap.tensor, off,
                                 [[4096, 128], [128 * 4096, 32], [1, 512]])
                cp.inc(s.dma_start(
                    out=ygbuf[:, 0:32 * 512].rearrange(
                        "p (k t) -> p k t", k=32),
                    in_=src_ap), "lda", 16)

        @blk.tensor
        def _(t):
            if load_yg is not None:
                t.wait_ge(cp.sems["lda"], lda0 + 16)
            for m in range(nm):
                if m >= 2:
                    t.wait_ge(cp.sems["evd"], evd0 + m - 1)
                for h in range(nchunk):
                    gidx = m * nchunk + h
                    t.wait_ge(cp.sems["ldw"], ldw0 + (gidx + 1) * 16)
                    off = (gidx % 2) * kw * 128
                    for k in range(kw):
                        cp.pe_mm(ps[:, (m % 2) * 512:(m % 2) * 512 + 512],
                                 wbuf[:, off + k * 128:off + (k + 1) * 128],
                                 moving(h * kw + k),
                                 start=(h == 0 and k == 0),
                                 stop=(h == nchunk - 1 and k == kw - 1))
                cp.pe_drain()

        @blk.vector
        def _(v):
            kind, a_t, o_t = evac
            for m in range(nm):
                v.wait_ge(cp.sems["pe"], pe_m(m))
                cp.inc(v.tensor_add(o_t(m), a_t(m),
                                    ps[:, (m % 2) * 512:(m % 2) * 512 + 512]),
                       "evd", 1)


def _mlp_gateup(cp, nc, wbuf, psA, psD, sgt, h2, hh, wg, wu, nm):
    with nc.Block() as blk:
        pe0 = cp.pe_issued
        ldw0 = cp.cnt["ldw"]
        eva0 = cp.cnt["eva"]
        evd0 = cp.cnt["evd"]

        def pe_pair(m):
            return pe0 + (m + 1) * 34

        @blk.gpsimd
        def _(g):
            for m in range(nm):
                for half, w in ((0, wg), (1, wu)):
                    if m >= 1:
                        g.wait_ge(cp.sems["pe"],
                                  pe0 + (m - 1) * 34 + (half + 1) * 17)
                    cp.inc(g.dma_start(
                        out=wbuf[:, half * 16 * 128:(half + 1) * 16 * 128
                                 ].rearrange("p (k c) -> p k c", k=16),
                        in_=w[m]), "ldw", 16)

        @blk.tensor
        def _(t):
            for m in range(nm):
                t.wait_ge(cp.sems["ldw"], ldw0 + (2 * m + 1) * 16)
                if m >= 2:
                    t.wait_ge(cp.sems["eva"], eva0 + m - 1)
                for k in range(16):
                    cp.pe_mm(psA[:, (m % 2) * 512:(m % 2) * 512 + 512],
                             wbuf[:, k * 128:(k + 1) * 128],
                             h2[:, k * 512:(k + 1) * 512],
                             start=(k == 0), stop=(k == 15))
                cp.pe_drain()
                t.wait_ge(cp.sems["ldw"], ldw0 + (2 * m + 2) * 16)
                if m >= 2:
                    t.wait_ge(cp.sems["evd"], evd0 + m - 1)
                for k in range(16):
                    cp.pe_mm(psD[:, (m % 2) * 512:(m % 2) * 512 + 512],
                             wbuf[:, 16 * 128 + k * 128:
                                  16 * 128 + (k + 1) * 128],
                             h2[:, k * 512:(k + 1) * 512],
                             start=(k == 0), stop=(k == 15))
                cp.pe_drain()

        @blk.scalar
        def _(sc):
            for m in range(nm):
                sc.wait_ge(cp.sems["pe"], pe0 + m * 34 + 17)
                if m >= 2:
                    sc.wait_ge(cp.sems["evd"], evd0 + m - 1)
                cp.inc(sc.activation(sgt[:, (m % 2) * 512:(m % 2) * 512 + 512],
                                     psA[:, (m % 2) * 512:(m % 2) * 512 + 512],
                                     AF.Silu), "eva", 1)

        @blk.vector
        def _(v):
            for m in range(nm):
                v.wait_ge(cp.sems["pe"], pe_pair(m))
                v.wait_ge(cp.sems["eva"], eva0 + m + 1)
                cp.inc(v.tensor_mul(hh[:, m * 512:(m + 1) * 512],
                                    sgt[:, (m % 2) * 512:(m % 2) * 512 + 512],
                                    psD[:, (m % 2) * 512:(m % 2) * 512 + 512]),
                       "evd", 1)


def _ffn_gateup(cp, nc, wbuf, psA, psD, sgt, u1, hh2, fp4g_all, nm):

    with nc.Block() as blk:
        pe0 = cp.pe_issued
        ldw0 = cp.cnt["ldw"]
        eva0 = cp.cnt["eva"]
        evd0 = cp.cnt["evd"]

        @blk.sync
        def _(s):
            for m in range(nm):
                for half in range(2):
                    j = m if half == 0 else 44 + m
                    if m >= 1:
                        s.wait_ge(cp.sems["pe"],
                                  pe0 + (m - 1) * 34 + (half + 1) * 17)
                    cp.inc(s.dma_start(
                        out=wbuf[:, half * 16 * 128:(half + 1) * 16 * 128],
                        in_=fp4g_all[j * 128:(j + 1) * 128, :]), "lda", 16)

        lda0 = cp.cnt["lda"] - 2 * nm * 16

        @blk.tensor
        def _(t):
            for m in range(nm):
                t.wait_ge(cp.sems["lda"], lda0 + (2 * m + 1) * 16)
                if m >= 2:
                    t.wait_ge(cp.sems["eva"], eva0 + m - 1)
                for k in range(16):
                    cp.pe_mm(psA[:, (m % 2) * 512:(m % 2) * 512 + 512],
                             wbuf[:, k * 128:(k + 1) * 128],
                             u1[:, k * 512:(k + 1) * 512],
                             start=(k == 0), stop=(k == 15))
                cp.pe_drain()
                t.wait_ge(cp.sems["lda"], lda0 + (2 * m + 2) * 16)
                if m >= 2:
                    t.wait_ge(cp.sems["evd"], evd0 + m - 1)
                for k in range(16):
                    cp.pe_mm(psD[:, (m % 2) * 512:(m % 2) * 512 + 512],
                             wbuf[:, 16 * 128 + k * 128:
                                  16 * 128 + (k + 1) * 128],
                             u1[:, k * 512:(k + 1) * 512],
                             start=(k == 0), stop=(k == 15))
                cp.pe_drain()

        @blk.scalar
        def _(sc):
            for m in range(nm):
                sc.wait_ge(cp.sems["pe"], pe0 + m * 34 + 17)
                if m >= 2:
                    sc.wait_ge(cp.sems["evd"], evd0 + m - 1)
                cp.inc(sc.activation(sgt[:, (m % 2) * 512:(m % 2) * 512 + 512],
                                     psA[:, (m % 2) * 512:(m % 2) * 512 + 512],
                                     AF.Silu), "eva", 1)

        @blk.vector
        def _(v):
            for m in range(nm):
                v.wait_ge(cp.sems["pe"], pe0 + (m + 1) * 34)
                v.wait_ge(cp.sems["eva"], eva0 + m + 1)
                cp.inc(v.tensor_mul(hh2[:, m * 512:(m + 1) * 512],
                                    sgt[:, (m % 2) * 512:(m % 2) * 512 + 512],
                                    psD[:, (m % 2) * 512:(m % 2) * 512 + 512]),
                       "evd", 1)


def _ffn_down(cp, nc, wbuf, psA, hh2, fp4d_all, yout, kw=11):
    nchunk = 44 // kw

    with nc.Block() as blk:
        pe0 = cp.pe_issued
        lda0 = cp.cnt["lda"]
        eva0 = cp.cnt["eva"]

        def pe_chunk(m, h):
            return pe0 + m * 45 + (h + 1) * kw

        @blk.sync
        def _(s):
            for m in range(16):
                for h in range(nchunk):
                    gidx = m * nchunk + h
                    if gidx >= 2:
                        pm, ph = (gidx - 2) // nchunk, (gidx - 2) % nchunk
                        s.wait_ge(cp.sems["pe"], pe_chunk(pm, ph))
                    off = (gidx % 2) * kw * 128
                    cp.inc(s.dma_start(
                        out=wbuf[:, off:off + kw * 128],
                        in_=fp4d_all[m * 128:(m + 1) * 128,
                                     h * kw * 128:(h + 1) * kw * 128]),
                        "lda", 16)

        @blk.tensor
        def _(t):
            for m in range(16):
                if m >= 2:
                    t.wait_ge(cp.sems["eva"], eva0 + m - 1)
                for h in range(nchunk):
                    gidx = m * nchunk + h
                    t.wait_ge(cp.sems["lda"], lda0 + (gidx + 1) * 16)
                    off = (gidx % 2) * kw * 128
                    for k in range(kw):
                        cp.pe_mm(psA[:, (m % 2) * 512:(m % 2) * 512 + 512],
                                 wbuf[:, off + k * 128:off + (k + 1) * 128],
                                 hh2[:, (h * kw + k) * 512:
                                     (h * kw + k + 1) * 512],
                                 start=(h == 0 and k == 0),
                                 stop=(h == nchunk - 1 and k == kw - 1))
                cp.pe_drain()

        @blk.scalar
        def _(sc):
            for m in range(16):
                sc.wait_ge(cp.sems["pe"], pe0 + (m + 1) * 45)
                cp.inc(sc.activation(yout[:, m * 512:(m + 1) * 512],
                                     psA[:, (m % 2) * 512:(m % 2) * 512 + 512],
                                     AF.Copy), "eva", 1)


TOKEN_IN_SHAPES = {
    "xT": ([2048, 1024], F32),
    "w_outproj": ([16, 128, 32, 128], F32),
    "w_gate": ([64, 128, 16, 128], F32),
    "w_up": ([64, 128, 16, 128], F32),
    "w_down": ([16, 128, 64, 128], F32),
    "gu_packed": ([1408, 256], I32),
    "gu_scales": ([128, 11 * 16], F16),
    "dn_packed": ([256, 704], I32),
    "dn_scales": ([128, 2 * 44], F16),
}
RANKSZ_FP4 = (11 * 16 + 2 * 44) * 128


def build_full():
    import contextlib
    nc = bass.Bass("TRN2", target_bir_lowering=False, debug=False,
                   detect_race_conditions=False)
    cp = CoreProgram(nc)
    cp.sem_init()
    t_in = _declare_mixer_inputs(nc)
    for name, (shp, dt) in TOKEN_IN_SHAPES.items():
        t_in[name] = nc.dram_tensor(name, shp, dt, kind="ExternalInput").ap()
    out_d = nc.dram_tensor("out", [2048, 1024], F32, kind="ExternalOutput").ap()

    bzs = _mixer_bounces(nc)
    v_mine = nc.dram_tensor("v_mine", [2048, 1024], BF16)
    v_all = nc.dram_tensor("v_all", [4 * 2048, 1024], BF16)
    yg_mine = nc.dram_tensor("yg_mine", [1024, SEQ], BF16)
    yg_all = nc.dram_tensor("yg_all", [4 * 1024, SEQ], BF16)
    fp4g_mine = nc.dram_tensor("fp4g_mine", [11 * 128, 2048], BF16)
    fp4d_mine = nc.dram_tensor("fp4d_mine", [2 * 128, 5632], BF16)
    fp4g_all = nc.dram_tensor("fp4g_all", [88 * 128, 2048], BF16)
    fp4d_all = nc.dram_tensor("fp4d_all", [16 * 128, 5632], BF16)

    build_consts(cp)
    import contextlib as _ctxlib
    mix_es = _ctxlib.ExitStack()
    _alloc_repl(cp, mix_es)
    cp.build_p0(t_in)
    build_p1(cp, t_in["xT"], v_mine)
    with nc.Block() as blk:
        @blk.gpsimd
        def _(g):
            g.wait_ge(cp.sems["stg"], cp.cnt["stg"])
            cp.inc(g.collective_compute(
                "AllGather", ALU.bypass,
                replica_groups=[[0, 1, 2, 3], [4, 5, 6, 7]],
                ins=[v_mine.ap().opt()],
                outs=[v_all.ap().opt()]), "cc", 1)
    cp.build_m1(t_in, v_all.ap(), bzs, wait_cc=1)
    build_dequant(cp, t_in, (fp4g_mine, fp4d_mine))
    with nc.Block() as blk:
        @blk.gpsimd
        def _(g):
            g.wait_ge(cp.sems["stg"], cp.cnt["stg"])
            cp.inc(g.collective_compute(
                "AllGather", ALU.bypass,
                replica_groups=[list(range(8))],
                ins=[fp4g_mine.ap().opt()],
                outs=[fp4g_all.ap().opt()]), "cc", 1)
            cp.inc(g.collective_compute(
                "AllGather", ALU.bypass,
                replica_groups=[list(range(8))],
                ins=[fp4d_mine.ap().opt()],
                outs=[fp4d_all.ap().opt()]), "cc", 1)
    cp.build_m2(bzs, yg_mine.ap())
    mix_es.close()
    with nc.Block() as blk:
        @blk.gpsimd
        def _(g):
            g.wait_ge(cp.sems["stg"], cp.cnt["stg"])
            cp.inc(g.collective_compute(
                "AllGather", ALU.bypass,
                replica_groups=[[0, 1, 2, 3], [4, 5, 6, 7]],
                ins=[yg_mine.ap().opt()],
                outs=[yg_all.ap().opt()]), "cc", 1)
    with nc.Block() as blk:
        @blk.sync
        def _(s):
            s.wait_ge(cp.sems["cc"], 4)
    build_token(cp, t_in, yg_all.ap(), t_in["xT"],
                (fp4g_all.ap(), fp4d_all.ap()), out_d)
    with nc.Block() as blk:
        @blk.sync
        def _(s):
            s.wait_ge(cp.sems["sto"], cp.cnt["sto"])
    return nc, cp


def _host_prep_token(inputs, c):
    d = {}
    x = np.asarray(inputs["L0_attn_hidden_states"])
    b, q = c // 4, c % 4
    d["xT"] = np.ascontiguousarray(x[b, TOK * q:TOK * (q + 1), :].T)

    dperm_rank = np.concatenate([_dlocal(gp) for gp in range(NGP)])
    yg_dglobal = np.concatenate([1024 * r + dperm_rank for r in range(4)])
    w_out = np.asarray(inputs["L0_attn_out_proj_w"])
    w_out_lhsT = np.ascontiguousarray(w_out.T[yg_dglobal, :])
    d["w_outproj"] = np.ascontiguousarray(
        w_out_lhsT.reshape(32, 128, 16, 128).transpose(2, 1, 0, 3))

    def mk_lhsT(w):
        out_f, in_f = w.shape
        lt = np.ascontiguousarray(w.T)
        return np.ascontiguousarray(
            lt.reshape(in_f // 128, 128, out_f // 128, 128).transpose(2, 1, 0, 3))

    d["w_gate"] = mk_lhsT(np.asarray(inputs["L0_attn_gate_w"]))
    d["w_up"] = mk_lhsT(np.asarray(inputs["L0_attn_up_w"]))
    d["w_down"] = mk_lhsT(np.asarray(inputs["L0_attn_down_w"]))

    gu_p = np.asarray(inputs["L0_ffn_gate_up_packed"])
    gu_s = np.asarray(inputs["L0_ffn_gate_up_scales"])
    dn_p = np.asarray(inputs["L0_ffn_down_packed"])
    dn_s = np.asarray(inputs["L0_ffn_down_scales"])
    d["gu_packed"] = np.ascontiguousarray(gu_p[1408 * c:1408 * (c + 1), :])
    s = gu_s[:, 1408 * c:1408 * (c + 1)]
    d["gu_scales"] = np.ascontiguousarray(
        s.T.reshape(11, 128, 16).transpose(1, 0, 2).reshape(128, 11 * 16))
    d["dn_packed"] = np.ascontiguousarray(dn_p[256 * c:256 * (c + 1), :])
    s2 = dn_s[:, 256 * c:256 * (c + 1)]
    d["dn_scales"] = np.ascontiguousarray(
        s2.T.reshape(2, 128, 44).transpose(1, 0, 2).reshape(128, 2 * 44))
    return d


def kernel(**inputs):
    per_core = []
    for c in range(NCORE):
        d = _host_prep_mixer(inputs, c)
        d.update(_host_prep_token(inputs, c))
        per_core.append(d)
    nc, cp = build_full()
    res = run_bass_kernel_spmd(nc, per_core, core_ids=list(range(NCORE)))
    x = np.asarray(inputs["L0_attn_hidden_states"])
    out = np.empty_like(x)
    for c in range(NCORE):
        b, q = c // 4, c % 4
        out[b, TOK * q:TOK * (q + 1), :] = res.results[c]["out"].T
    return out
